# version 23
# baseline (speedup 1.0000x reference)
import numpy as np
import concourse.bass as bass
import concourse.mybir as mybir
from concourse.bass_utils import run_bass_kernel_spmd
from contextlib import ExitStack

F32 = mybir.dt.float32
BF16 = mybir.dt.bfloat16
F8 = mybir.dt.float8e4
AF = mybir.ActivationFunctionType
OP = mybir.AluOpType

NCORE = 8
T = 2048
DM = 1024
E = 2048
EC = 16
KD = 8
TT = 256
NTI = T // TT
H = 128
ENGS = ("sync", "gpsimd", "tensor", "vector", "scalar")


class Sched:
    def __init__(self):
        self.st = {e: [] for e in ENGS}
        self.cnt = {}
        self.lastw = {}

    def inc(self, eng, fn, sem, semname, n=1):
        v = self.cnt.get(semname, 0) + n
        self.cnt[semname] = v
        self.st[eng].append(lambda e, fn=fn, sem=sem, n=n: fn(e).then_inc(sem, n))
        return (semname, sem, v)

    def emit(self, eng, fn):
        self.st[eng].append(fn)

    def wait(self, eng, ticket):
        if ticket is None:
            return
        if isinstance(ticket, (tuple, list)) and ticket and isinstance(ticket[0], (tuple, list, type(None))):
            for t in ticket:
                self.wait(eng, t)
            return
        semname, sem, v = ticket
        key = (eng, semname)
        if self.lastw.get(key, -1) >= v:
            return
        self.lastw[key] = v
        self.st[eng].append(lambda e, sem=sem, v=v: e.wait_ge(sem, v))

    def run(self, block):
        for name in ENGS:
            def body(e, name=name):
                for fn in self.st[name]:
                    fn(e)
            getattr(block, name)(body)


def build_nc():
    nc = bass.Bass("TRN2", target_bir_lowering=False)
    dt = nc.dram_tensor
    x_in = dt("x", [T, DM], F32, kind="ExternalInput").ap()
    xh_in = dt("xh", [3, DM], F32, kind="ExternalInput").ap()
    w_xp = dt("w_xp", [DM, E], F32, kind="ExternalInput").ap()
    w_z = dt("w_z", [DM, E], F32, kind="ExternalInput").ap()
    w_b0 = dt("w_b0", [128, KD], F32, kind="ExternalInput").ap()
    w_x = dt("w_x", [E, 192], F32, kind="ExternalInput").ap()
    w_dt0 = dt("w_dt0", [64, 1], F32, kind="ExternalInput").ap()
    w_out = dt("w_out", [E, DM], F32, kind="ExternalInput").ap()
    w_ft = dt("w_ft", [DM, DM], F32, kind="ExternalInput").ap()
    cst_in = dt("cst_all", [128, 171], F32, kind="ExternalInput").ap()
    y_out = dt("y", [T, DM], F32, kind="ExternalOutput").ap()
    cc1i = dt("cc1i", [128, NCORE], F32)
    cc1o = dt("cc1o", [128, NCORE], F32)
    cc2i = dt("cc2i", [128, 2 * NCORE], F32)
    cc2o = dt("cc2o", [128, 2 * NCORE], F32)
    identf_d = nc.inline_tensor(np.eye(128, dtype=np.float32), "identf_dram")

    ctx = ExitStack()
    with ctx:
        sb = lambda n, s, d: ctx.enter_context(nc.sbuf_tensor(n, s, d))
        pst = lambda n: ctx.enter_context(nc.psum_tensor(n, [128, 512], F32))
        mksem = lambda n: ctx.enter_context(nc.semaphore(n))

        W1 = sb("W1", [128, KD * E], BF16)
        W2 = sb("W2", [128, KD * E], BF16)
        W3 = sb("W3", [128, EC * DM], BF16)
        Wq = sb("Wq", [128, KD * DM], F8)
        Wxs = sb("Wxs", [128, EC * 192], BF16)
        DG = sb("DG", [128, EC * 4 * 128], BF16)
        wb0 = sb("wb0", [128, KD], BF16)
        wdt0b = sb("wdt0b", [64, 1], BF16)
        identf = sb("identf", [128, 128], F32)
        identb = sb("identb", [128, 128], BF16)
        onesr = sb("onesr", [33, 128], F32)
        onesbf = sb("onesbf", [128, 1], BF16)
        onesrb = sb("onesrb", [33, 128], BF16)
        cst = sb("cst", [128, 4], F32)
        XS = sb("XS", [128, EC * TT], BF16)
        XSQ2 = sb("XSQ2", [128, 4 * TT], BF16)
        XT = [sb(f"XT{i}", [128, DM], F32) for i in range(2)]
        XN = [sb(f"XN{i}", [128, DM], F32) for i in range(2)]
        DIAGS = [XN[0].ap()[:, 0:256], XN[1].ap()[:, 0:256]]
        XNB = sb("XNB", [128, 2 * DM], BF16)
        XQ = sb("XQ", [128, 2 * DM], F8)
        QA = sb("QA", [128, DM], F32)
        QB = sb("QB", [128, DM], F32)
        XPB = sb("XPB", [128, 2 * 260], BF16)
        XPT = sb("XPT", [128, EC * 3], BF16)
        A_loc = sb("A_loc", [128, T], BF16)
        B_loc = sb("B_loc", [128, T], BF16)
        C_loc = sb("C_loc", [128, T], BF16)
        SCA = sb("SCA", [128, TT], BF16)
        SCB = sb("SCB", [128, TT], BF16)
        SCC = sb("SCC", [128, TT], BF16)
        DRB = sb("DRB", [64, TT], BF16)
        M1B = sb("M1B", [128, TT], F32)
        RSB = sb("RSB", [128, TT], F32)
        SZB = sb("SZB", [128, 2 * TT], BF16)
        TNH = sb("TNH", [128, TT], F32)
        YNF = [sb(f"YNF{i}", [128, TT], F32) for i in range(2)]
        YNBF = sb("YNBF", [128, 2 * TT], BF16)
        YGB = sb("YGB", [128, 4 * TT], BF16)
        GT = sb("GT", [128, 2 * TT], BF16)
        GF = sb("GF", [128, 2 * TT], F32)
        SGB = sb("SGB", [128, 2 * TT], F32)
        SG1 = sb("SG1", [128, 256], F32)
        HFM = sb("HFM", [128, KD * 3], BF16)
        MI = sb("MI", [128, 16], F32)
        MN = sb("MN", [128, 16], F32)
        MNI = sb("MNI", [128, 16], F32)
        ROWB = [sb(f"ROWB{i}", [33, 4 * 128], BF16) for i in range(2)]
        A0c = sb("A0c", [128, 1], F32)
        Dm1 = sb("Dm1", [128, EC], BF16)
        Dm2 = sb("Dm2", [128, EC], BF16)
        CONSTS = sb("CONSTS", [128, 171], F32)
        CW = CONSTS.ap()[:, 0:64]
        CB = CONSTS.ap()[:, 64:80]
        DD = CONSTS.ap()[:, 80:96]
        LNG = CONSTS.ap()[:, 96:112]
        LNB = CONSTS.ap()[:, 112:128]
        FFB = CONSTS.ap()[:, 128:136]
        RS1 = CONSTS.ap()[:, 136:137]
        SEL = CONSTS.ap()[:, 137:145]
        M8 = CONSTS.ap()[:, 145:153]
        M16 = CONSTS.ap()[:, 153:169]
        BDT = CONSTS.ap()[0:1, 169:170]
        ALG = CONSTS.ap()[:, 170:171]
        SC = sb("SC", [128, 16], F32)
        LL = sb("LL", [128, 9], F32)
        PP = sb("PP", [128, 9], F32)
        SS = sb("SS", [128, 9], F32)
        SST = sb("SST", [128, NTI], F32)
        SIN = sb("SIN", [128, 1], F32)
        EQF = sb("EQF", [128, 2 * NTI], F32)
        SELT = sb("SELT", [128, NCORE], F32)
        PAY1 = sb("PAY1", [128, NCORE], F32)
        PAY2 = sb("PAY2", [128, 2 * NCORE], F32)
        PAY2b = sb("PAY2b", [128, 2 * NCORE], F32)
        G1 = sb("G1", [128, NCORE], F32)
        G2 = sb("G2", [128, 2 * NCORE], F32)
        HCOL = sb("HCOL", [3, 4], F32)
        B0R = sb("B0R", [1, TT], BF16)
        XS0R = sb("XS0R", [1, TT], BF16)
        T0R = sb("T0R", [1, TT], BF16)
        T1R = sb("T1R", [1, TT], BF16)
        DLTR = sb("DLTR", [1, TT], BF16)
        BM0R = sb("BM0R", [1, TT], BF16)
        BDR = sb("BDR", [1, TT], BF16)
        MROWS1 = sb("MROWS1", [33, TT], F32)
        MROWS2 = sb("MROWS2", [33, TT], F32)
        M1R = MROWS1.ap()[0:1, :]
        RSTR = MROWS1.ap()[32:33, :]
        VARR = MROWS2.ap()[0:1, :]
        M2R = MROWS2.ap()[32:33, :]

        P = [pst(f"p{i}") for i in range(8)]

        semx = mksem("semx")
        semw = mksem("semw")
        semt = mksem("semt")
        semv = mksem("semv")
        sems = mksem("sems")
        semc = mksem("semc")
        semo = mksem("semo")
        semx2 = mksem("semx2")
        semw2 = mksem("semw2")
        semo2 = mksem("semo2")
        semg = mksem("semg")
        sema = mksem("sema")
        semb = mksem("semb")
        semh = mksem("semh")
        semq = mksem("semq")
        semsx = mksem("semsx")
        S = Sched()
        SEM = {"x": semx, "w": semw, "t": semt, "v": semv, "s": sems,
               "c": semc, "o": semo, "x2": semx2, "w2": semw2, "o2": semo2,
               "g": semg, "a": sema, "b": semb, "h": semh, "q": semq,
               "sx": semsx}
        _rot = {"x": ["x", "x2"], "w": ["w", "w2"], "o": ["o", "o2"]}
        _ri = {"x": 0, "w": 0, "o": 0}

        def inc(eng, fn, s, n=1):
            return S.inc(eng, fn, SEM[s], s, n)

        def w(eng, *tickets):
            for tk in tickets:
                S.wait(eng, tk)

        def dma(eng, out, in_, s):
            name = _rot[s][_ri[s]]
            _ri[s] ^= 1
            prior = S.cnt.get(name, 0)
            if prior:
                w(eng, (name, SEM[name], prior))
            return inc(eng, lambda e, o=out, i=in_: e.dma_start(out=o, in_=i), name, 16)

        def drain(eng):
            S.emit(eng, lambda e: e.drain())

        xnb3 = XNB.ap().rearrange("p (s c) -> p s c", c=DM)
        xq3 = XQ.ap().rearrange("p (s c) -> p s c", c=DM)

        tk_idf = dma("sync", identf[:], identf_d.ap(), "x")
        tk_idb = dma("gpsimd", identb[:], identf_d.ap(), "w")
        tk_hx = dma("gpsimd", XS[0:3, 8 * TT:8 * TT + DM], xh_in, "w")
        tk_small = dma("sync", CONSTS[:], cst_in, "x")
        tk_w1 = dma("gpsimd", W1.ap().rearrange("p (kc c) -> p kc c", c=E),
                    w_xp.rearrange("(kc p) c -> p kc c", p=128), "w")
        tk_wfl = [None] * KD
        for g_ in range(KD):
            if g_ == 1:
                w("gpsimd", ("h", semh, KD))
            if g_ >= 2:
                w("gpsimd", ("a", sema, g_ - 1))
            tk_wfl[g_] = dma("gpsimd", XN[g_ % 2][:],
                             w_ft[g_ * 128:(g_ + 1) * 128, :], "w")
        tk_wx = dma("gpsimd", Wxs.ap().rearrange("p (ec c) -> p ec c", c=192),
                    w_x.rearrange("(ec p) c -> p ec c", p=128), "w")
        tk_wb0 = dma("gpsimd", wb0[:], w_b0, "w")
        tk_wdt = dma("gpsimd", wdt0b[:], w_dt0, "w")
        qslot = [GF.ap()[:, 0:256], GF.ap()[:, 256:512],
                 SGB.ap()[:, 0:256], SGB.ap()[:, 256:512],
                 M1B.ap(), RSB.ap(), TNH.ap(), YNF[0].ap()]
        tk_wql = [None] * 32
        for qt in range(32):
            if qt >= 8:
                w("gpsimd", ("q", semq, qt - 7))
            kc, q4 = qt // 4, qt % 4
            tk_wql[qt] = dma("gpsimd", qslot[qt % 8],
                             w_ft[kc * 128:(kc + 1) * 128,
                                  q4 * 256:(q4 + 1) * 256], "w")
        tk_w2 = dma("gpsimd", W2.ap().rearrange("p (kc c) -> p kc c", c=E),
                    w_z.rearrange("(kc p) c -> p kc c", p=128), "w")
        tk_w3a = dma("gpsimd", W3.ap()[:, 0:8 * DM].rearrange("p (ec c) -> p ec c", c=DM),
                     w_out[0:1024, :].rearrange("(ec p) c -> p ec c", p=128), "w")

        S.emit("vector", lambda e: e.memset(cst[:, 0:1], 1e-5))
        S.emit("vector", lambda e: e.memset(cst[:, 1:2], -0.5))
        S.emit("vector", lambda e: e.memset(cst[:, 2:3], 0.5))
        S.emit("vector", lambda e: e.memset(onesbf[:], 1.0))
        S.emit("vector", lambda e: e.memset(onesr[:], 1.0))
        S.emit("vector", lambda e: e.memset(onesrb[:], 1.0))
        S.emit("vector", lambda e: e.memset(XPT[:], 0.0))
        S.emit("vector", lambda e: e.memset(SC[:, 0:1], 0.0))
        S.emit("vector", lambda e: e.memset(SC[:, 6:7], 0.0))
        tk_cst = inc("vector", lambda e: e.memset(XPB[:], 0.0), "v")
        w("scalar", tk_small)
        S.emit("scalar", lambda e: e.activation(out=SC[:, 11:12], in_=ALG, func=AF.Exp))
        drain("scalar")
        tk_a0 = inc("scalar", lambda e: e.mul(A0c[:], SC[:, 11:12], -1.0), "s")
        w("vector", tk_small)
        S.emit("vector", lambda e: e.tensor_scalar_mul(Dm1[:], DD, 1.0 / 2048.0))
        S.emit("vector", lambda e: e.tensor_mul(QA[:, 0:EC], DD, DD))
        drain("vector")
        S.emit("vector", lambda e: e.tensor_scalar_mul(Dm2[:], QA[:, 0:EC], 1.0 / 2048.0))
        drain("vector")
        S.emit("vector", lambda e: e.tensor_scalar(QA[:, 0:EC], LNG, RS1, None, OP.mult))
        S.emit("vector", lambda e: e.tensor_scalar(QA[:, EC:2 * EC], LNB, RS1, None, OP.mult))
        drain("vector")
        S.emit("vector", lambda e: e.tensor_copy(LNG, QA[:, 0:EC]))
        tk_ln = inc("vector", lambda e: e.tensor_copy(LNB, QA[:, EC:2 * EC]), "v")
        drain("vector")
        w("scalar", tk_idb)
        dgj = [0]

        def emit_dg(n):
            tk = None
            for _ in range(n):
                j = dgj[0]
                if j >= EC * 4:
                    return None
                dgj[0] += 1
                tk = inc("scalar", lambda e, j=j: e.activation(
                    out=DG[:, j * 128:(j + 1) * 128], in_=identb[:], func=AF.Copy,
                    scale=CW[:, j:j + 1]), "s")
            return tk

        hx0, hs0 = 8 * TT, 8 * TT + DM
        w("scalar", tk_hx)
        tk_hsq = inc("scalar", lambda e: e.square(XS[0:3, hs0:hs0 + DM],
                                                  XS[0:3, hx0:hx0 + DM]), "s")

        xt_free = [None, None]
        w("vector", tk_a0)
        tk_ax = tk_sw = tk_q = tk_aw = tk_aw128 = None
        tk_sum_prev = None
        for g in range(16):
            sl = g % 2
            w("sync", xt_free[sl])
            tk_xt = dma("sync", XT[sl][:], x_in[g * 128:(g + 1) * 128, :], "x")
            w("scalar", tk_xt, tk_sum_prev, tk_ln)
            tk_sq = inc("scalar", lambda e, sl=sl: e.square(QA[:], XT[sl][:]), "s")
            w("vector", tk_xt, tk_cst)
            tk_vs = inc("vector", lambda e, sl=sl: e.reduce_sum(
                SC[:, 1:2], XT[sl][:], axis=mybir.AxisListType.X), "v")
            w("vector", tk_sq)
            tk_sum_prev = inc("vector", lambda e: e.reduce_sum(
                SC[:, 11:12], QA[:], axis=mybir.AxisListType.X), "v")
            drain("vector")
            S.emit("vector", lambda e: e.tensor_scalar_mul(SC[:, 12:13], SC[:, 1:2], 1.0 / DM))
            drain("vector")
            S.emit("vector", lambda e, g=g: e.tensor_scalar_mul(MN[:, g:g + 1],
                                                                SC[:, 12:13], -1.0))
            S.emit("vector", lambda e: e.tensor_mul(SC[:, 13:14], SC[:, 12:13], SC[:, 12:13]))
            S.emit("vector", lambda e: e.tensor_scalar_mul(SC[:, 14:15], SC[:, 11:12], 1.0 / DM))
            drain("vector")
            tk_var = inc("vector", lambda e: e.tensor_sub(SC[:, 15:16], SC[:, 14:15],
                                                          SC[:, 13:14]), "v")
            if 4 <= g < 12:
                j = g - 4
                w("vector", tk_wfl[j])
                S.emit("vector", lambda e, j=j: e.tensor_reduce(
                    SC[:, 9:10], XN[j % 2][:], axis=mybir.AxisListType.X, op=OP.max,
                    apply_absolute_value=True))
                drain("vector")
                wco, wci = (3, 6) if j % 2 == 0 else (6, 3)
                tk_wm = inc("vector", lambda e, wco=wco, wci=wci: e.tensor_max(
                    SC[:, wco:wco + 1], SC[:, wci:wci + 1], SC[:, 9:10]), "a")
                drain("vector")
            w("scalar", tk_var)
            tk_sd = inc("scalar", lambda e: e.activation(out=SC[:, 11:12], in_=SC[:, 15:16],
                                                         func=AF.Sqrt, bias=cst[:, 0:1]), "s")
            w("vector", tk_sd)
            S.emit("vector", lambda e, g=g: e.reciprocal(MI[:, g:g + 1], SC[:, 11:12]))
            drain("vector")
            tk_xn_v = inc("vector", lambda e, g=g, sl=sl: e.tensor_scalar(
                QB[:], XT[sl][:], MN[:, g:g + 1], MI[:, g:g + 1],
                OP.add, OP.mult), "v")
            drain("vector")
            S.emit("vector", lambda e: e.tensor_reduce(
                SC[:, 1:2], QB[:], axis=mybir.AxisListType.X, op=OP.max,
                apply_absolute_value=True))
            drain("vector")
            xco, xci = (2, 0) if g % 2 == 0 else (0, 2)
            tk_ax = inc("vector", lambda e, xco=xco, xci=xci: e.tensor_max(
                SC[:, xco:xco + 1], SC[:, xci:xci + 1], SC[:, 1:2]), "b")
            drain("vector")
            if g == 5:
                w("scalar", tk_hm)
                tk_he = inc("scalar", lambda e: e.copy(out=XPT[:],
                                                       in_=P[3][:, 0:EC * 3]), "s")
            if g >= 12:
                for qq in range(8):
                    qt = (g - 12) * 8 + qq
                    kc, q4 = qt // 4, qt % 4
                    w("scalar", tk_wql[qt], tk_sw)
                    tk_q = inc("scalar", lambda e, kc=kc, q4=q4, qt=qt: e.activation(
                        out=Wq[:, kc * DM + q4 * 256: kc * DM + (q4 + 1) * 256],
                        in_=qslot[qt % 8], func=AF.Identity,
                        scale=SC[:, 8:9]), "q")
            emit_dg(6 if g < 12 else 0)
            xt_free[sl] = (tk_xn_v, tk_sq)
            if g == 11:
                w("tensor", tk_wm, tk_idf)
                tk_tw = inc("tensor", lambda e: e.transpose(
                    P[0][0:1, 128:256], SC[:, 6:7], identf[:]), "t")
                w("vector", tk_tw)
                S.emit("vector", lambda e: e.tensor_reduce(
                    SC[0:1, 1:2], P[0][0:1, 128:256], axis=mybir.AxisListType.X, op=OP.max))
                drain("vector")
                tk_aw = inc("vector", lambda e: e.tensor_copy(M1R[:, 0:1], SC[0:1, 1:2]), "v")
                w("tensor", tk_aw)
                tk_bc2 = inc("tensor", lambda e: e.matmul(P[1][:, 1:2], onesr[0:1, :],
                                                          M1R[:, 0:1],
                                                          start=True, stop=True), "t")
                w("vector", tk_bc2)
                tk_aw128 = inc("vector", lambda e: e.tensor_copy(SC[:, 7:8], P[1][:, 1:2]), "v")
                drain("vector")
                S.emit("vector", lambda e: e.tensor_scalar_max(SC[:, 1:2], SC[:, 7:8], 1e-12))
                drain("vector")
                S.emit("vector", lambda e: e.reciprocal(SC[:, 11:12], SC[:, 1:2]))
                drain("vector")
                S.emit("vector", lambda e: e.tensor_scalar(SC[:, 12:13], SC[:, 11:12],
                                                           448.0, 448.0, OP.mult, OP.min))
                drain("vector")
                S.emit("vector", lambda e: e.tensor_scalar_mul(SC[:, 8:9], SC[:, 12:13], 0.5))
                tk_sw = inc("vector", lambda e: e.reciprocal(SC[:, 9:10], SC[:, 12:13]), "v")
                drain("vector")
            if g == 0:
                w("vector", tk_hx, tk_hsq)
                S.emit("vector", lambda e: e.reduce_sum(HCOL[0:3, 0:1], XS[0:3, hx0:hx0 + DM],
                                                        axis=mybir.AxisListType.X))
                S.emit("vector", lambda e: e.reduce_sum(HCOL[0:3, 1:2], XS[0:3, hs0:hs0 + DM],
                                                        axis=mybir.AxisListType.X))
                drain("vector")
                S.emit("vector", lambda e: e.tensor_scalar_mul(HCOL[0:3, 2:3], HCOL[0:3, 0:1],
                                                               1.0 / DM))
                drain("vector")
                S.emit("vector", lambda e: e.tensor_mul(HCOL[0:3, 0:1], HCOL[0:3, 2:3],
                                                        HCOL[0:3, 2:3]))
                S.emit("vector", lambda e: e.tensor_scalar_mul(HCOL[0:3, 3:4], HCOL[0:3, 1:2],
                                                               1.0 / DM))
                drain("vector")
                tk_hv = inc("vector", lambda e: e.tensor_sub(HCOL[0:3, 1:2], HCOL[0:3, 3:4],
                                                             HCOL[0:3, 0:1]), "v")
                w("scalar", tk_hv)
                tk_hsd = inc("scalar", lambda e: e.activation(
                    out=HCOL[0:3, 3:4], in_=HCOL[0:3, 1:2],
                    func=AF.Sqrt, bias=cst[0:3, 0:1]), "s")
                w("vector", tk_hsd)
                S.emit("vector", lambda e: e.reciprocal(HCOL[0:3, 0:1], HCOL[0:3, 3:4]))
                drain("vector")
                S.emit("vector", lambda e: e.tensor_scalar_mul(HCOL[0:3, 1:2],
                                                               HCOL[0:3, 2:3], -1.0))
                drain("vector")
                tk_hap = inc("vector", lambda e: e.tensor_scalar(
                    XN[1][0:3, :], XS[0:3, hx0:hx0 + DM],
                    HCOL[0:3, 1:2], HCOL[0:3, 0:1], OP.add, OP.mult), "v")
                drain("vector")
                w("tensor", tk_hap, tk_idf)
                for f in range(KD):
                    tk_ht = inc("tensor", lambda e, f=f: e.transpose(
                        P[2][:, f * 3:(f + 1) * 3], XN[1][0:3, f * 128:(f + 1) * 128],
                        identf[0:3, 0:3]), "h")
                w("scalar", tk_ht)
                tk_hf = inc("scalar", lambda e: e.copy(out=HFM[:], in_=P[2][:, 0:24]), "s")
                w("tensor", tk_hf, tk_w1)
                for ec in range(EC):
                    for kc in range(KD):
                        tk_hm = inc("tensor", lambda e, ec=ec, kc=kc: e.matmul(
                            P[3][:, ec * 3:(ec + 1) * 3],
                            W1[:, kc * E + ec * 128: kc * E + ec * 128 + 128],
                            HFM[:, kc * 3:(kc + 1) * 3],
                            start=(ec == 0 and kc == 0), stop=(kc == 7),
                            skip_group_check=True), "t")
        tk_x_pre = [None, None]
        for s_ in range(2):
            w("sync", xt_free[s_])
            tk_x_pre[s_] = dma("sync", XT[s_][:], x_in[s_ * 128:(s_ + 1) * 128, :], "x")
            xt_free[s_] = None
        w("tensor", tk_ax, tk_idf)
        tk_tr = inc("tensor", lambda e: e.transpose(P[0][0:1, 0:128], SC[:, 0:1], identf[:]), "t")
        w("vector", tk_tr)
        S.emit("vector", lambda e: e.tensor_reduce(SC[0:1, 1:2], P[0][0:1, 0:128],
                                                   axis=mybir.AxisListType.X, op=OP.max))
        drain("vector")
        tk_axs = inc("vector", lambda e: e.tensor_copy(M1R[:, 0:1], SC[0:1, 1:2]), "v")
        w("tensor", tk_axs)
        tk_bc = inc("tensor", lambda e: e.matmul(P[1][:, 0:1], onesr[0:1, :], M1R[:, 0:1],
                                                 start=True, stop=True), "t")
        w("vector", tk_bc, tk_small)
        tk_ax128 = inc("vector", lambda e: e.tensor_copy(SC[:, 2:3], P[1][:, 0:1]), "v")
        drain("vector")
        tk_p1 = inc("vector", lambda e: e.tensor_scalar(PAY1[:], M8, SC[:, 2:3],
                                                        None, OP.mult), "v")
        w("sync", tk_p1)
        tk_ccin = dma("sync", cc1i.ap(), PAY1[:], "x")
        w("gpsimd", tk_ccin)
        tk_cc1 = inc("gpsimd", lambda e: e.collective_compute(
            "AllReduce", OP.add, replica_groups=[list(range(NCORE))],
            ins=[cc1i.ap().opt()], outs=[cc1o.ap().opt()]), "c")
        w("gpsimd", tk_cc1)
        w("sync", tk_cc1)
        tk_g1 = dma("sync", G1[:], cc1o.ap(), "x")
        tk_w3b = dma("gpsimd", W3.ap()[:, 8 * DM:16 * DM].rearrange(
            "p (ec c) -> p ec c", c=DM),
            w_out[1024:2048, :].rearrange("(ec p) c -> p ec c", p=128), "w")
        tk_diag = [None, None]
        tk_rowb = [None, None]
        tk_diag_free = [None, None]
        tk_m1r = tk_m2r = None

        def emit_lnprep(tin):
            ds_ = tin % 2
            gg = tin * 2
            w("vector", tk_diag_free[ds_])
            for s_ in range(2):
                S.emit("vector", lambda e, s_=s_, gg=gg: e.tensor_mul(
                    MNI[:, gg + s_:gg + s_ + 1], MN[:, gg + s_:gg + s_ + 1],
                    MI[:, gg + s_:gg + s_ + 1]))
            drain("vector")
            tkm = None
            for s_ in range(2):
                tkm = inc("vector", lambda e, s_=s_, gg=gg, ds_=ds_: e.tensor_scalar(
                    DIAGS[ds_][:, s_ * 128:(s_ + 1) * 128], identf[:],
                    MI[:, gg + s_:gg + s_ + 1], None, OP.mult), "v")
            tk_diag[ds_] = tkm
            w("tensor", tkm, tk_m1r, tk_m2r, tk_idf)
            inc("tensor", lambda e, gg=gg: e.matmul(
                P[6][0:1, 256:384], MNI[:, gg:gg + 1], identf[:],
                start=True, stop=True), "t")
            tkt = inc("tensor", lambda e, gg=gg: e.matmul(
                P[7][0:1, 256:384], MNI[:, gg + 1:gg + 2], identf[:],
                start=True, stop=True), "t")
            w("scalar", tkt)
            tkr = None
            for r4 in range(4):
                for s_ in range(2):
                    tkr = inc("scalar", lambda e, r4=r4, s_=s_, ds_=ds_: e.copy(
                        out=ROWB[ds_][s_ * 32:s_ * 32 + 1, r4 * 128:(r4 + 1) * 128],
                        in_=P[6 + s_][0:1, 256:384]), "s")
            tk_rowb[ds_] = tkr
            return tkr
        tk_prep_evac = emit_lnprep(0)
        pfree = {0: tk_axs, 1: tk_ax128, 2: tk_hf, 3: tk_he,
                 4: None, 5: None, 6: None, 7: None}
        tk_store = [None] * 16
        tk_xs_free = None
        tk_xnb_free = None
        tk_xq_free = None
        tk_drb_free = None
        tk_szb_free = [None, None]
        tk_ynbf_free = [None, None]
        tk_ygb_free = [None, None, None, None]
        tk_xsq_free = [None, None]
        tk_gt_free = [None, None]
        tk_gf_free = [None, None]
        tk_sgb_free = [None, None]
        tk_sca_free = None
        tk_qab_prev = None
        tk_scan_done = [None] * NTI

        for ti in range(NTI):
            g0 = ti * 2
            if ti == 0:
                tk_x = tk_x_pre
            else:
                tk_x = [None, None]
                for s_ in range(2):
                    g = g0 + s_
                    sl = g % 2
                    w("sync", xt_free[sl])
                    tk_x[s_] = dma("sync", XT[sl][:], x_in[g * 128:(g + 1) * 128, :], "x")
            tk_tp = [None] * 4
            ds_ = ti % 2
            for s_ in range(2):
                g = g0 + s_
                sl = g % 2
                for half in range(2):
                    bank = 4 + s_ * 2 + half
                    w("tensor", pfree[bank], tk_x[s_], tk_diag[ds_], tk_rowb[ds_])
                    for f4 in range(4):
                        f = half * 4 + f4
                        inc("tensor", lambda e, sl=sl, f=f, bank=bank, f4=f4, s_=s_, ds_=ds_:
                            e.matmul(P[bank][:, f4 * 128:(f4 + 1) * 128],
                                     XT[sl][:, f * 128:(f + 1) * 128],
                                     DIAGS[ds_][:, s_ * 128:(s_ + 1) * 128],
                                     start=(f4 == 0), stop=False,
                                     skip_group_check=True), "t")
                    tk_tp[bank - 4] = inc("tensor", lambda e, bank=bank, s_=s_, ds_=ds_:
                        e.matmul(P[bank][:, 0:512], onesrb[s_ * 32:s_ * 32 + 1, :],
                                 ROWB[ds_][s_ * 32:s_ * 32 + 1, 0:512],
                                 start=False, stop=True, skip_group_check=True), "t")
            tk_diag_free[ds_] = (tk_tp[0], tk_tp[1], tk_tp[2], tk_tp[3])
            def emit_sx_chain():
                w("vector", tk_g1)
                S.emit("vector", lambda e: e.tensor_reduce(
                    SC[:, 3:4], G1[:], axis=mybir.AxisListType.X, op=OP.max))
                drain("vector")
                S.emit("vector", lambda e: e.tensor_scalar_max(SC[:, 1:2], SC[:, 3:4], 1e-12))
                drain("vector")
                S.emit("vector", lambda e: e.reciprocal(SC[:, 11:12], SC[:, 1:2]))
                drain("vector")
                S.emit("vector", lambda e: e.tensor_scalar(SC[:, 12:13], SC[:, 11:12],
                                                           448.0, 448.0, OP.mult, OP.min))
                drain("vector")
                S.emit("vector", lambda e: e.tensor_scalar_mul(SC[:, 4:5], SC[:, 12:13], 0.5))
                tk = inc("vector", lambda e: e.reciprocal(SC[:, 5:6], SC[:, 12:13]), "sx")
                drain("vector")
                S.emit("vector", lambda e: e.tensor_scalar_mul(SC[:, 11:12], SC[:, 5:6], 4.0))
                drain("vector")
                tk4 = inc("vector", lambda e: e.tensor_mul(SC[:, 10:11], SC[:, 11:12],
                                                           SC[:, 9:10]), "v")
                drain("vector")
                return tk, tk4
            tk_xnb = [None, None]
            tk_xq = [None, None]
            for s_ in range(2):
                w("scalar", tk_tp[s_ * 2], tk_tp[s_ * 2 + 1], tk_xnb_free)
                for half in range(2):
                    bank = 4 + s_ * 2 + half
                    tk_xnb[s_] = inc("scalar", lambda e, s_=s_, half=half, bank=bank: e.copy(
                        out=XNB[:, s_ * DM + half * 512: s_ * DM + half * 512 + 512],
                        in_=P[bank][:]), "s")
            def emit_xq():
                for s_ in range(2):
                    w("scalar", ("sx", semsx, 1), tk_xq_free)
                    for half in range(2):
                        bank = 4 + s_ * 2 + half
                        tk_xq[s_] = inc("scalar", lambda e, s_=s_, half=half, bank=bank:
                                        e.activation(
                            out=XQ[:, s_ * DM + half * 512: s_ * DM + half * 512 + 512],
                            in_=P[bank][:], func=AF.Identity, scale=SC[:, 4:5]), "s")
                return (tk_xq[0], tk_xq[1])
            if ti > 0:
                pf47 = emit_xq()

            w("tensor", tk_xnb[0], tk_xnb[1], pfree[2], tk_wb0)
            for kc in range(KD):
                tk_b0 = inc("tensor", lambda e, kc=kc: e.matmul(
                    P[2][0:1, 0:TT], wb0[:, kc:kc + 1], xnb3[:, :, kc * 128:(kc + 1) * 128],
                    start=(kc == 0), stop=(kc == 7)), "t")
            w("scalar", tk_b0)
            tk_b0r = inc("scalar", lambda e: e.copy(out=B0R[:], in_=P[2][0:1, 0:TT]), "s")
            pfree[2] = tk_b0r

            tk_proj = [None] * EC
            tk_xpev = [None] * EC
            tk_silu = [None] * EC
            tk_xsq = [None] * EC
            tk_convd = [None] * EC
            tk_cdr = [None] * EC

            def emit_proj(e_, ti=ti):
                bank = e_ % 2
                w("tensor", pfree[bank])
                if e_ >= 2:
                    w("tensor", tk_xpev[e_ - 2])
                for kc in range(KD):
                    tk = inc("tensor", lambda e, kc=kc, e_=e_, bank=bank: e.matmul(
                        P[bank][0:128, 0:TT],
                        W1[:, kc * E + e_ * 128: kc * E + e_ * 128 + 128],
                        xnb3[:, :, kc * 128:(kc + 1) * 128],
                        start=(kc == 0), stop=(kc == 7)), "t")
                tk_proj[e_] = tk

            def emit_xpev(e_):
                hb = e_ % 2
                base = hb * 260
                w("vector", tk_proj[e_])
                if e_ >= 2:
                    w("vector", tk_convd[e_ - 2])
                if ti == 0:
                    w("vector", tk_he)
                S.emit("vector", lambda e, e_=e_, base=base: e.tensor_copy(
                    out=XPB[:, base + 1:base + 4], in_=XPT[:, e_ * 3:(e_ + 1) * 3]))
                S.emit("vector", lambda e, e_=e_, base=base, hb=hb: e.tensor_copy(
                    out=XPB[:, base + 4:base + 260], in_=P[hb][0:128, 0:TT]))
                drain("vector")
                tk_xpev[e_] = inc("vector", lambda e, e_=e_, base=base: e.tensor_copy(
                    out=XPT[:, e_ * 3:(e_ + 1) * 3], in_=XPB[:, base + 257:base + 260]), "v")
                drain("vector")

            def emit_conv(e_):
                hb = e_ % 2
                base = hb * 260
                bank = 2 + e_ % 2
                w("tensor", tk_xpev[e_], pfree[bank])
                for k in range(4):
                    j = e_ * 4 + k
                    tk = inc("tensor", lambda e, j=j, k=k, base=base, bank=bank: e.matmul(
                        P[bank][0:128, 0:TT], DG[:, j * 128:(j + 1) * 128],
                        XPB[:, base + 1 + k: base + 1 + k + TT],
                        start=(k == 0), stop=(k == 3)), "t")
                tk_convd[e_] = tk

            def emit_silu(e_, ti=ti):
                bank = 2 + e_ % 2
                w("scalar", tk_convd[e_], tk_xs_free)
                if ti == 0 and e_ == 8:
                    w("scalar", tk_hap)
                tk = inc("scalar", lambda e, e_=e_, bank=bank: e.activation(
                    out=XS[:, e_ * TT:(e_ + 1) * TT], in_=P[bank][0:128, 0:TT],
                    func=AF.Silu, bias=CB[:, e_:e_ + 1]), "s")
                if e_ == 0:
                    tk = inc("scalar", lambda e, bank=bank: e.activation(
                        out=XS0R[:], in_=P[bank][0:1, 0:TT],
                        func=AF.Silu, bias=CB[0:1, 0:1]), "s")
                tk_silu[e_] = tk
                pfree[bank] = tk

            def emit_xsq(e_):
                w("vector", tk_silu[e_])
                if e_ >= 4:
                    w("vector", tk_stats_q[(e_ - 4) // 4])
                elif ti > 0:
                    w("vector", tk_xsq_free[0])
                tk_xsq[e_] = inc("vector", lambda e, e_=e_: e.tensor_mul(
                    XSQ2[:, (e_ % 4) * TT:(e_ % 4 + 1) * TT], XS[:, e_ * TT:(e_ + 1) * TT],
                    XS[:, e_ * TT:(e_ + 1) * TT]), "v")

            def emit_stats_half(r0, r1):
                w("tensor", tk_silu[r1 - 1], tk_xsq[r1 - 1], tk_wx)
                if r0 == 0:
                    w("tensor", pf47[0], pf47[1], tk_prep_evac)
                tk = None
                for e_ in range(r0, r1):
                    inc("tensor", lambda e, e_=e_: e.matmul(
                        P[6][0:1, 0:TT], Dm1[:, e_:e_ + 1], XS[:, e_ * TT:(e_ + 1) * TT],
                        start=(e_ == 0), stop=(e_ == EC - 1)), "t")
                    tk = inc("tensor", lambda e, e_=e_: e.matmul(
                        P[7][0:1, 0:TT], Dm2[:, e_:e_ + 1],
                        XSQ2[:, (e_ % 4) * TT:(e_ % 4 + 1) * TT],
                        start=(e_ == 0), stop=(e_ == EC - 1)), "t")
                    inc("tensor", lambda e, e_=e_: e.matmul(
                        P[5][0:128, 0:TT], Wxs[:, e_ * 192 + 64:(e_ + 1) * 192],
                        XS[:, e_ * TT:(e_ + 1) * TT],
                        start=(e_ == 0), stop=(e_ == EC - 1)), "t")
                    tk_cdr[e_] = inc("tensor", lambda e, e_=e_: e.matmul(
                        P[4][0:64, 0:TT], Wxs[:, e_ * 192: e_ * 192 + 64],
                        XS[:, e_ * TT:(e_ + 1) * TT],
                        start=(e_ == 0), stop=(e_ == EC - 1)), "t")
                return tk
            tk_stats_q = [None, None, None, None]
            for step in range(EC + 4):
                if ti == 0 and step == 5:
                    tk_sx, tk_s4 = emit_sx_chain()
                    pf47 = emit_xq()
                if ti < NTI - 1 and step == 3:
                    tk_prep_evac = emit_lnprep(ti + 1)
                if step < EC:
                    emit_proj(step)
                    emit_xpev(step)
                if 1 <= step <= EC:
                    emit_conv(step - 1)
                    emit_silu(step - 1)
                if 4 <= step <= EC + 3:
                    emit_xsq(step - 4)
                if step in (7, 11, 15):
                    tk_stats_q[(step - 7) // 4] = emit_stats_half(step - 7, step - 3)
            emit_stats_half(12, EC)
            tk_eloop = tk_cdr[EC - 1]
            tk_xsq_free[0] = tk_eloop

            w("scalar", tk_eloop, tk_drb_free)
            tk_drb = inc("scalar", lambda e: e.copy(out=DRB[:], in_=P[4][0:64, 0:TT]), "s")
            pfree[4] = tk_drb
            tk_cl = inc("scalar", lambda e, ti=ti: e.copy(
                out=C_loc[:, ti * TT:(ti + 1) * TT], in_=P[5][0:128, 0:TT]), "s")
            pfree[5] = tk_cl
            tk_m1r = inc("scalar", lambda e: e.copy(out=M1R, in_=P[6][0:1, 0:TT]), "s")
            tk_m2r = inc("scalar", lambda e: e.copy(out=M2R, in_=P[7][0:1, 0:TT]), "s")
            pfree[6] = tk_m1r
            pfree[7] = tk_m2r
            w("vector", tk_m1r, tk_m2r)
            S.emit("vector", lambda e: e.tensor_mul(RSTR, M1R, M1R))
            drain("vector")
            tk_var2 = inc("vector", lambda e: e.tensor_sub(VARR, M2R, RSTR), "v")
            w("tensor", tk_drb, pfree[2], tk_wdt)
            tk_dmm = inc("tensor", lambda e: e.matmul(
                P[2][0:1, 0:TT], wdt0b[:], DRB[:], start=True, stop=True), "t")
            w("scalar", tk_dmm)
            tk_dltev = inc("scalar", lambda e: e.copy(out=T0R[:], in_=P[2][0:1, 0:TT]), "s")
            pfree[2] = tk_dltev
            drain("scalar")
            S.emit("scalar", lambda e: e.activation(out=T1R[:], in_=T0R[:], func=AF.Abs,
                                                    bias=BDT))
            w("scalar", tk_b0r)
            tk_habs = inc("scalar", lambda e: e.activation(out=BM0R[:], in_=B0R[:],
                                                           func=AF.Abs), "s")
            drain("scalar")
            S.emit("scalar", lambda e: e.activation(out=DLTR[:], in_=T1R[:], func=AF.Exp,
                                                    scale=-1.0))
            S.emit("scalar", lambda e: e.activation(out=BDR[:], in_=BM0R[:], func=AF.Exp,
                                                    scale=-1.0))
            w("scalar", tk_var2)
            S.emit("scalar", lambda e: e.activation(out=M2R, in_=VARR, func=AF.Ln,
                                                    bias=cst[0:1, 0:1]))
            drain("scalar")
            tk_sp1 = inc("scalar", lambda e: e.activation(out=T1R[:], in_=DLTR[:],
                                                          func=AF.Ln, bias=1.0), "s")
            tk_sp2 = inc("scalar", lambda e: e.activation(out=BM0R[:], in_=BDR[:],
                                                          func=AF.Ln, bias=1.0), "s")
            tk_rst = inc("scalar", lambda e: e.activation(out=RSTR, in_=M2R, func=AF.Exp,
                                                          scale=cst[0:1, 1:2]), "s")
            drain("scalar")

            tk_zmm = [None] * EC
            tk_sz = [None] * EC
            tk_yg = [None] * EC
            tk_wout = [None] * EC
            tk_m1b = tk_rsb = None
            tk_dltev = tk_dl1 = tk_bd = tk_aev = tk_bev = None
            tk_ynf1_free = [None]

            def emit_z(e_):
                bank = e_ % 2
                if e_ < 2:
                    w("tensor", tk_xpev[EC - 2 + e_], pfree[e_ % 2])
                else:
                    w("tensor", tk_sz[e_ - 2])
                w("tensor", tk_w2)
                for kc in range(KD):
                    tk = inc("tensor", lambda e, kc=kc, e_=e_, bank=bank: e.matmul(
                        P[bank][0:128, 0:TT],
                        W2[:, kc * E + e_ * 128: kc * E + e_ * 128 + 128],
                        xnb3[:, :, kc * 128:(kc + 1) * 128],
                        start=(kc == 0), stop=(kc == 7)), "t")
                tk_zmm[e_] = tk

            def emit_ychain(e_):
                bank = e_ % 2
                sl = e_ % 2
                gsl = e_ % 4
                w("scalar", tk_zmm[e_], tk_szb_free[sl])
                S.emit("scalar", lambda e, bank=bank: e.activation(
                    out=TNH[:], in_=P[bank][0:128, 0:TT], func=AF.Tanh,
                    scale=cst[:, 2:3]))
                drain("scalar")
                tk_sz[e_] = inc("scalar", lambda e, sl=sl: e.activation(
                    out=SZB[:, sl * TT:(sl + 1) * TT], in_=TNH[:],
                    func=AF.Identity, scale=cst[:, 2:3], bias=cst[:, 2:3]), "s")
                drain("scalar")
                w("vector", tk_m1b, tk_rsb, tk_eloop, tk_ln)
                S.emit("vector", lambda e, e_=e_: e.scalar_tensor_tensor(
                    out=YNF[0][:], in0=XS[:, e_ * TT:(e_ + 1) * TT], scalar=DD[:, e_:e_ + 1],
                    in1=M1B[:], op0=OP.mult, op1=OP.subtract))
                drain("vector")
                w("vector", tk_ynf1_free[0])
                tk_stt2 = inc("vector", lambda e, e_=e_: e.scalar_tensor_tensor(
                    out=YNF[1][:], in0=YNF[0][:], scalar=1.0,
                    in1=RSB[:], op0=OP.mult, op1=OP.mult), "v")
                drain("vector")
                w("scalar", tk_stt2, tk_ynbf_free[sl])
                tk_yn = inc("scalar", lambda e, e_=e_, sl=sl: e.activation(
                    out=YNBF[:, sl * TT:(sl + 1) * TT], in_=YNF[1][:], func=AF.Identity,
                    scale=LNG[:, e_:e_ + 1], bias=LNB[:, e_:e_ + 1]), "s")
                w("vector", tk_sz[e_], tk_yn, tk_ygb_free[gsl])
                tk_yg[e_] = inc("vector", lambda e, sl=sl, gsl=gsl: e.tensor_mul(
                    YGB[:, gsl * TT:(gsl + 1) * TT], YNBF[:, sl * TT:(sl + 1) * TT],
                    SZB[:, sl * TT:(sl + 1) * TT]), "v")
                tk_ynbf_free[sl] = tk_yg[e_]
                tk_szb_free[sl] = tk_yg[e_]
                tk_ynf1_free[0] = tk_yn

            def emit_wout(e_):
                gsl = e_ % 4
                w("tensor", tk_yg[e_], tk_w3a if e_ < 8 else tk_w3b)
                for d_ in range(KD):
                    bank = 4 + d_ // 2
                    col = (d_ % 2) * TT
                    st = (e_ == 0 and d_ % 2 == 0)
                    tk = inc("tensor", lambda e, e_=e_, d_=d_, bank=bank, col=col, st=st, gsl=gsl:
                             e.matmul(P[bank][0:128, col:col + TT],
                                      W3[:, e_ * DM + d_ * 128: e_ * DM + d_ * 128 + 128],
                                      YGB[:, gsl * TT:(gsl + 1) * TT],
                                      start=st, stop=(e_ == EC - 1),
                                      skip_group_check=True), "t")
                tk_wout[e_] = tk
                tk_ygb_free[gsl] = tk

            def emit_scan_level(li, ti=ti):
                sl0 = ti * TT
                d = 1 << li
                if li == 0:
                    w("vector", tk_aev, tk_bev)
                    drain("vector")
                S.emit("vector", lambda e, sl0=sl0: e.tensor_copy(SCA[:], A_loc[:, sl0:sl0 + TT]))
                S.emit("vector", lambda e, sl0=sl0: e.tensor_copy(SCB[:], B_loc[:, sl0:sl0 + TT]))
                drain("vector")
                S.emit("vector", lambda e, sl0=sl0, d=d: e.tensor_mul(
                    A_loc[:, sl0 + d:sl0 + TT], SCA[:, d:TT], SCA[:, 0:TT - d]))
                S.emit("vector", lambda e, d=d: e.tensor_mul(
                    SCC[:, 0:TT - d], SCA[:, d:TT], SCB[:, 0:TT - d]))
                drain("vector")
                S.emit("vector", lambda e, sl0=sl0, d=d: e.tensor_add(
                    B_loc[:, sl0 + d:sl0 + TT], SCB[:, d:TT], SCC[:, 0:TT - d]))
                drain("vector")
                if li == 7:
                    tk_scan_done[ti] = inc("vector", lambda e, sl0=sl0: e.tensor_copy(
                        SCA[:, 0:1], A_loc[:, sl0:sl0 + 1]), "v")

            for step in range(EC + 3):
                if 2 <= step < EC + 2:
                    emit_ychain(step - 2)
                if 11 <= step < 19:
                    emit_scan_level(step - 11)
                if step < EC:
                    emit_z(step)
                if step == 0:
                    w("tensor", tk_m1r, pfree[2])
                    tk_m1bc = inc("tensor", lambda e: e.matmul(
                        P[2][0:128, 0:TT], onesr[0:1, :], M1R, start=True, stop=True), "t")
                    w("vector", tk_m1bc)
                    tk_m1b = inc("vector", lambda e: e.tensor_copy(M1B[:], P[2][0:128, 0:TT]), "v")
                    pfree[2] = tk_m1b
                if step == 1:
                    w("tensor", tk_rst, pfree[3])
                    tk_rsbc = inc("tensor", lambda e: e.matmul(
                        P[3][0:128, 0:TT], onesr[32:33, :], RSTR,
                        start=True, stop=True), "t")
                    w("vector", tk_rsbc)
                    tk_rsb = inc("vector", lambda e: e.tensor_copy(RSB[:], P[3][0:128, 0:TT]), "v")
                    pfree[3] = tk_rsb
                if step == 5:
                    w("vector", tk_sp1)
                    S.emit("vector", lambda e: e.tensor_scalar(DLTR[:], T0R[:], BDT,
                                                               0.0, OP.add, OP.max))
                    drain("vector")
                    S.emit("vector", lambda e: e.tensor_add(T0R[:], DLTR[:], T1R[:]))
                    drain("vector")
                    tk_dl1 = inc("vector", lambda e: e.tensor_scalar(DLTR[:], T0R[:], 1e-6,
                                                                     100.0, OP.max, OP.min), "v")
                    drain("vector")
                if step == 9:
                    w("vector", tk_sp2, tk_dl1)
                    S.emit("vector", lambda e: e.tensor_scalar_max(BDR[:], B0R[:], 0.0))
                    drain("vector")
                    S.emit("vector", lambda e: e.tensor_add(T0R[:], BDR[:], BM0R[:]))
                    drain("vector")
                    w("vector", tk_silu[0])
                    S.emit("vector", lambda e: e.tensor_mul(T1R[:], DLTR[:], T0R[:]))
                    drain("vector")
                    tk_bd = inc("vector", lambda e: e.tensor_mul(BDR[:], T1R[:], XS0R[:]), "v")
                    drain("vector")
                if step == 8:
                    w("tensor", tk_dl1, pfree[2])
                    tk_abc = inc("tensor", lambda e: e.matmul(P[2][0:128, 0:TT], onesrb[0:1, :],
                                                              DLTR[:], start=True, stop=True), "t")
                    w("scalar", tk_abc, tk_a0)
                    tk_aev = inc("scalar", lambda e, ti=ti: e.activation(
                        out=A_loc[:, ti * TT:(ti + 1) * TT], in_=P[2][0:128, 0:TT],
                        func=AF.Exp, scale=A0c[:]), "s")
                    pfree[2] = tk_aev
                if step == 10:
                    w("tensor", tk_bd, pfree[3])
                    tk_bbc = inc("tensor", lambda e: e.matmul(P[3][0:128, 0:TT], onesrb[0:1, :],
                                                              BDR[:], start=True, stop=True), "t")
                    w("scalar", tk_bbc)
                    tk_bev = inc("scalar", lambda e, ti=ti: e.copy(
                        out=B_loc[:, ti * TT:(ti + 1) * TT], in_=P[3][0:128, 0:TT]), "s")
                    pfree[3] = tk_bev
                if step >= 3 and step - 3 < EC:
                    emit_wout(step - 3)
            tk_youts = tk_wout[EC - 1]
            tk_xs_free = tk_yg[EC - 1]



            tk_fmm = [None] * KD
            tk_epi = [None] * KD
            tk_gfo = [None] * KD
            tk_sg = [None] * KD
            tk_tps0 = {2: None, 3: None}
            tk_tps1 = {4: None, 5: None}
            tk_c1s = [None, None]
            tk_adds0 = [None, None]
            tk_adds1 = [None, None]

            def emit_ffn(o):
                bank = o % 2
                w("tensor", tk_xq[0], tk_xq[1], tk_q)
                if o < 2:
                    w("tensor", tk_sz[EC - 2 + o])
                else:
                    w("tensor", tk_epi[o - 2])
                for kc in range(KD):
                    tk = inc("tensor", lambda e, kc=kc, o=o, bank=bank: e.matmul(
                        P[bank][0:128, 0:TT],
                        Wq[:, kc * DM + o * 128: kc * DM + o * 128 + 128],
                        xq3[:, :, kc * 128:(kc + 1) * 128],
                        start=(kc == 0), stop=(kc == 7)), "t")
                tk_fmm[o] = tk

            def emit_epi(o):
                bank = o % 2
                sl = o % 2
                w("vector", tk_fmm[o], tk_s4, tk_gt_free[sl])
                tk_epi[o] = inc("vector", lambda e, o=o, bank=bank, sl=sl: e.tensor_scalar(
                    GT[:, sl * TT:(sl + 1) * TT], P[bank][0:128, 0:TT],
                    SC[:, 10:11], FFB[:, o:o + 1], OP.mult, OP.add), "v")
                w("scalar", tk_epi[o], tk_gf_free[sl])
                tk_gfo[o] = inc("scalar", lambda e, sl=sl: e.activation(
                    out=GF[:, sl * TT:(sl + 1) * TT], in_=GT[:, sl * TT:(sl + 1) * TT],
                    func=AF.Silu), "s")
                sbank = 4 + o // 2
                scol = (o % 2) * TT
                w("vector", tk_gfo[o], tk_youts, tk_sgb_free[sl])
                tk_sg[o] = inc("vector", lambda e, sl=sl, sbank=sbank, scol=scol: e.tensor_mul(
                    SGB[:, sl * TT:(sl + 1) * TT], GF[:, sl * TT:(sl + 1) * TT],
                    P[sbank][0:128, scol:scol + TT]), "v")
                tk_gt_free[sl] = tk_gfo[o]
                tk_gf_free[sl] = tk_sg[o]

            def emit_sgt(o):
                sl = o % 2
                b0_ = 2 if o < 4 else 3
                w("tensor", tk_sg[o])
                if o == 0:
                    w("tensor", pfree[2])
                if o == 4:
                    w("tensor", pfree[3])
                tk_tps0[b0_] = inc("tensor", lambda e, o=o, sl=sl, b0_=b0_: e.transpose(
                    P[b0_][:, (o % 4) * 128:(o % 4 + 1) * 128],
                    SGB[:, sl * TT: sl * TT + 128], identf[:]), "t")
                if o < 2:
                    w("vector", tk_tps0[b0_])
                    tk_c1 = inc("vector", lambda e, o=o, sl=sl: e.tensor_copy(
                        SG1[:, o * 128:(o + 1) * 128],
                        SGB[:, sl * TT + 128: sl * TT + 256]), "v")
                    tk_c1s[o] = tk_c1
                    tk_sgb_free[sl] = (tk_tps0[b0_], tk_c1)
                else:
                    b1_ = 4 if o < 4 else 5
                    if o == 2:
                        w("tensor", tk_sg[1], tk_c1s[0], tk_c1s[1])
                        for oo in range(2):
                            inc("tensor", lambda e, oo=oo: e.transpose(
                                P[4][:, oo * 128:(oo + 1) * 128],
                                SG1[:, oo * 128:(oo + 1) * 128], identf[:]), "t")
                    if o == 4:
                        w("tensor", tk_sg[3])
                    tk_tps1[b1_] = inc("tensor", lambda e, o=o, sl=sl, b1_=b1_: e.transpose(
                        P[b1_][:, (o % 4) * 128:(o % 4 + 1) * 128],
                        SGB[:, sl * TT + 128: sl * TT + 256], identf[:]), "t")
                    tk_sgb_free[sl] = tk_tps1[b1_]

            for step in range(KD + 2):
                if step < KD:
                    emit_ffn(step)
                if 1 <= step <= KD:
                    emit_epi(step - 1)
                if step >= 2:
                    emit_sgt(step - 2)
                if step - 2 == 3:
                    w("vector", tk_tps0[2], tk_xq[1], tk_qab_prev)
                    tk_adds0[0] = inc("vector", lambda e, g0=g0: e.tensor_add(
                        QA[:, 0:512], P[2][:, 0:512], XT[g0 % 2][:, 0:512]), "v")
                    pfree[2] = tk_adds0[0]
                if step - 2 == 7:
                    w("vector", tk_tps0[3])
                    tk_adds0[1] = inc("vector", lambda e, g0=g0: e.tensor_add(
                        QA[:, 512:1024], P[3][:, 0:512], XT[g0 % 2][:, 512:1024]), "v")
                    pfree[3] = tk_adds0[1]
                    w("vector", tk_tps1[4], tk_xq[1], tk_qab_prev)
                    tk_adds1[0] = inc("vector", lambda e, g0=g0: e.tensor_add(
                        QB[:, 0:512], P[4][:, 0:512], XT[(g0 + 1) % 2][:, 0:512]), "v")
                    pfree[4] = tk_adds1[0]
                    w("vector", tk_tps1[5])
                    tk_adds1[1] = inc("vector", lambda e, g0=g0: e.tensor_add(
                        QB[:, 512:1024], P[5][:, 0:512], XT[(g0 + 1) % 2][:, 512:1024]), "v")
                    pfree[5] = tk_adds1[1]
            for s_ in range(2):
                g = g0 + s_
                w("sync", tk_adds0[0] if s_ == 0 else tk_adds1[0])
                w("sync", tk_adds0[1] if s_ == 0 else tk_adds1[1])
                tk_store[g] = dma("sync", y_out[g * 128:(g + 1) * 128, :],
                                  (QA if s_ == 0 else QB)[:], "o")
            xt_free[0] = (tk_adds0[0], tk_adds0[1])
            xt_free[1] = (tk_adds1[0], tk_adds1[1])
            tk_qab_prev = (tk_store[g0], tk_store[g0 + 1])
            tk_xnb_free = tk_zmm[EC - 1]
            tk_xq_free = tk_fmm[KD - 1]
            tk_drb_free = tk_dmm
            pfree[0] = tk_epi[KD - 2]
            pfree[1] = tk_epi[KD - 1]

        w("vector", tk_scan_done[NTI - 1])
        S.emit("vector", lambda e: e.memset(LL[:, 0:1], 0.0))
        S.emit("vector", lambda e: e.memset(PP[:, 0:1], 1.0))
        for q in range(NTI):
            ec_ = (q + 1) * TT - 1
            S.emit("vector", lambda e, q=q, ec_=ec_: e.tensor_copy(
                EQF[:, 2 * q:2 * q + 1], A_loc[:, ec_:ec_ + 1]))
            S.emit("vector", lambda e, q=q, ec_=ec_: e.tensor_copy(
                EQF[:, 2 * q + 1:2 * q + 2], B_loc[:, ec_:ec_ + 1]))
        drain("vector")
        for q in range(NTI):
            S.emit("vector", lambda e, q=q: e.scalar_tensor_tensor(
                out=LL[:, q + 1:q + 2], in0=LL[:, q:q + 1], scalar=EQF[:, 2 * q:2 * q + 1],
                in1=EQF[:, 2 * q + 1:2 * q + 2], op0=OP.mult, op1=OP.add))
            S.emit("vector", lambda e, q=q: e.tensor_scalar(
                PP[:, q + 1:q + 2], PP[:, q:q + 1], EQF[:, 2 * q:2 * q + 1], None, OP.mult))
            drain("vector")
        for j in range(NCORE):
            S.emit("vector", lambda e, j=j: e.tensor_copy(PAY2[:, 2 * j:2 * j + 1],
                                                          PP[:, NTI:NTI + 1]))
            S.emit("vector", lambda e, j=j: e.tensor_copy(PAY2[:, 2 * j + 1:2 * j + 2],
                                                          LL[:, NTI:NTI + 1]))
        drain("vector")
        tk_p2 = inc("vector", lambda e: e.tensor_mul(PAY2b[:], PAY2[:], M16), "v")
        w("gpsimd", tk_p2)
        tk_cc2in = dma("gpsimd", cc2i.ap(), PAY2b[:], "w")
        w("gpsimd", tk_cc2in)
        tk_cc2 = inc("gpsimd", lambda e: e.collective_compute(
            "AllReduce", OP.add, replica_groups=[list(range(NCORE))],
            ins=[cc2i.ap().opt()], outs=[cc2o.ap().opt()]), "c")
        w("gpsimd", tk_cc2)
        tk_g2 = dma("gpsimd", G2[:], cc2o.ap(), "w")
        w("vector", tk_g2)
        S.emit("vector", lambda e: e.memset(SS[:, 0:1], 0.0))
        drain("vector")
        for j in range(NCORE - 1):
            S.emit("vector", lambda e, j=j: e.scalar_tensor_tensor(
                out=SS[:, j + 1:j + 2], in0=SS[:, j:j + 1], scalar=G2[:, 2 * j:2 * j + 1],
                in1=G2[:, 2 * j + 1:2 * j + 2], op0=OP.mult, op1=OP.add))
            drain("vector")
        S.emit("vector", lambda e: e.tensor_mul(SELT[:], SS[:, 0:NCORE], SEL))
        drain("vector")
        S.emit("vector", lambda e: e.tensor_add(SELT[:, 0:4], SELT[:, 0:4], SELT[:, 4:8]))
        drain("vector")
        S.emit("vector", lambda e: e.tensor_add(SELT[:, 0:2], SELT[:, 0:2], SELT[:, 2:4]))
        drain("vector")
        S.emit("vector", lambda e: e.tensor_add(SIN[:], SELT[:, 0:1], SELT[:, 1:2]))
        drain("vector")
        for q in range(NTI):
            S.emit("vector", lambda e, q=q: e.scalar_tensor_tensor(
                out=SST[:, q:q + 1], in0=PP[:, q:q + 1], scalar=SIN[:],
                in1=LL[:, q:q + 1], op0=OP.mult, op1=OP.add))
        drain("vector")
        tk_ye = None
        for q in range(NTI):
            sl0 = q * TT
            if tk_ye is not None:
                w("vector", tk_ye)
            S.emit("vector", lambda e, q=q, sl0=sl0: e.scalar_tensor_tensor(
                out=SCB[:], in0=A_loc[:, sl0:sl0 + TT], scalar=SST[:, q:q + 1],
                in1=B_loc[:, sl0:sl0 + TT], op0=OP.mult, op1=OP.add))
            drain("vector")
            tk_cs = inc("vector", lambda e, sl0=sl0: e.tensor_mul(
                SCA[:], SCB[:], C_loc[:, sl0:sl0 + TT]), "v")
            drain("vector")
            w("tensor", tk_cs, pfree[0])
            tk_ymm = inc("tensor", lambda e: e.matmul(P[0][0:1, 0:TT], onesbf[:], SCA[:],
                                                      start=True, stop=True), "t")
            w("scalar", tk_ymm)
            tk_ye = inc("scalar", lambda e, sl0=sl0: e.copy(out=C_loc[0:1, sl0:sl0 + TT],
                                                            in_=P[0][0:1, 0:TT]), "s")
            pfree[0] = tk_ye

        for g in range(16):
            w("sync", tk_store[g])

        block = ctx.enter_context(nc.Block())
        S.run(block)
    return nc


def kernel(**inputs):
    x = np.asarray(inputs["L0_attn_x"])[0]
    Win = np.asarray(inputs["L0_attn_Win"])
    conv_w = np.asarray(inputs["L0_attn_conv_w"])
    conv_b = np.asarray(inputs["L0_attn_conv_b"])
    Wx = np.asarray(inputs["L0_attn_Wx"])
    Wdt = np.asarray(inputs["L0_attn_Wdt"])
    bdt = np.asarray(inputs["L0_attn_bdt"])
    A_log = np.asarray(inputs["L0_attn_A_log"])
    D = np.asarray(inputs["L0_attn_D"])
    ln_g = np.asarray(inputs["L0_attn_ln_g"])
    ln_b = np.asarray(inputs["L0_attn_ln_b"])
    Wout = np.asarray(inputs["L0_attn_Wout"])
    res_scale = np.asarray(inputs["L0_attn_res_scale"])
    Wf = np.asarray(inputs["L0_ffn_weight"])
    ffb = np.asarray(inputs["L0_ffn_bias"])

    C = np.ascontiguousarray
    w_xp = C(Win[:, :E])
    w_z = C(Win[:, E:2 * E])
    w_b0 = C(Win[:, 2 * E].reshape(KD, 128).T)
    w_ft = C(Wf.T)
    wdt0 = C(Wdt[:, 0:1])
    alog0 = C(A_log[0].reshape(H, 1))
    cw = C(conv_w.reshape(EC, 128, 4).transpose(1, 0, 2).reshape(128, EC * 4))
    col = lambda v: C(v.reshape(EC, 128).T)
    ffbc = C(ffb.reshape(KD, 128).T)
    rs128 = C(np.broadcast_to(res_scale.reshape(1, 1), (128, 1)))
    bdt0 = C(bdt[0].reshape(1, 1))

    in_maps = []
    for i in range(NCORE):
        t0 = i * T
        xh = x[t0 - 3:t0] if i > 0 else np.zeros((3, DM), np.float32)
        sel = np.zeros((128, NCORE), np.float32)
        sel[:, i] = 1.0
        m8 = np.zeros((128, NCORE), np.float32)
        m8[:, i] = 1.0
        m16 = np.zeros((128, 2 * NCORE), np.float32)
        m16[:, 2 * i:2 * i + 2] = 1.0
        cst_all = np.concatenate([
            cw, col(conv_b), col(D), col(ln_g), col(ln_b), ffbc, rs128,
            sel, m8, m16, np.broadcast_to(bdt0, (128, 1)), alog0], axis=1)
        in_maps.append({
            "x": C(x[t0:t0 + T]), "xh": C(xh),
            "w_xp": w_xp, "w_z": w_z, "w_b0": w_b0, "w_x": Wx, "w_dt0": wdt0,
            "w_out": Wout, "w_ft": w_ft,
            "cst_all": C(cst_all.astype(np.float32)),
        })
    nc = build_nc()
    res = run_bass_kernel_spmd(nc, in_maps, core_ids=list(range(NCORE)))
    out = np.concatenate([r["y"] for r in res.results], axis=0)
    return out.reshape(1, NCORE * T, DM)


# version 24
# speedup vs baseline: 1.0231x; 1.0231x over previous
import numpy as np
import concourse.bass as bass
import concourse.mybir as mybir
from concourse.bass_utils import run_bass_kernel_spmd
from contextlib import ExitStack

F32 = mybir.dt.float32
BF16 = mybir.dt.bfloat16
F8 = mybir.dt.float8e4
AF = mybir.ActivationFunctionType
OP = mybir.AluOpType

NCORE = 8
T = 2048
DM = 1024
E = 2048
EC = 16
KD = 8
TT = 256
NTI = T // TT
H = 128
ENGS = ("sync", "gpsimd", "tensor", "vector", "scalar")


class Sched:
    def __init__(self):
        self.st = {e: [] for e in ENGS}
        self.cnt = {}
        self.lastw = {}

    def inc(self, eng, fn, sem, semname, n=1):
        v = self.cnt.get(semname, 0) + n
        self.cnt[semname] = v
        self.st[eng].append(lambda e, fn=fn, sem=sem, n=n: fn(e).then_inc(sem, n))
        return (semname, sem, v)

    def emit(self, eng, fn):
        self.st[eng].append(fn)

    def wait(self, eng, ticket):
        if ticket is None:
            return
        if isinstance(ticket, (tuple, list)) and ticket and isinstance(ticket[0], (tuple, list, type(None))):
            for t in ticket:
                self.wait(eng, t)
            return
        semname, sem, v = ticket
        key = (eng, semname)
        if self.lastw.get(key, -1) >= v:
            return
        self.lastw[key] = v
        self.st[eng].append(lambda e, sem=sem, v=v: e.wait_ge(sem, v))

    def run(self, block):
        for name in ENGS:
            def body(e, name=name):
                for fn in self.st[name]:
                    fn(e)
            getattr(block, name)(body)


def build_nc():
    nc = bass.Bass("TRN2", target_bir_lowering=False)
    dt = nc.dram_tensor
    x_in = dt("x", [T, DM], F32, kind="ExternalInput").ap()
    xh_in = dt("xh", [3, DM], F32, kind="ExternalInput").ap()
    w_xp = dt("w_xp", [DM, E], F32, kind="ExternalInput").ap()
    w_z = dt("w_z", [DM, E], F32, kind="ExternalInput").ap()
    w_b0 = dt("w_b0", [128, KD], F32, kind="ExternalInput").ap()
    w_x = dt("w_x", [E, 192], F32, kind="ExternalInput").ap()
    w_dt0 = dt("w_dt0", [64, 1], F32, kind="ExternalInput").ap()
    w_out = dt("w_out", [E, DM], F32, kind="ExternalInput").ap()
    w_ft = dt("w_ft", [DM, DM], F32, kind="ExternalInput").ap()
    cst_in = dt("cst_all", [128, 171], F32, kind="ExternalInput").ap()
    y_out = dt("y", [T, DM], F32, kind="ExternalOutput").ap()
    cc1i = dt("cc1i", [128, NCORE], F32)
    cc1o = dt("cc1o", [128, NCORE], F32)
    cc2i = dt("cc2i", [128, 2 * NCORE], F32)
    cc2o = dt("cc2o", [128, 2 * NCORE], F32)
    identf_d = nc.inline_tensor(np.eye(128, dtype=np.float32), "identf_dram")

    ctx = ExitStack()
    with ctx:
        sb = lambda n, s, d: ctx.enter_context(nc.sbuf_tensor(n, s, d))
        pst = lambda n: ctx.enter_context(nc.psum_tensor(n, [128, 512], F32))
        mksem = lambda n: ctx.enter_context(nc.semaphore(n))

        W1 = sb("W1", [128, KD * E], BF16)
        W2 = sb("W2", [128, KD * E], BF16)
        W3 = sb("W3", [128, EC * DM], BF16)
        Wq = sb("Wq", [128, KD * DM], F8)
        Wxs = sb("Wxs", [128, EC * 192], BF16)
        DG = sb("DG", [128, EC * 4 * 128], BF16)
        wb0 = sb("wb0", [128, KD], BF16)
        wdt0b = sb("wdt0b", [64, 1], BF16)
        identf = sb("identf", [128, 128], F32)
        identb = sb("identb", [128, 128], BF16)
        onesr = sb("onesr", [1, 128], F32)
        onesbf = sb("onesbf", [128, 1], BF16)
        onesrb = sb("onesrb", [1, 128], BF16)
        cst = sb("cst", [128, 4], F32)
        XS = sb("XS", [128, EC * TT], BF16)
        XSQ2 = sb("XSQ2", [128, 4 * TT], BF16)
        XT = [sb(f"XT{i}", [128, DM], F32) for i in range(2)]
        XN = [sb(f"XN{i}", [128, DM], F32) for i in range(2)]
        XNB = sb("XNB", [128, 2 * DM], BF16)
        XQ = sb("XQ", [128, 2 * DM], F8)
        QA = sb("QA", [128, DM], F32)
        QB = sb("QB", [128, DM], F32)
        XPB = sb("XPB", [128, 2 * 260], BF16)
        XPT = sb("XPT", [128, EC * 3], BF16)
        A_loc = sb("A_loc", [128, T], BF16)
        B_loc = sb("B_loc", [128, T], BF16)
        C_loc = sb("C_loc", [128, T], BF16)
        SCA = sb("SCA", [128, TT], BF16)
        SCB = sb("SCB", [128, TT], BF16)
        SCC = sb("SCC", [128, TT], BF16)
        DRB = sb("DRB", [64, TT], BF16)
        M1B = sb("M1B", [128, TT], F32)
        RSB = sb("RSB", [128, TT], F32)
        SZB = sb("SZB", [128, 2 * TT], BF16)
        TNH = sb("TNH", [128, TT], F32)
        YNF = [sb(f"YNF{i}", [128, TT], F32) for i in range(2)]
        YNBF = sb("YNBF", [128, 2 * TT], BF16)
        YGB = sb("YGB", [128, 4 * TT], BF16)
        GT = sb("GT", [128, 2 * TT], BF16)
        GF = sb("GF", [128, 2 * TT], F32)
        SGB = sb("SGB", [128, 2 * TT], F32)
        SG1 = sb("SG1", [128, 256], F32)
        HFM = sb("HFM", [128, KD * 3], BF16)
        MI = sb("MI", [128, 16], F32)
        MN = sb("MN", [128, 16], F32)
        MNI = sb("MNI", [128, 16], F32)
        A0c = sb("A0c", [128, 1], F32)
        Dm1 = sb("Dm1", [128, EC], BF16)
        Dm2 = sb("Dm2", [128, EC], BF16)
        CONSTS = sb("CONSTS", [128, 171], F32)
        CW = CONSTS.ap()[:, 0:64]
        CB = CONSTS.ap()[:, 64:80]
        DD = CONSTS.ap()[:, 80:96]
        LNG = CONSTS.ap()[:, 96:112]
        LNB = CONSTS.ap()[:, 112:128]
        FFB = CONSTS.ap()[:, 128:136]
        RS1 = CONSTS.ap()[:, 136:137]
        SEL = CONSTS.ap()[:, 137:145]
        M8 = CONSTS.ap()[:, 145:153]
        M16 = CONSTS.ap()[:, 153:169]
        BDT = CONSTS.ap()[0:1, 169:170]
        ALG = CONSTS.ap()[:, 170:171]
        SC = sb("SC", [128, 16], F32)
        LL = sb("LL", [128, 9], F32)
        PP = sb("PP", [128, 9], F32)
        SS = sb("SS", [128, 9], F32)
        SST = sb("SST", [128, NTI], F32)
        SIN = sb("SIN", [128, 1], F32)
        EQF = sb("EQF", [128, 2 * NTI], F32)
        SELT = sb("SELT", [128, NCORE], F32)
        PAY1 = sb("PAY1", [128, NCORE], F32)
        PAY2 = sb("PAY2", [128, 2 * NCORE], F32)
        PAY2b = sb("PAY2b", [128, 2 * NCORE], F32)
        G1 = sb("G1", [128, NCORE], F32)
        G2 = sb("G2", [128, 2 * NCORE], F32)
        HCOL = sb("HCOL", [3, 4], F32)
        B0R = sb("B0R", [1, TT], BF16)
        XS0R = sb("XS0R", [1, TT], BF16)
        T0R = sb("T0R", [1, TT], BF16)
        T1R = sb("T1R", [1, TT], BF16)
        DLTR = sb("DLTR", [1, TT], BF16)
        BM0R = sb("BM0R", [1, TT], BF16)
        BDR = sb("BDR", [1, TT], BF16)
        M1R = sb("M1R", [1, TT], F32)
        M2R = sb("M2R", [1, TT], F32)
        VARR = sb("VARR", [1, TT], F32)
        RSTR = sb("RSTR", [1, TT], F32)

        P = [pst(f"p{i}") for i in range(8)]

        semx = mksem("semx")
        semw = mksem("semw")
        semt = mksem("semt")
        semv = mksem("semv")
        sems = mksem("sems")
        semc = mksem("semc")
        semo = mksem("semo")
        semx2 = mksem("semx2")
        semw2 = mksem("semw2")
        semo2 = mksem("semo2")
        semg = mksem("semg")
        sema = mksem("sema")
        semb = mksem("semb")
        semh = mksem("semh")
        semq = mksem("semq")
        semsx = mksem("semsx")
        S = Sched()
        SEM = {"x": semx, "w": semw, "t": semt, "v": semv, "s": sems,
               "c": semc, "o": semo, "x2": semx2, "w2": semw2, "o2": semo2,
               "g": semg, "a": sema, "b": semb, "h": semh, "q": semq,
               "sx": semsx}
        _rot = {"x": ["x", "x2"], "w": ["w", "w2"], "o": ["o", "o2"]}
        _ri = {"x": 0, "w": 0, "o": 0}

        def inc(eng, fn, s, n=1):
            return S.inc(eng, fn, SEM[s], s, n)

        def w(eng, *tickets):
            for tk in tickets:
                S.wait(eng, tk)

        def dma(eng, out, in_, s):
            name = _rot[s][_ri[s]]
            _ri[s] ^= 1
            prior = S.cnt.get(name, 0)
            if prior:
                w(eng, (name, SEM[name], prior))
            return inc(eng, lambda e, o=out, i=in_: e.dma_start(out=o, in_=i), name, 16)

        def drain(eng):
            S.emit(eng, lambda e: e.drain())

        xnb3 = XNB.ap().rearrange("p (s c) -> p s c", c=DM)
        xq3 = XQ.ap().rearrange("p (s c) -> p s c", c=DM)

        tk_idf = dma("sync", identf[:], identf_d.ap(), "x")
        tk_idb = dma("gpsimd", identb[:], identf_d.ap(), "w")
        tk_hx = dma("gpsimd", XS[0:3, 8 * TT:8 * TT + DM], xh_in, "w")
        tk_small = dma("sync", CONSTS[:], cst_in, "x")
        tk_w1 = dma("gpsimd", W1.ap().rearrange("p (kc c) -> p kc c", c=E),
                    w_xp.rearrange("(kc p) c -> p kc c", p=128), "w")
        tk_wfl = [None] * KD
        for g_ in range(KD):
            if g_ == 1:
                w("gpsimd", ("h", semh, KD))
            if g_ >= 2:
                w("gpsimd", ("a", sema, g_ - 1))
            tk_wfl[g_] = dma("gpsimd", XN[g_ % 2][:],
                             w_ft[g_ * 128:(g_ + 1) * 128, :], "w")
        tk_wx = dma("gpsimd", Wxs.ap().rearrange("p (ec c) -> p ec c", c=192),
                    w_x.rearrange("(ec p) c -> p ec c", p=128), "w")
        tk_wb0 = dma("gpsimd", wb0[:], w_b0, "w")
        tk_wdt = dma("gpsimd", wdt0b[:], w_dt0, "w")
        qslot = [GF.ap()[:, 0:256], GF.ap()[:, 256:512],
                 SGB.ap()[:, 0:256], SGB.ap()[:, 256:512],
                 M1B.ap(), RSB.ap(), TNH.ap(), YNF[0].ap()]
        tk_wql = [None] * 32
        for qt in range(32):
            if qt >= 8:
                w("gpsimd", ("q", semq, qt - 7))
            kc, q4 = qt // 4, qt % 4
            tk_wql[qt] = dma("gpsimd", qslot[qt % 8],
                             w_ft[kc * 128:(kc + 1) * 128,
                                  q4 * 256:(q4 + 1) * 256], "w")
        tk_w2 = dma("gpsimd", W2.ap().rearrange("p (kc c) -> p kc c", c=E),
                    w_z.rearrange("(kc p) c -> p kc c", p=128), "w")
        tk_w3 = dma("gpsimd", W3.ap().rearrange("p (ec c) -> p ec c", c=DM),
                    w_out.rearrange("(ec p) c -> p ec c", p=128), "w")

        S.emit("vector", lambda e: e.memset(cst[:, 0:1], 1e-5))
        S.emit("vector", lambda e: e.memset(cst[:, 1:2], -0.5))
        S.emit("vector", lambda e: e.memset(cst[:, 2:3], 0.5))
        S.emit("vector", lambda e: e.memset(onesbf[:], 1.0))
        S.emit("vector", lambda e: e.memset(onesr[:], 1.0))
        S.emit("vector", lambda e: e.memset(onesrb[:], 1.0))
        S.emit("vector", lambda e: e.memset(XPT[:], 0.0))
        S.emit("vector", lambda e: e.memset(SC[:, 0:1], 0.0))
        S.emit("vector", lambda e: e.memset(SC[:, 6:7], 0.0))
        tk_cst = inc("vector", lambda e: e.memset(XPB[:], 0.0), "v")
        w("scalar", tk_small)
        S.emit("scalar", lambda e: e.activation(out=SC[:, 11:12], in_=ALG, func=AF.Exp))
        drain("scalar")
        tk_a0 = inc("scalar", lambda e: e.mul(A0c[:], SC[:, 11:12], -1.0), "s")
        w("vector", tk_small)
        S.emit("vector", lambda e: e.tensor_scalar_mul(Dm1[:], DD, 1.0 / 2048.0))
        S.emit("vector", lambda e: e.tensor_mul(QA[:, 0:EC], DD, DD))
        drain("vector")
        S.emit("vector", lambda e: e.tensor_scalar_mul(Dm2[:], QA[:, 0:EC], 1.0 / 2048.0))
        drain("vector")
        S.emit("vector", lambda e: e.tensor_scalar(QA[:, 0:EC], LNG, RS1, None, OP.mult))
        S.emit("vector", lambda e: e.tensor_scalar(QA[:, EC:2 * EC], LNB, RS1, None, OP.mult))
        drain("vector")
        S.emit("vector", lambda e: e.tensor_copy(LNG, QA[:, 0:EC]))
        tk_ln = inc("vector", lambda e: e.tensor_copy(LNB, QA[:, EC:2 * EC]), "v")
        drain("vector")
        w("scalar", tk_idb)
        dgj = [0]

        def emit_dg(n):
            tk = None
            for _ in range(n):
                j = dgj[0]
                if j >= EC * 4:
                    return None
                dgj[0] += 1
                tk = inc("scalar", lambda e, j=j: e.activation(
                    out=DG[:, j * 128:(j + 1) * 128], in_=identb[:], func=AF.Copy,
                    scale=CW[:, j:j + 1]), "s")
            return tk

        hx0, hs0 = 8 * TT, 8 * TT + DM
        w("scalar", tk_hx)
        tk_hsq = inc("scalar", lambda e: e.square(XS[0:3, hs0:hs0 + DM],
                                                  XS[0:3, hx0:hx0 + DM]), "s")

        xt_free = [None, None]
        w("vector", tk_a0)
        tk_ax = tk_sw = tk_q = tk_aw = tk_aw128 = None
        tk_sum_prev = None
        xnm_free_w = [None, None]
        for g in range(16):
            sl = g % 2
            w("sync", xt_free[sl])
            tk_xt = dma("sync", XT[sl][:], x_in[g * 128:(g + 1) * 128, :], "x")
            w("scalar", tk_xt, tk_sum_prev, tk_ln)
            tk_sq = inc("scalar", lambda e, sl=sl: e.square(QA[:], XT[sl][:]), "s")
            w("vector", tk_xt, tk_cst)
            tk_vs = inc("vector", lambda e, sl=sl: e.reduce_sum(
                SC[:, 1:2], XT[sl][:], axis=mybir.AxisListType.X), "v")
            w("vector", tk_sq)
            tk_sum_prev = inc("vector", lambda e: e.reduce_sum(
                SC[:, 11:12], QA[:], axis=mybir.AxisListType.X), "v")
            drain("vector")
            S.emit("vector", lambda e: e.tensor_scalar_mul(SC[:, 12:13], SC[:, 1:2], 1.0 / DM))
            drain("vector")
            S.emit("vector", lambda e, g=g: e.tensor_scalar_mul(MN[:, g:g + 1],
                                                                SC[:, 12:13], -1.0))
            S.emit("vector", lambda e: e.tensor_mul(SC[:, 13:14], SC[:, 12:13], SC[:, 12:13]))
            S.emit("vector", lambda e: e.tensor_scalar_mul(SC[:, 14:15], SC[:, 11:12], 1.0 / DM))
            drain("vector")
            tk_var = inc("vector", lambda e: e.tensor_sub(SC[:, 15:16], SC[:, 14:15],
                                                          SC[:, 13:14]), "v")
            if 4 <= g < 12:
                j = g - 4
                w("vector", tk_wfl[j])
                S.emit("vector", lambda e, j=j: e.tensor_reduce(
                    SC[:, 9:10], XN[j % 2][:], axis=mybir.AxisListType.X, op=OP.max,
                    apply_absolute_value=True))
                drain("vector")
                wco, wci = (3, 6) if j % 2 == 0 else (6, 3)
                tk_wm = inc("vector", lambda e, wco=wco, wci=wci: e.tensor_max(
                    SC[:, wco:wco + 1], SC[:, wci:wci + 1], SC[:, 9:10]), "a")
                if j >= 6:
                    xnm_free_w[j % 2] = tk_wm
                drain("vector")
            w("scalar", tk_var)
            tk_sd = inc("scalar", lambda e: e.activation(out=SC[:, 11:12], in_=SC[:, 15:16],
                                                         func=AF.Sqrt, bias=cst[:, 0:1]), "s")
            w("vector", tk_sd)
            S.emit("vector", lambda e, g=g: e.reciprocal(MI[:, g:g + 1], SC[:, 11:12]))
            drain("vector")
            tk_xn_v = inc("vector", lambda e, g=g, sl=sl: e.tensor_scalar(
                QB[:], XT[sl][:], MN[:, g:g + 1], MI[:, g:g + 1],
                OP.add, OP.mult), "v")
            drain("vector")
            S.emit("vector", lambda e: e.tensor_reduce(
                SC[:, 1:2], QB[:], axis=mybir.AxisListType.X, op=OP.max,
                apply_absolute_value=True))
            drain("vector")
            xco, xci = (2, 0) if g % 2 == 0 else (0, 2)
            tk_ax = inc("vector", lambda e, xco=xco, xci=xci: e.tensor_max(
                SC[:, xco:xco + 1], SC[:, xci:xci + 1], SC[:, 1:2]), "b")
            drain("vector")
            if g == 5:
                w("scalar", tk_hm)
                tk_he = inc("scalar", lambda e: e.copy(out=XPT[:],
                                                       in_=P[3][:, 0:EC * 3]), "s")
            if g >= 12:
                for qq in range(8):
                    qt = (g - 12) * 8 + qq
                    kc, q4 = qt // 4, qt % 4
                    w("scalar", tk_wql[qt], tk_sw)
                    tk_q = inc("scalar", lambda e, kc=kc, q4=q4, qt=qt: e.activation(
                        out=Wq[:, kc * DM + q4 * 256: kc * DM + (q4 + 1) * 256],
                        in_=qslot[qt % 8], func=AF.Identity,
                        scale=SC[:, 8:9]), "q")
            emit_dg(6 if g < 12 else 0)
            xt_free[sl] = (tk_xn_v, tk_sq)
            if g == 11:
                w("tensor", tk_wm, tk_idf)
                tk_tw = inc("tensor", lambda e: e.transpose(
                    P[0][0:1, 128:256], SC[:, 6:7], identf[:]), "t")
                w("vector", tk_tw)
                S.emit("vector", lambda e: e.tensor_reduce(
                    SC[0:1, 1:2], P[0][0:1, 128:256], axis=mybir.AxisListType.X, op=OP.max))
                drain("vector")
                tk_aw = inc("vector", lambda e: e.tensor_copy(VARR[0:1, 0:1], SC[0:1, 1:2]), "v")
                w("tensor", tk_aw)
                tk_bc2 = inc("tensor", lambda e: e.matmul(P[1][:, 1:2], onesr[:],
                                                          VARR[0:1, 0:1],
                                                          start=True, stop=True), "t")
                w("vector", tk_bc2)
                tk_aw128 = inc("vector", lambda e: e.tensor_copy(SC[:, 7:8], P[1][:, 1:2]), "v")
                drain("vector")
                S.emit("vector", lambda e: e.tensor_scalar_max(SC[:, 1:2], SC[:, 7:8], 1e-12))
                drain("vector")
                S.emit("vector", lambda e: e.reciprocal(SC[:, 11:12], SC[:, 1:2]))
                drain("vector")
                S.emit("vector", lambda e: e.tensor_scalar(SC[:, 12:13], SC[:, 11:12],
                                                           448.0, 448.0, OP.mult, OP.min))
                drain("vector")
                S.emit("vector", lambda e: e.tensor_scalar_mul(SC[:, 8:9], SC[:, 12:13], 0.5))
                tk_sw = inc("vector", lambda e: e.reciprocal(SC[:, 9:10], SC[:, 12:13]), "v")
                drain("vector")
            if g == 0:
                w("vector", tk_hx, tk_hsq)
                S.emit("vector", lambda e: e.reduce_sum(HCOL[0:3, 0:1], XS[0:3, hx0:hx0 + DM],
                                                        axis=mybir.AxisListType.X))
                S.emit("vector", lambda e: e.reduce_sum(HCOL[0:3, 1:2], XS[0:3, hs0:hs0 + DM],
                                                        axis=mybir.AxisListType.X))
                drain("vector")
                S.emit("vector", lambda e: e.tensor_scalar_mul(HCOL[0:3, 2:3], HCOL[0:3, 0:1],
                                                               1.0 / DM))
                drain("vector")
                S.emit("vector", lambda e: e.tensor_mul(HCOL[0:3, 0:1], HCOL[0:3, 2:3],
                                                        HCOL[0:3, 2:3]))
                S.emit("vector", lambda e: e.tensor_scalar_mul(HCOL[0:3, 3:4], HCOL[0:3, 1:2],
                                                               1.0 / DM))
                drain("vector")
                tk_hv = inc("vector", lambda e: e.tensor_sub(HCOL[0:3, 1:2], HCOL[0:3, 3:4],
                                                             HCOL[0:3, 0:1]), "v")
                w("scalar", tk_hv)
                tk_hsd = inc("scalar", lambda e: e.activation(
                    out=HCOL[0:3, 3:4], in_=HCOL[0:3, 1:2],
                    func=AF.Sqrt, bias=cst[0:3, 0:1]), "s")
                w("vector", tk_hsd)
                S.emit("vector", lambda e: e.reciprocal(HCOL[0:3, 0:1], HCOL[0:3, 3:4]))
                drain("vector")
                S.emit("vector", lambda e: e.tensor_scalar_mul(HCOL[0:3, 1:2],
                                                               HCOL[0:3, 2:3], -1.0))
                drain("vector")
                tk_hap = inc("vector", lambda e: e.tensor_scalar(
                    XN[1][0:3, :], XS[0:3, hx0:hx0 + DM],
                    HCOL[0:3, 1:2], HCOL[0:3, 0:1], OP.add, OP.mult), "v")
                drain("vector")
                w("tensor", tk_hap, tk_idf)
                for f in range(KD):
                    tk_ht = inc("tensor", lambda e, f=f: e.transpose(
                        P[2][:, f * 3:(f + 1) * 3], XN[1][0:3, f * 128:(f + 1) * 128],
                        identf[0:3, 0:3]), "h")
                w("scalar", tk_ht)
                tk_hf = inc("scalar", lambda e: e.copy(out=HFM[:], in_=P[2][:, 0:24]), "s")
                w("tensor", tk_hf, tk_w1)
                for ec in range(EC):
                    for kc in range(KD):
                        tk_hm = inc("tensor", lambda e, ec=ec, kc=kc: e.matmul(
                            P[3][:, ec * 3:(ec + 1) * 3],
                            W1[:, kc * E + ec * 128: kc * E + ec * 128 + 128],
                            HFM[:, kc * 3:(kc + 1) * 3],
                            start=(ec == 0 and kc == 0), stop=(kc == 7),
                            skip_group_check=True), "t")
        tk_x_pre = [None, None]
        for s_ in range(2):
            w("sync", xt_free[s_])
            tk_x_pre[s_] = dma("sync", XT[s_][:], x_in[s_ * 128:(s_ + 1) * 128, :], "x")
            xt_free[s_] = None
        w("tensor", tk_ax, tk_idf)
        tk_tr = inc("tensor", lambda e: e.transpose(P[0][0:1, 0:128], SC[:, 0:1], identf[:]), "t")
        w("vector", tk_tr)
        S.emit("vector", lambda e: e.tensor_reduce(SC[0:1, 1:2], P[0][0:1, 0:128],
                                                   axis=mybir.AxisListType.X, op=OP.max))
        drain("vector")
        tk_axs = inc("vector", lambda e: e.tensor_copy(VARR[0:1, 0:1], SC[0:1, 1:2]), "v")
        w("tensor", tk_axs)
        tk_bc = inc("tensor", lambda e: e.matmul(P[1][:, 0:1], onesr[:], VARR[0:1, 0:1],
                                                 start=True, stop=True), "t")
        w("vector", tk_bc, tk_small)
        tk_ax128 = inc("vector", lambda e: e.tensor_copy(SC[:, 2:3], P[1][:, 0:1]), "v")
        drain("vector")
        tk_p1 = inc("vector", lambda e: e.tensor_scalar(PAY1[:], M8, SC[:, 2:3],
                                                        None, OP.mult), "v")
        w("sync", tk_p1)
        tk_ccin = dma("sync", cc1i.ap(), PAY1[:], "x")
        w("gpsimd", tk_ccin)
        tk_cc1 = inc("gpsimd", lambda e: e.collective_compute(
            "AllReduce", OP.add, replica_groups=[list(range(NCORE))],
            ins=[cc1i.ap().opt()], outs=[cc1o.ap().opt()]), "c")
        w("gpsimd", tk_cc1)
        w("sync", tk_cc1)
        tk_g1 = dma("sync", G1[:], cc1o.ap(), "x")
        xnm_free = [xnm_free_w[0], (xnm_free_w[1], tk_ht)]

        tl = {"g2": None, "cc2": None, "cs": [None] * NTI, "p2": None}

        def emit_tail_prefix():
            w("vector", tk_scan_done[NTI - 2])
            S.emit("vector", lambda e: e.memset(LL[:, 0:1], 0.0))
            S.emit("vector", lambda e: e.memset(PP[:, 0:1], 1.0))
            for q in range(NTI - 1):
                ec_ = (q + 1) * TT - 1
                S.emit("vector", lambda e, q=q, ec_=ec_: e.tensor_copy(
                    EQF[:, 2 * q:2 * q + 1], A_loc[:, ec_:ec_ + 1]))
                S.emit("vector", lambda e, q=q, ec_=ec_: e.tensor_copy(
                    EQF[:, 2 * q + 1:2 * q + 2], B_loc[:, ec_:ec_ + 1]))
            drain("vector")
            for q in range(NTI - 1):
                S.emit("vector", lambda e, q=q: e.scalar_tensor_tensor(
                    out=LL[:, q + 1:q + 2], in0=LL[:, q:q + 1],
                    scalar=EQF[:, 2 * q:2 * q + 1],
                    in1=EQF[:, 2 * q + 1:2 * q + 2], op0=OP.mult, op1=OP.add))
                S.emit("vector", lambda e, q=q: e.tensor_scalar(
                    PP[:, q + 1:q + 2], PP[:, q:q + 1], EQF[:, 2 * q:2 * q + 1],
                    None, OP.mult))
                drain("vector")

        def emit_tail_cc():
            q = NTI - 1
            ec_ = (q + 1) * TT - 1
            w("vector", tk_scan_done[NTI - 1])
            S.emit("vector", lambda e, q=q, ec_=ec_: e.tensor_copy(
                EQF[:, 2 * q:2 * q + 1], A_loc[:, ec_:ec_ + 1]))
            S.emit("vector", lambda e, q=q, ec_=ec_: e.tensor_copy(
                EQF[:, 2 * q + 1:2 * q + 2], B_loc[:, ec_:ec_ + 1]))
            drain("vector")
            S.emit("vector", lambda e, q=q: e.scalar_tensor_tensor(
                out=LL[:, q + 1:q + 2], in0=LL[:, q:q + 1],
                scalar=EQF[:, 2 * q:2 * q + 1],
                in1=EQF[:, 2 * q + 1:2 * q + 2], op0=OP.mult, op1=OP.add))
            S.emit("vector", lambda e, q=q: e.tensor_scalar(
                PP[:, q + 1:q + 2], PP[:, q:q + 1], EQF[:, 2 * q:2 * q + 1],
                None, OP.mult))
            drain("vector")
            for j in range(NCORE):
                S.emit("vector", lambda e, j=j: e.tensor_copy(
                    PAY2[:, 2 * j:2 * j + 1], PP[:, NTI:NTI + 1]))
                S.emit("vector", lambda e, j=j: e.tensor_copy(
                    PAY2[:, 2 * j + 1:2 * j + 2], LL[:, NTI:NTI + 1]))
            drain("vector")
            tl["p2"] = inc("vector", lambda e: e.tensor_mul(PAY2b[:], PAY2[:], M16), "v")
            w("sync", tl["p2"])
            tk_cc2in = dma("sync", cc2i.ap(), PAY2b[:], "x")
            w("gpsimd", tk_cc2in)
            tl["cc2"] = inc("gpsimd", lambda e: e.collective_compute(
                "AllReduce", OP.add, replica_groups=[list(range(NCORE))],
                ins=[cc2i.ap().opt()], outs=[cc2o.ap().opt()]), "c")
            w("gpsimd", tl["cc2"])
            w("sync", tl["cc2"])
            tl["g2"] = dma("sync", G2[:], cc2o.ap(), "x")

        def emit_tail_schain():
            w("vector", tl["g2"])
            S.emit("vector", lambda e: e.memset(SS[:, 0:1], 0.0))
            drain("vector")
            for j in range(NCORE - 1):
                S.emit("vector", lambda e, j=j: e.scalar_tensor_tensor(
                    out=SS[:, j + 1:j + 2], in0=SS[:, j:j + 1],
                    scalar=G2[:, 2 * j:2 * j + 1],
                    in1=G2[:, 2 * j + 1:2 * j + 2], op0=OP.mult, op1=OP.add))
                drain("vector")
            S.emit("vector", lambda e: e.tensor_mul(SELT[:], SS[:, 0:NCORE], SEL))
            drain("vector")
            S.emit("vector", lambda e: e.tensor_add(SELT[:, 0:4], SELT[:, 0:4], SELT[:, 4:8]))
            drain("vector")
            S.emit("vector", lambda e: e.tensor_add(SELT[:, 0:2], SELT[:, 0:2], SELT[:, 2:4]))
            drain("vector")
            S.emit("vector", lambda e: e.tensor_add(SIN[:], SELT[:, 0:1], SELT[:, 1:2]))
            drain("vector")
            for q in range(NTI):
                S.emit("vector", lambda e, q=q: e.scalar_tensor_tensor(
                    out=SST[:, q:q + 1], in0=PP[:, q:q + 1], scalar=SIN[:],
                    in1=LL[:, q:q + 1], op0=OP.mult, op1=OP.add))
            drain("vector")

        def emit_tail_q(q):
            sl0 = q * TT
            S.emit("vector", lambda e, q=q, sl0=sl0: e.scalar_tensor_tensor(
                out=SCB[:], in0=A_loc[:, sl0:sl0 + TT], scalar=SST[:, q:q + 1],
                in1=B_loc[:, sl0:sl0 + TT], op0=OP.mult, op1=OP.add))
            drain("vector")
            tl["cs"][q] = inc("vector", lambda e, sl0=sl0: e.tensor_mul(
                A_loc[:, sl0:sl0 + TT], SCB[:], C_loc[:, sl0:sl0 + TT]), "v")
            drain("vector")

        def emit_tail_ymm(tk_pf):
            regions = [(0, 0), (0, TT), (1, 0), (1, TT)]
            tk_ymm = [None] * NTI
            tk_ye = [None, None, None, None]
            for rnd in range(2):
                w("tensor", tk_pf)
                if rnd == 1:
                    w("tensor", tk_ye[0], tk_ye[1], tk_ye[2], tk_ye[3])
                for i4 in range(4):
                    q = rnd * 4 + i4
                    b, c0 = regions[i4]
                    w("tensor", tl["cs"][q])
                    tk_ymm[q] = inc("tensor", lambda e, q=q, b=b, c0=c0: e.matmul(
                        P[b][0:1, c0:c0 + TT], onesbf[:], A_loc[:, q * TT:(q + 1) * TT],
                        start=(i4 % 2 == 0), stop=True, skip_group_check=True), "t")
                for i4 in range(4):
                    q = rnd * 4 + i4
                    b, c0 = regions[i4]
                    w("scalar", tk_ymm[q])
                    tk_ye[i4] = inc("scalar", lambda e, q=q, b=b, c0=c0: e.copy(
                        out=C_loc[0:1, q * TT:(q + 1) * TT],
                        in_=P[b][0:1, c0:c0 + TT]), "s")
        pfree = {0: tk_axs, 1: tk_ax128, 2: tk_hf, 3: tk_he,
                 4: None, 5: None, 6: None, 7: None}
        tk_store = [None] * 16
        tk_xs_free = None
        tk_xnb_free = None
        tk_xq_free = None
        tk_drb_free = None
        tk_szb_free = [None, None]
        tk_ynbf_free = [None, None]
        tk_ygb_free = [None, None, None, None]
        tk_xsq_free = [None, None]
        tk_gt_free = [None, None]
        tk_gf_free = [None, None]
        tk_sgb_free = [None, None]
        tk_sca_free = None
        tk_qab_prev = None
        tk_scan_done = [None] * NTI

        for ti in range(NTI):
            g0 = ti * 2
            if ti == 0:
                tk_x = tk_x_pre
            else:
                tk_x = [None, None]
                for s_ in range(2):
                    g = g0 + s_
                    sl = g % 2
                    w("sync", xt_free[sl])
                    tk_x[s_] = dma("sync", XT[sl][:], x_in[g * 128:(g + 1) * 128, :], "x")
            tk_xn = [None, None]
            for s_ in range(2):
                g = g0 + s_
                sl = g % 2
                w("vector", tk_x[s_], xnm_free[s_])
                tk_xn[s_] = inc("vector", lambda e, g=g, sl=sl, s_=s_: e.tensor_scalar(
                    XN[s_][:], XT[sl][:], MN[:, g:g + 1], MI[:, g:g + 1],
                    OP.add, OP.mult), "v")
            tk_tp = [None] * 4
            for s_ in range(2):
                w("tensor", tk_xn[s_])
                for half in range(2):
                    bank = 4 + s_ * 2 + half
                    w("tensor", pfree[bank])
                    for f4 in range(4):
                        f = half * 4 + f4
                        tk_tp[bank - 4] = inc("tensor", lambda e, s_=s_, f=f, bank=bank, f4=f4:
                                              e.transpose(P[bank][:, f4 * 128:(f4 + 1) * 128],
                                                          XN[s_][:, f * 128:(f + 1) * 128],
                                                          identf[:]), "t")
            def emit_sx_chain():
                w("vector", tk_g1)
                S.emit("vector", lambda e: e.tensor_reduce(
                    SC[:, 3:4], G1[:], axis=mybir.AxisListType.X, op=OP.max))
                drain("vector")
                S.emit("vector", lambda e: e.tensor_scalar_max(SC[:, 1:2], SC[:, 3:4], 1e-12))
                drain("vector")
                S.emit("vector", lambda e: e.reciprocal(SC[:, 11:12], SC[:, 1:2]))
                drain("vector")
                S.emit("vector", lambda e: e.tensor_scalar(SC[:, 12:13], SC[:, 11:12],
                                                           448.0, 448.0, OP.mult, OP.min))
                drain("vector")
                S.emit("vector", lambda e: e.tensor_scalar_mul(SC[:, 4:5], SC[:, 12:13], 0.5))
                tk = inc("vector", lambda e: e.reciprocal(SC[:, 5:6], SC[:, 12:13]), "sx")
                drain("vector")
                S.emit("vector", lambda e: e.tensor_scalar_mul(SC[:, 11:12], SC[:, 5:6], 4.0))
                drain("vector")
                tk4 = inc("vector", lambda e: e.tensor_mul(SC[:, 10:11], SC[:, 11:12],
                                                           SC[:, 9:10]), "v")
                drain("vector")
                return tk, tk4
            tk_xnb = [None, None]
            tk_xq = [None, None]
            for s_ in range(2):
                w("scalar", tk_tp[s_ * 2], tk_tp[s_ * 2 + 1], tk_xnb_free)
                for half in range(2):
                    bank = 4 + s_ * 2 + half
                    tk_xnb[s_] = inc("scalar", lambda e, s_=s_, half=half, bank=bank: e.copy(
                        out=XNB[:, s_ * DM + half * 512: s_ * DM + half * 512 + 512],
                        in_=P[bank][:]), "s")
            def emit_xq():
                for s_ in range(2):
                    w("scalar", ("sx", semsx, 1), tk_xq_free)
                    for half in range(2):
                        bank = 4 + s_ * 2 + half
                        tk_xq[s_] = inc("scalar", lambda e, s_=s_, half=half, bank=bank:
                                        e.activation(
                            out=XQ[:, s_ * DM + half * 512: s_ * DM + half * 512 + 512],
                            in_=P[bank][:], func=AF.Identity, scale=SC[:, 4:5]), "s")
                return (tk_xq[0], tk_xq[1])
            if ti > 0:
                pf47 = emit_xq()

            w("tensor", tk_xnb[0], tk_xnb[1], pfree[2], tk_wb0)
            for kc in range(KD):
                tk_b0 = inc("tensor", lambda e, kc=kc: e.matmul(
                    P[2][0:1, 0:TT], wb0[:, kc:kc + 1], xnb3[:, :, kc * 128:(kc + 1) * 128],
                    start=(kc == 0), stop=(kc == 7)), "t")
            w("scalar", tk_b0)
            tk_b0r = inc("scalar", lambda e: e.copy(out=B0R[:], in_=P[2][0:1, 0:TT]), "s")
            pfree[2] = tk_b0r

            tk_proj = [None] * EC
            tk_xpev = [None] * EC
            tk_silu = [None] * EC
            tk_xsq = [None] * EC
            tk_convd = [None] * EC
            tk_cdr = [None] * EC

            def emit_proj(e_, ti=ti):
                bank = e_ % 2
                w("tensor", pfree[bank])
                if e_ >= 2:
                    w("tensor", tk_xpev[e_ - 2])
                for kc in range(KD):
                    tk = inc("tensor", lambda e, kc=kc, e_=e_, bank=bank: e.matmul(
                        P[bank][0:128, 0:TT],
                        W1[:, kc * E + e_ * 128: kc * E + e_ * 128 + 128],
                        xnb3[:, :, kc * 128:(kc + 1) * 128],
                        start=(kc == 0), stop=(kc == 7)), "t")
                tk_proj[e_] = tk

            def emit_xpev(e_):
                hb = e_ % 2
                base = hb * 260
                w("vector", tk_proj[e_])
                if e_ >= 2:
                    w("vector", tk_convd[e_ - 2])
                if ti == 0:
                    w("vector", tk_he)
                S.emit("vector", lambda e, e_=e_, base=base: e.tensor_copy(
                    out=XPB[:, base + 1:base + 4], in_=XPT[:, e_ * 3:(e_ + 1) * 3]))
                S.emit("vector", lambda e, e_=e_, base=base, hb=hb: e.tensor_copy(
                    out=XPB[:, base + 4:base + 260], in_=P[hb][0:128, 0:TT]))
                drain("vector")
                tk_xpev[e_] = inc("vector", lambda e, e_=e_, base=base: e.tensor_copy(
                    out=XPT[:, e_ * 3:(e_ + 1) * 3], in_=XPB[:, base + 257:base + 260]), "v")
                drain("vector")

            def emit_conv(e_):
                hb = e_ % 2
                base = hb * 260
                bank = 2 + e_ % 2
                w("tensor", tk_xpev[e_], pfree[bank])
                for k in range(4):
                    j = e_ * 4 + k
                    tk = inc("tensor", lambda e, j=j, k=k, base=base, bank=bank: e.matmul(
                        P[bank][0:128, 0:TT], DG[:, j * 128:(j + 1) * 128],
                        XPB[:, base + 1 + k: base + 1 + k + TT],
                        start=(k == 0), stop=(k == 3)), "t")
                tk_convd[e_] = tk

            def emit_silu(e_, ti=ti):
                bank = 2 + e_ % 2
                w("scalar", tk_convd[e_], tk_xs_free)
                if ti == 0 and e_ == 8:
                    w("scalar", tk_hap)
                tk = inc("scalar", lambda e, e_=e_, bank=bank: e.activation(
                    out=XS[:, e_ * TT:(e_ + 1) * TT], in_=P[bank][0:128, 0:TT],
                    func=AF.Silu, bias=CB[:, e_:e_ + 1]), "s")
                if e_ == 0:
                    tk = inc("scalar", lambda e, bank=bank: e.activation(
                        out=XS0R[:], in_=P[bank][0:1, 0:TT],
                        func=AF.Silu, bias=CB[0:1, 0:1]), "s")
                tk_silu[e_] = tk
                pfree[bank] = tk

            def emit_xsq(e_):
                w("vector", tk_silu[e_])
                if e_ >= 4:
                    w("vector", tk_stats_q[(e_ - 4) // 4])
                elif ti > 0:
                    w("vector", tk_xsq_free[0])
                tk_xsq[e_] = inc("vector", lambda e, e_=e_: e.tensor_mul(
                    XSQ2[:, (e_ % 4) * TT:(e_ % 4 + 1) * TT], XS[:, e_ * TT:(e_ + 1) * TT],
                    XS[:, e_ * TT:(e_ + 1) * TT]), "v")

            def emit_stats_half(r0, r1):
                w("tensor", tk_silu[r1 - 1], tk_xsq[r1 - 1], tk_wx)
                if r0 == 0:
                    w("tensor", pf47[0], pf47[1])
                tk = None
                for e_ in range(r0, r1):
                    inc("tensor", lambda e, e_=e_: e.matmul(
                        P[6][0:1, 0:TT], Dm1[:, e_:e_ + 1], XS[:, e_ * TT:(e_ + 1) * TT],
                        start=(e_ == 0), stop=(e_ == EC - 1)), "t")
                    tk = inc("tensor", lambda e, e_=e_: e.matmul(
                        P[7][0:1, 0:TT], Dm2[:, e_:e_ + 1],
                        XSQ2[:, (e_ % 4) * TT:(e_ % 4 + 1) * TT],
                        start=(e_ == 0), stop=(e_ == EC - 1)), "t")
                    inc("tensor", lambda e, e_=e_: e.matmul(
                        P[5][0:128, 0:TT], Wxs[:, e_ * 192 + 64:(e_ + 1) * 192],
                        XS[:, e_ * TT:(e_ + 1) * TT],
                        start=(e_ == 0), stop=(e_ == EC - 1)), "t")
                    tk_cdr[e_] = inc("tensor", lambda e, e_=e_: e.matmul(
                        P[4][0:64, 0:TT], Wxs[:, e_ * 192: e_ * 192 + 64],
                        XS[:, e_ * TT:(e_ + 1) * TT],
                        start=(e_ == 0), stop=(e_ == EC - 1)), "t")
                return tk
            tk_stats_q = [None, None, None, None]
            for step in range(EC + 4):
                if ti == 0 and step == 5:
                    tk_sx, tk_s4 = emit_sx_chain()
                    pf47 = emit_xq()
                if ti == NTI - 1 and step == 10:
                    emit_tail_prefix()
                if step < EC:
                    emit_proj(step)
                    emit_xpev(step)
                if 1 <= step <= EC:
                    emit_conv(step - 1)
                    emit_silu(step - 1)
                if 4 <= step <= EC + 3:
                    emit_xsq(step - 4)
                if step in (7, 11, 15):
                    tk_stats_q[(step - 7) // 4] = emit_stats_half(step - 7, step - 3)
            emit_stats_half(12, EC)
            tk_eloop = tk_cdr[EC - 1]
            tk_xsq_free[0] = tk_eloop

            w("scalar", tk_eloop, tk_drb_free)
            tk_drb = inc("scalar", lambda e: e.copy(out=DRB[:], in_=P[4][0:64, 0:TT]), "s")
            pfree[4] = tk_drb
            tk_cl = inc("scalar", lambda e, ti=ti: e.copy(
                out=C_loc[:, ti * TT:(ti + 1) * TT], in_=P[5][0:128, 0:TT]), "s")
            pfree[5] = tk_cl
            tk_m1r = inc("scalar", lambda e: e.copy(out=M1R[:], in_=P[6][0:1, 0:TT]), "s")
            tk_m2r = inc("scalar", lambda e: e.copy(out=M2R[:], in_=P[7][0:1, 0:TT]), "s")
            pfree[6] = tk_m1r
            pfree[7] = tk_m2r
            w("vector", tk_m1r, tk_m2r)
            S.emit("vector", lambda e: e.tensor_mul(RSTR[:], M1R[:], M1R[:]))
            drain("vector")
            tk_var2 = inc("vector", lambda e: e.tensor_sub(VARR[:], M2R[:], RSTR[:]), "v")
            w("tensor", tk_drb, pfree[2], tk_wdt)
            tk_dmm = inc("tensor", lambda e: e.matmul(
                P[2][0:1, 0:TT], wdt0b[:], DRB[:], start=True, stop=True), "t")
            w("scalar", tk_dmm)
            tk_dltev = inc("scalar", lambda e: e.copy(out=T0R[:], in_=P[2][0:1, 0:TT]), "s")
            pfree[2] = tk_dltev
            drain("scalar")
            S.emit("scalar", lambda e: e.activation(out=T1R[:], in_=T0R[:], func=AF.Abs,
                                                    bias=BDT))
            w("scalar", tk_b0r)
            tk_habs = inc("scalar", lambda e: e.activation(out=BM0R[:], in_=B0R[:],
                                                           func=AF.Abs), "s")
            drain("scalar")
            S.emit("scalar", lambda e: e.activation(out=DLTR[:], in_=T1R[:], func=AF.Exp,
                                                    scale=-1.0))
            S.emit("scalar", lambda e: e.activation(out=BDR[:], in_=BM0R[:], func=AF.Exp,
                                                    scale=-1.0))
            w("scalar", tk_var2)
            S.emit("scalar", lambda e: e.activation(out=M2R[:], in_=VARR[:], func=AF.Ln,
                                                    bias=cst[0:1, 0:1]))
            drain("scalar")
            tk_sp1 = inc("scalar", lambda e: e.activation(out=T1R[:], in_=DLTR[:],
                                                          func=AF.Ln, bias=1.0), "s")
            tk_sp2 = inc("scalar", lambda e: e.activation(out=BM0R[:], in_=BDR[:],
                                                          func=AF.Ln, bias=1.0), "s")
            tk_rst = inc("scalar", lambda e: e.activation(out=RSTR[:], in_=M2R[:], func=AF.Exp,
                                                          scale=cst[0:1, 1:2]), "s")
            drain("scalar")

            tk_zmm = [None] * EC
            tk_sz = [None] * EC
            tk_yg = [None] * EC
            tk_wout = [None] * EC
            tk_m1b = tk_rsb = None
            tk_dltev = tk_dl1 = tk_bd = tk_aev = tk_bev = None
            tk_ynf1_free = [None]

            def emit_z(e_):
                bank = e_ % 2
                if e_ < 2:
                    w("tensor", tk_xpev[EC - 2 + e_], pfree[e_ % 2])
                else:
                    w("tensor", tk_sz[e_ - 2])
                w("tensor", tk_w2)
                for kc in range(KD):
                    tk = inc("tensor", lambda e, kc=kc, e_=e_, bank=bank: e.matmul(
                        P[bank][0:128, 0:TT],
                        W2[:, kc * E + e_ * 128: kc * E + e_ * 128 + 128],
                        xnb3[:, :, kc * 128:(kc + 1) * 128],
                        start=(kc == 0), stop=(kc == 7)), "t")
                tk_zmm[e_] = tk

            def emit_ychain(e_):
                bank = e_ % 2
                sl = e_ % 2
                gsl = e_ % 4
                w("scalar", tk_zmm[e_], tk_szb_free[sl])
                S.emit("scalar", lambda e, bank=bank: e.activation(
                    out=TNH[:], in_=P[bank][0:128, 0:TT], func=AF.Tanh,
                    scale=cst[:, 2:3]))
                drain("scalar")
                tk_sz[e_] = inc("scalar", lambda e, sl=sl: e.activation(
                    out=SZB[:, sl * TT:(sl + 1) * TT], in_=TNH[:],
                    func=AF.Identity, scale=cst[:, 2:3], bias=cst[:, 2:3]), "s")
                drain("scalar")
                w("vector", tk_m1b, tk_rsb, tk_eloop, tk_ln)
                S.emit("vector", lambda e, e_=e_: e.scalar_tensor_tensor(
                    out=YNF[0][:], in0=XS[:, e_ * TT:(e_ + 1) * TT], scalar=DD[:, e_:e_ + 1],
                    in1=M1B[:], op0=OP.mult, op1=OP.subtract))
                drain("vector")
                w("vector", tk_ynf1_free[0])
                tk_stt2 = inc("vector", lambda e, e_=e_: e.scalar_tensor_tensor(
                    out=YNF[1][:], in0=YNF[0][:], scalar=1.0,
                    in1=RSB[:], op0=OP.mult, op1=OP.mult), "v")
                drain("vector")
                w("scalar", tk_stt2, tk_ynbf_free[sl])
                tk_yn = inc("scalar", lambda e, e_=e_, sl=sl: e.activation(
                    out=YNBF[:, sl * TT:(sl + 1) * TT], in_=YNF[1][:], func=AF.Identity,
                    scale=LNG[:, e_:e_ + 1], bias=LNB[:, e_:e_ + 1]), "s")
                w("vector", tk_sz[e_], tk_yn, tk_ygb_free[gsl])
                tk_yg[e_] = inc("vector", lambda e, sl=sl, gsl=gsl: e.tensor_mul(
                    YGB[:, gsl * TT:(gsl + 1) * TT], YNBF[:, sl * TT:(sl + 1) * TT],
                    SZB[:, sl * TT:(sl + 1) * TT]), "v")
                tk_ynbf_free[sl] = tk_yg[e_]
                tk_szb_free[sl] = tk_yg[e_]
                tk_ynf1_free[0] = tk_yn

            def emit_wout(e_):
                gsl = e_ % 4
                w("tensor", tk_yg[e_], tk_w3)
                for d_ in range(KD):
                    bank = 4 + d_ // 2
                    col = (d_ % 2) * TT
                    st = (e_ == 0 and d_ % 2 == 0)
                    tk = inc("tensor", lambda e, e_=e_, d_=d_, bank=bank, col=col, st=st, gsl=gsl:
                             e.matmul(P[bank][0:128, col:col + TT],
                                      W3[:, e_ * DM + d_ * 128: e_ * DM + d_ * 128 + 128],
                                      YGB[:, gsl * TT:(gsl + 1) * TT],
                                      start=st, stop=(e_ == EC - 1),
                                      skip_group_check=True), "t")
                tk_wout[e_] = tk
                tk_ygb_free[gsl] = tk

            def emit_scan_level(li, ti=ti):
                sl0 = ti * TT
                d = 1 << li
                if li == 0:
                    w("vector", tk_aev, tk_bev)
                    drain("vector")
                S.emit("vector", lambda e, sl0=sl0: e.tensor_copy(SCA[:], A_loc[:, sl0:sl0 + TT]))
                S.emit("vector", lambda e, sl0=sl0: e.tensor_copy(SCB[:], B_loc[:, sl0:sl0 + TT]))
                drain("vector")
                S.emit("vector", lambda e, sl0=sl0, d=d: e.tensor_mul(
                    A_loc[:, sl0 + d:sl0 + TT], SCA[:, d:TT], SCA[:, 0:TT - d]))
                S.emit("vector", lambda e, d=d: e.tensor_mul(
                    SCC[:, 0:TT - d], SCA[:, d:TT], SCB[:, 0:TT - d]))
                drain("vector")
                S.emit("vector", lambda e, sl0=sl0, d=d: e.tensor_add(
                    B_loc[:, sl0 + d:sl0 + TT], SCB[:, d:TT], SCC[:, 0:TT - d]))
                drain("vector")
                if li == 7:
                    tk_scan_done[ti] = inc("vector", lambda e, sl0=sl0: e.tensor_copy(
                        SCA[:, 0:1], A_loc[:, sl0:sl0 + 1]), "v")

            for step in range(EC + 3):
                if 2 <= step < EC + 2:
                    emit_ychain(step - 2)
                if 11 <= step < 19:
                    emit_scan_level(step - 11)
                if step < EC:
                    emit_z(step)
                if step == 0:
                    w("tensor", tk_m1r, pfree[2])
                    tk_m1bc = inc("tensor", lambda e: e.matmul(
                        P[2][0:128, 0:TT], onesr[:], M1R[:], start=True, stop=True), "t")
                    w("vector", tk_m1bc)
                    tk_m1b = inc("vector", lambda e: e.tensor_copy(M1B[:], P[2][0:128, 0:TT]), "v")
                    pfree[2] = tk_m1b
                if step == 1:
                    w("tensor", tk_rst, pfree[3])
                    tk_rsbc = inc("tensor", lambda e: e.matmul(
                        P[3][0:128, 0:TT], onesr[:], RSTR[:], start=True, stop=True), "t")
                    w("vector", tk_rsbc)
                    tk_rsb = inc("vector", lambda e: e.tensor_copy(RSB[:], P[3][0:128, 0:TT]), "v")
                    pfree[3] = tk_rsb
                if step == 5:
                    w("vector", tk_sp1)
                    S.emit("vector", lambda e: e.tensor_scalar(DLTR[:], T0R[:], BDT,
                                                               0.0, OP.add, OP.max))
                    drain("vector")
                    S.emit("vector", lambda e: e.tensor_add(T0R[:], DLTR[:], T1R[:]))
                    drain("vector")
                    tk_dl1 = inc("vector", lambda e: e.tensor_scalar(DLTR[:], T0R[:], 1e-6,
                                                                     100.0, OP.max, OP.min), "v")
                    drain("vector")
                if step == 9:
                    w("vector", tk_sp2, tk_dl1)
                    S.emit("vector", lambda e: e.tensor_scalar_max(BDR[:], B0R[:], 0.0))
                    drain("vector")
                    S.emit("vector", lambda e: e.tensor_add(T0R[:], BDR[:], BM0R[:]))
                    drain("vector")
                    w("vector", tk_silu[0])
                    S.emit("vector", lambda e: e.tensor_mul(T1R[:], DLTR[:], T0R[:]))
                    drain("vector")
                    tk_bd = inc("vector", lambda e: e.tensor_mul(BDR[:], T1R[:], XS0R[:]), "v")
                    drain("vector")
                if step == 8:
                    w("tensor", tk_dl1, pfree[2])
                    tk_abc = inc("tensor", lambda e: e.matmul(P[2][0:128, 0:TT], onesrb[:],
                                                              DLTR[:], start=True, stop=True), "t")
                    w("scalar", tk_abc, tk_a0)
                    tk_aev = inc("scalar", lambda e, ti=ti: e.activation(
                        out=A_loc[:, ti * TT:(ti + 1) * TT], in_=P[2][0:128, 0:TT],
                        func=AF.Exp, scale=A0c[:]), "s")
                    pfree[2] = tk_aev
                if step == 10:
                    w("tensor", tk_bd, pfree[3])
                    tk_bbc = inc("tensor", lambda e: e.matmul(P[3][0:128, 0:TT], onesrb[:],
                                                              BDR[:], start=True, stop=True), "t")
                    w("scalar", tk_bbc)
                    tk_bev = inc("scalar", lambda e, ti=ti: e.copy(
                        out=B_loc[:, ti * TT:(ti + 1) * TT], in_=P[3][0:128, 0:TT]), "s")
                    pfree[3] = tk_bev
                if step >= 3 and step - 3 < EC:
                    emit_wout(step - 3)
            tk_youts = tk_wout[EC - 1]
            tk_xs_free = tk_yg[EC - 1]



            tk_fmm = [None] * KD
            tk_epi = [None] * KD
            tk_gfo = [None] * KD
            tk_sg = [None] * KD
            tk_tps0 = {2: None, 3: None}
            tk_tps1 = {4: None, 5: None}
            tk_c1s = [None, None]
            tk_adds0 = [None, None]
            tk_adds1 = [None, None]

            def emit_ffn(o):
                bank = o % 2
                w("tensor", tk_xq[0], tk_xq[1], tk_q)
                if o < 2:
                    w("tensor", tk_sz[EC - 2 + o])
                else:
                    w("tensor", tk_epi[o - 2])
                for kc in range(KD):
                    tk = inc("tensor", lambda e, kc=kc, o=o, bank=bank: e.matmul(
                        P[bank][0:128, 0:TT],
                        Wq[:, kc * DM + o * 128: kc * DM + o * 128 + 128],
                        xq3[:, :, kc * 128:(kc + 1) * 128],
                        start=(kc == 0), stop=(kc == 7)), "t")
                tk_fmm[o] = tk

            def emit_epi(o):
                bank = o % 2
                sl = o % 2
                w("vector", tk_fmm[o], tk_s4, tk_gt_free[sl])
                tk_epi[o] = inc("vector", lambda e, o=o, bank=bank, sl=sl: e.tensor_scalar(
                    GT[:, sl * TT:(sl + 1) * TT], P[bank][0:128, 0:TT],
                    SC[:, 10:11], FFB[:, o:o + 1], OP.mult, OP.add), "v")
                w("scalar", tk_epi[o], tk_gf_free[sl])
                tk_gfo[o] = inc("scalar", lambda e, sl=sl: e.activation(
                    out=GF[:, sl * TT:(sl + 1) * TT], in_=GT[:, sl * TT:(sl + 1) * TT],
                    func=AF.Silu), "s")
                sbank = 4 + o // 2
                scol = (o % 2) * TT
                w("vector", tk_gfo[o], tk_youts, tk_sgb_free[sl])
                tk_sg[o] = inc("vector", lambda e, sl=sl, sbank=sbank, scol=scol: e.tensor_mul(
                    SGB[:, sl * TT:(sl + 1) * TT], GF[:, sl * TT:(sl + 1) * TT],
                    P[sbank][0:128, scol:scol + TT]), "v")
                tk_gt_free[sl] = tk_gfo[o]
                tk_gf_free[sl] = tk_sg[o]

            def emit_sgt(o):
                sl = o % 2
                b0_ = 2 if o < 4 else 3
                w("tensor", tk_sg[o])
                if o == 0:
                    w("tensor", pfree[2])
                if o == 4:
                    w("tensor", pfree[3])
                tk_tps0[b0_] = inc("tensor", lambda e, o=o, sl=sl, b0_=b0_: e.transpose(
                    P[b0_][:, (o % 4) * 128:(o % 4 + 1) * 128],
                    SGB[:, sl * TT: sl * TT + 128], identf[:]), "t")
                if o < 2:
                    w("vector", tk_tps0[b0_])
                    tk_c1 = inc("vector", lambda e, o=o, sl=sl: e.tensor_copy(
                        SG1[:, o * 128:(o + 1) * 128],
                        SGB[:, sl * TT + 128: sl * TT + 256]), "v")
                    tk_c1s[o] = tk_c1
                    tk_sgb_free[sl] = (tk_tps0[b0_], tk_c1)
                else:
                    b1_ = 4 if o < 4 else 5
                    if o == 2:
                        w("tensor", tk_sg[1], tk_c1s[0], tk_c1s[1])
                        for oo in range(2):
                            inc("tensor", lambda e, oo=oo: e.transpose(
                                P[4][:, oo * 128:(oo + 1) * 128],
                                SG1[:, oo * 128:(oo + 1) * 128], identf[:]), "t")
                    if o == 4:
                        w("tensor", tk_sg[3])
                    tk_tps1[b1_] = inc("tensor", lambda e, o=o, sl=sl, b1_=b1_: e.transpose(
                        P[b1_][:, (o % 4) * 128:(o % 4 + 1) * 128],
                        SGB[:, sl * TT + 128: sl * TT + 256], identf[:]), "t")
                    tk_sgb_free[sl] = tk_tps1[b1_]

            for step in range(KD + 2):
                if ti == NTI - 1:
                    if step == 0:
                        emit_tail_cc()
                    elif step == 6:
                        emit_tail_schain()
                    elif step == 7:
                        emit_tail_q(0)
                        emit_tail_q(1)
                    elif step == 8:
                        emit_tail_q(2)
                        emit_tail_q(3)
                    elif step == 9:
                        for q_ in range(4, NTI):
                            emit_tail_q(q_)
                if step < KD:
                    emit_ffn(step)
                if 1 <= step <= KD:
                    emit_epi(step - 1)
                if step >= 2:
                    emit_sgt(step - 2)
                if step - 2 == 3:
                    w("vector", tk_tps0[2], tk_xq[1], tk_qab_prev)
                    tk_adds0[0] = inc("vector", lambda e, g0=g0: e.tensor_add(
                        QA[:, 0:512], P[2][:, 0:512], XT[g0 % 2][:, 0:512]), "v")
                    pfree[2] = tk_adds0[0]
                if step - 2 == 7:
                    w("vector", tk_tps0[3])
                    tk_adds0[1] = inc("vector", lambda e, g0=g0: e.tensor_add(
                        QA[:, 512:1024], P[3][:, 0:512], XT[g0 % 2][:, 512:1024]), "v")
                    pfree[3] = tk_adds0[1]
                    w("vector", tk_tps1[4], tk_xq[1], tk_qab_prev)
                    tk_adds1[0] = inc("vector", lambda e, g0=g0: e.tensor_add(
                        QB[:, 0:512], P[4][:, 0:512], XT[(g0 + 1) % 2][:, 0:512]), "v")
                    pfree[4] = tk_adds1[0]
                    w("vector", tk_tps1[5])
                    tk_adds1[1] = inc("vector", lambda e, g0=g0: e.tensor_add(
                        QB[:, 512:1024], P[5][:, 0:512], XT[(g0 + 1) % 2][:, 512:1024]), "v")
                    pfree[5] = tk_adds1[1]
            for s_ in range(2):
                g = g0 + s_
                w("sync", tk_adds0[0] if s_ == 0 else tk_adds1[0])
                w("sync", tk_adds0[1] if s_ == 0 else tk_adds1[1])
                tk_store[g] = dma("sync", y_out[g * 128:(g + 1) * 128, :],
                                  (QA if s_ == 0 else QB)[:], "o")
            xnm_free[0] = tk_tp[1]
            xnm_free[1] = tk_tp[3]
            xt_free[0] = (tk_adds0[0], tk_adds0[1])
            xt_free[1] = (tk_adds1[0], tk_adds1[1])
            tk_qab_prev = (tk_store[g0], tk_store[g0 + 1])
            tk_xnb_free = tk_zmm[EC - 1]
            tk_xq_free = tk_fmm[KD - 1]
            tk_drb_free = tk_dmm
            pfree[0] = tk_epi[KD - 2]
            pfree[1] = tk_epi[KD - 1]

        emit_tail_ymm((tk_epi[KD - 2], tk_epi[KD - 1]))

        for g in range(16):
            w("sync", tk_store[g])

        block = ctx.enter_context(nc.Block())
        S.run(block)
    return nc


def kernel(**inputs):
    x = np.asarray(inputs["L0_attn_x"])[0]
    Win = np.asarray(inputs["L0_attn_Win"])
    conv_w = np.asarray(inputs["L0_attn_conv_w"])
    conv_b = np.asarray(inputs["L0_attn_conv_b"])
    Wx = np.asarray(inputs["L0_attn_Wx"])
    Wdt = np.asarray(inputs["L0_attn_Wdt"])
    bdt = np.asarray(inputs["L0_attn_bdt"])
    A_log = np.asarray(inputs["L0_attn_A_log"])
    D = np.asarray(inputs["L0_attn_D"])
    ln_g = np.asarray(inputs["L0_attn_ln_g"])
    ln_b = np.asarray(inputs["L0_attn_ln_b"])
    Wout = np.asarray(inputs["L0_attn_Wout"])
    res_scale = np.asarray(inputs["L0_attn_res_scale"])
    Wf = np.asarray(inputs["L0_ffn_weight"])
    ffb = np.asarray(inputs["L0_ffn_bias"])

    C = np.ascontiguousarray
    w_xp = C(Win[:, :E])
    w_z = C(Win[:, E:2 * E])
    w_b0 = C(Win[:, 2 * E].reshape(KD, 128).T)
    w_ft = C(Wf.T)
    wdt0 = C(Wdt[:, 0:1])
    alog0 = C(A_log[0].reshape(H, 1))
    cw = C(conv_w.reshape(EC, 128, 4).transpose(1, 0, 2).reshape(128, EC * 4))
    col = lambda v: C(v.reshape(EC, 128).T)
    ffbc = C(ffb.reshape(KD, 128).T)
    rs128 = C(np.broadcast_to(res_scale.reshape(1, 1), (128, 1)))
    bdt0 = C(bdt[0].reshape(1, 1))

    in_maps = []
    for i in range(NCORE):
        t0 = i * T
        xh = x[t0 - 3:t0] if i > 0 else np.zeros((3, DM), np.float32)
        sel = np.zeros((128, NCORE), np.float32)
        sel[:, i] = 1.0
        m8 = np.zeros((128, NCORE), np.float32)
        m8[:, i] = 1.0
        m16 = np.zeros((128, 2 * NCORE), np.float32)
        m16[:, 2 * i:2 * i + 2] = 1.0
        cst_all = np.concatenate([
            cw, col(conv_b), col(D), col(ln_g), col(ln_b), ffbc, rs128,
            sel, m8, m16, np.broadcast_to(bdt0, (128, 1)), alog0], axis=1)
        in_maps.append({
            "x": C(x[t0:t0 + T]), "xh": C(xh),
            "w_xp": w_xp, "w_z": w_z, "w_b0": w_b0, "w_x": Wx, "w_dt0": wdt0,
            "w_out": Wout, "w_ft": w_ft,
            "cst_all": C(cst_all.astype(np.float32)),
        })
    nc = build_nc()
    res = run_bass_kernel_spmd(nc, in_maps, core_ids=list(range(NCORE)))
    out = np.concatenate([r["y"] for r in res.results], axis=0)
    return out.reshape(1, NCORE * T, DM)


# version 26
# speedup vs baseline: 1.0454x; 1.0218x over previous
import numpy as np
import concourse.bass as bass
import concourse.mybir as mybir
from concourse.bass_utils import run_bass_kernel_spmd
from contextlib import ExitStack

F32 = mybir.dt.float32
BF16 = mybir.dt.bfloat16
F8 = mybir.dt.float8e4
AF = mybir.ActivationFunctionType
OP = mybir.AluOpType

NCORE = 8
T = 2048
DM = 1024
E = 2048
EC = 16
KD = 8
TT = 256
NTI = T // TT
H = 128
ENGS = ("sync", "gpsimd", "tensor", "vector", "scalar")


class Sched:
    def __init__(self):
        self.st = {e: [] for e in ENGS}
        self.cnt = {}
        self.lastw = {}

    def inc(self, eng, fn, sem, semname, n=1):
        v = self.cnt.get(semname, 0) + n
        self.cnt[semname] = v
        self.st[eng].append(lambda e, fn=fn, sem=sem, n=n: fn(e).then_inc(sem, n))
        return (semname, sem, v)

    def emit(self, eng, fn):
        self.st[eng].append(fn)

    def wait(self, eng, ticket):
        if ticket is None:
            return
        if isinstance(ticket, (tuple, list)) and ticket and isinstance(ticket[0], (tuple, list, type(None))):
            for t in ticket:
                self.wait(eng, t)
            return
        semname, sem, v = ticket
        key = (eng, semname)
        if self.lastw.get(key, -1) >= v:
            return
        self.lastw[key] = v
        self.st[eng].append(lambda e, sem=sem, v=v: e.wait_ge(sem, v))

    def run(self, block):
        for name in ENGS:
            def body(e, name=name):
                for fn in self.st[name]:
                    fn(e)
            getattr(block, name)(body)


def build_nc():
    nc = bass.Bass("TRN2", target_bir_lowering=False)
    dt = nc.dram_tensor
    x_in = dt("x", [T, DM], F32, kind="ExternalInput").ap()
    xh_in = dt("xh", [3, DM], F32, kind="ExternalInput").ap()
    w_xp = dt("w_xp", [DM, E], F32, kind="ExternalInput").ap()
    w_z = dt("w_z", [DM, E], F32, kind="ExternalInput").ap()
    w_b0 = dt("w_b0", [128, KD], F32, kind="ExternalInput").ap()
    w_x = dt("w_x", [E, 192], F32, kind="ExternalInput").ap()
    w_dt0 = dt("w_dt0", [64, 1], F32, kind="ExternalInput").ap()
    w_out = dt("w_out", [E, DM], F32, kind="ExternalInput").ap()
    w_ft = dt("w_ft", [DM, DM], F32, kind="ExternalInput").ap()
    cst_in = dt("cst_all", [128, 171], F32, kind="ExternalInput").ap()
    y_out = dt("y", [T, DM], F32, kind="ExternalOutput").ap()
    cc1i = dt("cc1i", [128, NCORE], F32)
    cc1o = dt("cc1o", [128, NCORE], F32)
    cc2i = dt("cc2i", [128, 2 * NCORE], F32)
    cc2o = dt("cc2o", [128, 2 * NCORE], F32)
    identf_d = nc.inline_tensor(np.eye(128, dtype=np.float32), "identf_dram")

    ctx = ExitStack()
    with ctx:
        sb = lambda n, s, d: ctx.enter_context(nc.sbuf_tensor(n, s, d))
        pst = lambda n: ctx.enter_context(nc.psum_tensor(n, [128, 512], F32))
        mksem = lambda n: ctx.enter_context(nc.semaphore(n))

        W1 = sb("W1", [128, KD * E], BF16)
        W2 = sb("W2", [128, KD * E], BF16)
        W3 = sb("W3", [128, EC * DM], BF16)
        Wq = sb("Wq", [128, KD * DM], F8)
        Wxs = sb("Wxs", [128, EC * 192], BF16)
        DG = sb("DG", [128, EC * 4 * 128], BF16)
        wb0 = sb("wb0", [128, KD], BF16)
        wdt0b = sb("wdt0b", [64, 1], BF16)
        identf = sb("identf", [128, 128], F32)
        identb = sb("identb", [128, 128], BF16)
        onesr = sb("onesr", [1, 128], F32)
        onesbf = sb("onesbf", [128, 1], BF16)
        onesrb = sb("onesrb", [1, 128], BF16)
        cst = sb("cst", [128, 4], F32)
        XS = sb("XS", [128, EC * TT], BF16)
        XSQ2 = sb("XSQ2", [128, 4 * TT], BF16)
        XT = [sb(f"XT{i}", [128, DM], F32) for i in range(2)]
        XN = [sb(f"XN{i}", [128, DM], F32) for i in range(2)]
        XNB = sb("XNB", [128, 2 * DM], BF16)
        XQ = sb("XQ", [128, 2 * DM], F8)
        QA = sb("QA", [128, DM], F32)
        QB = sb("QB", [128, DM], F32)
        XPB = sb("XPB", [128, 2 * 260], BF16)
        XPT = sb("XPT", [128, EC * 3], BF16)
        A_loc = sb("A_loc", [128, T], BF16)
        B_loc = sb("B_loc", [128, T], BF16)
        C_loc = sb("C_loc", [128, T], BF16)
        SCA = sb("SCA", [128, TT], BF16)
        SCB = sb("SCB", [128, TT], BF16)
        SCC = sb("SCC", [128, TT], BF16)
        DRB = sb("DRB", [64, TT], BF16)
        M1B = sb("M1B", [128, TT], F32)
        RSB = sb("RSB", [128, TT], F32)
        SZB = sb("SZB", [128, 2 * TT], BF16)
        TNH = sb("TNH", [128, TT], F32)
        YNF = [sb(f"YNF{i}", [128, TT], F32) for i in range(2)]
        YNBF = sb("YNBF", [128, 2 * TT], BF16)
        YGB = sb("YGB", [128, 4 * TT], BF16)
        GT = sb("GT", [128, 2 * TT], BF16)
        GF = sb("GF", [128, 2 * TT], F32)
        SGB = sb("SGB", [128, 2 * TT], F32)
        SG1 = sb("SG1", [128, 256], F32)
        HFM = sb("HFM", [128, KD * 3], BF16)
        MI = sb("MI", [128, 16], F32)
        MN = sb("MN", [128, 16], F32)
        MNI = sb("MNI", [128, 16], F32)
        A0c = sb("A0c", [128, 1], F32)
        Dm1 = sb("Dm1", [128, EC], BF16)
        Dm2 = sb("Dm2", [128, EC], BF16)
        CONSTS = sb("CONSTS", [128, 171], F32)
        CW = CONSTS.ap()[:, 0:64]
        CB = CONSTS.ap()[:, 64:80]
        DD = CONSTS.ap()[:, 80:96]
        LNG = CONSTS.ap()[:, 96:112]
        LNB = CONSTS.ap()[:, 112:128]
        FFB = CONSTS.ap()[:, 128:136]
        RS1 = CONSTS.ap()[:, 136:137]
        SEL = CONSTS.ap()[:, 137:145]
        M8 = CONSTS.ap()[:, 145:153]
        M16 = CONSTS.ap()[:, 153:169]
        BDT = CONSTS.ap()[0:1, 169:170]
        ALG = CONSTS.ap()[:, 170:171]
        SC = sb("SC", [128, 16], F32)
        LL = sb("LL", [128, 9], F32)
        PP = sb("PP", [128, 9], F32)
        SS = sb("SS", [128, 9], F32)
        SST = sb("SST", [128, NTI], F32)
        SIN = sb("SIN", [128, 1], F32)
        EQF = sb("EQF", [128, 2 * NTI], F32)
        SELT = sb("SELT", [128, NCORE], F32)
        PAY1 = sb("PAY1", [128, NCORE], F32)
        PAY2 = sb("PAY2", [128, 2 * NCORE], F32)
        PAY2b = sb("PAY2b", [128, 2 * NCORE], F32)
        G1 = sb("G1", [128, NCORE], F32)
        G2 = sb("G2", [128, 2 * NCORE], F32)
        HCOL = sb("HCOL", [3, 4], F32)
        B0R = sb("B0R", [1, TT], BF16)
        XS0R = sb("XS0R", [1, TT], BF16)
        T0R = sb("T0R", [1, TT], BF16)
        T1R = sb("T1R", [1, TT], BF16)
        DLTR = sb("DLTR", [1, TT], BF16)
        BM0R = sb("BM0R", [1, TT], BF16)
        BDR = sb("BDR", [1, TT], BF16)
        M1R = sb("M1R", [1, TT], F32)
        M2R = sb("M2R", [1, TT], F32)
        VARR = sb("VARR", [1, TT], F32)
        RSTR = sb("RSTR", [1, TT], F32)

        P = [pst(f"p{i}") for i in range(8)]

        semx = mksem("semx")
        semw = mksem("semw")
        semt = mksem("semt")
        semv = mksem("semv")
        sems = mksem("sems")
        semc = mksem("semc")
        semo = mksem("semo")
        semx2 = mksem("semx2")
        semw2 = mksem("semw2")
        semo2 = mksem("semo2")
        semg = mksem("semg")
        sema = mksem("sema")
        semb = mksem("semb")
        semh = mksem("semh")
        semq = mksem("semq")
        semsx = mksem("semsx")
        S = Sched()
        SEM = {"x": semx, "w": semw, "t": semt, "v": semv, "s": sems,
               "c": semc, "o": semo, "x2": semx2, "w2": semw2, "o2": semo2,
               "g": semg, "a": sema, "b": semb, "h": semh, "q": semq,
               "sx": semsx}
        _rot = {"x": ["x", "x2"], "w": ["w", "w2"], "o": ["o", "o2"]}
        _ri = {"x": 0, "w": 0, "o": 0}

        def inc(eng, fn, s, n=1):
            return S.inc(eng, fn, SEM[s], s, n)

        def w(eng, *tickets):
            for tk in tickets:
                S.wait(eng, tk)

        def dma(eng, out, in_, s, accum=None):
            name = _rot[s][_ri[s]]
            _ri[s] ^= 1
            prior = S.cnt.get(name, 0)
            if prior:
                w(eng, (name, SEM[name], prior))
            if accum is not None:
                return inc(eng, lambda e, o=out, i=in_, a=accum: e.dma_start(
                    out=o, in_=i, accum_op=a), name, 16)
            return inc(eng, lambda e, o=out, i=in_: e.dma_start(out=o, in_=i), name, 16)

        def drain(eng):
            S.emit(eng, lambda e: e.drain())

        xnb3 = XNB.ap().rearrange("p (s c) -> p s c", c=DM)
        xq3 = XQ.ap().rearrange("p (s c) -> p s c", c=DM)

        tk_idf = dma("sync", identf[:], identf_d.ap(), "x")
        tk_idb = dma("gpsimd", identb[:], identf_d.ap(), "w")
        tk_hx = dma("gpsimd", XS[0:3, 8 * TT:8 * TT + DM], xh_in, "w")
        tk_small = dma("sync", CONSTS[:], cst_in, "x")
        tk_w1 = dma("gpsimd", W1.ap().rearrange("p (kc c) -> p kc c", c=E),
                    w_xp.rearrange("(kc p) c -> p kc c", p=128), "w")
        tk_wfl = [None] * KD
        for g_ in range(KD):
            if g_ == 1:
                w("gpsimd", ("h", semh, KD))
            if g_ >= 2:
                w("gpsimd", ("a", sema, g_ - 1))
            tk_wfl[g_] = dma("gpsimd", XN[g_ % 2][:],
                             w_ft[g_ * 128:(g_ + 1) * 128, :], "w")
        tk_wx = dma("gpsimd", Wxs.ap().rearrange("p (ec c) -> p ec c", c=192),
                    w_x.rearrange("(ec p) c -> p ec c", p=128), "w")
        tk_wb0 = dma("gpsimd", wb0[:], w_b0, "w")
        tk_wdt = dma("gpsimd", wdt0b[:], w_dt0, "w")
        tk_w2 = dma("gpsimd", W2.ap().rearrange("p (kc c) -> p kc c", c=E),
                    w_z.rearrange("(kc p) c -> p kc c", p=128), "w")
        tk_w3 = dma("gpsimd", W3.ap().rearrange("p (ec c) -> p ec c", c=DM),
                    w_out.rearrange("(ec p) c -> p ec c", p=128), "w")
        qslot = [GF.ap()[:, 0:256], GF.ap()[:, 256:512],
                 SGB.ap()[:, 0:256], SGB.ap()[:, 256:512],
                 M1B.ap(), RSB.ap(), TNH.ap(), YNF[0].ap()]
        tk_wql = [None] * 32
        for qt in range(32):
            if qt >= 8:
                w("gpsimd", ("q", semq, qt - 7))
            kc, q4 = qt // 4, qt % 4
            tk_wql[qt] = dma("gpsimd", qslot[qt % 8],
                             w_ft[kc * 128:(kc + 1) * 128,
                                  q4 * 256:(q4 + 1) * 256], "w")

        S.emit("vector", lambda e: e.memset(cst[:, 0:1], 1e-5))
        S.emit("vector", lambda e: e.memset(cst[:, 1:2], -0.5))
        S.emit("vector", lambda e: e.memset(cst[:, 2:3], 0.5))
        S.emit("vector", lambda e: e.memset(onesbf[:], 1.0))
        S.emit("vector", lambda e: e.memset(onesr[:], 1.0))
        S.emit("vector", lambda e: e.memset(onesrb[:], 1.0))
        S.emit("vector", lambda e: e.memset(XPT[:], 0.0))
        S.emit("vector", lambda e: e.memset(SC[:, 0:1], 0.0))
        S.emit("vector", lambda e: e.memset(SC[:, 6:7], 0.0))
        tk_cst = inc("vector", lambda e: e.memset(XPB[:], 0.0), "v")
        w("scalar", tk_small)
        S.emit("scalar", lambda e: e.activation(out=SC[:, 11:12], in_=ALG, func=AF.Exp))
        drain("scalar")
        tk_a0 = inc("scalar", lambda e: e.mul(A0c[:], SC[:, 11:12], -1.0), "s")
        w("vector", tk_small)
        S.emit("vector", lambda e: e.tensor_scalar_mul(Dm1[:], DD, 1.0 / 2048.0))
        S.emit("vector", lambda e: e.tensor_mul(QA[:, 0:EC], DD, DD))
        drain("vector")
        S.emit("vector", lambda e: e.tensor_scalar_mul(Dm2[:], QA[:, 0:EC], 1.0 / 2048.0))
        drain("vector")
        S.emit("vector", lambda e: e.tensor_scalar(QA[:, 0:EC], LNG, RS1, None, OP.mult))
        S.emit("vector", lambda e: e.tensor_scalar(QA[:, EC:2 * EC], LNB, RS1, None, OP.mult))
        drain("vector")
        S.emit("vector", lambda e: e.tensor_copy(LNG, QA[:, 0:EC]))
        tk_ln = inc("vector", lambda e: e.tensor_copy(LNB, QA[:, EC:2 * EC]), "v")
        drain("vector")
        w("scalar", tk_idb)
        dgj = [0]

        def emit_dg(n):
            tk = None
            for _ in range(n):
                j = dgj[0]
                if j >= EC * 4:
                    return None
                dgj[0] += 1
                tk = inc("scalar", lambda e, j=j: e.activation(
                    out=DG[:, j * 128:(j + 1) * 128], in_=identb[:], func=AF.Copy,
                    scale=CW[:, j:j + 1]), "s")
            return tk

        hx0, hs0 = 8 * TT, 8 * TT + DM
        w("scalar", tk_hx)
        tk_hsq = inc("scalar", lambda e: e.square(XS[0:3, hs0:hs0 + DM],
                                                  XS[0:3, hx0:hx0 + DM]), "s")

        xt_free = [None, None]
        w("vector", tk_a0)
        tk_ax = tk_sw = tk_q = tk_aw = tk_aw128 = None
        tk_sum_prev = None
        xnm_free_w = [None, None]
        for g in range(16):
            sl = g % 2
            w("sync", xt_free[sl])
            tk_xt = dma("sync", XT[sl][:], x_in[g * 128:(g + 1) * 128, :], "x")
            w("scalar", tk_xt, tk_sum_prev, tk_ln)
            tk_sq = inc("scalar", lambda e, sl=sl: e.square(QA[:], XT[sl][:]), "s")
            w("vector", tk_xt, tk_cst)
            tk_vs = inc("vector", lambda e, sl=sl: e.reduce_sum(
                SC[:, 1:2], XT[sl][:], axis=mybir.AxisListType.X), "v")
            w("vector", tk_sq)
            tk_sum_prev = inc("vector", lambda e: e.reduce_sum(
                SC[:, 11:12], QA[:], axis=mybir.AxisListType.X), "v")
            drain("vector")
            S.emit("vector", lambda e: e.tensor_scalar_mul(SC[:, 12:13], SC[:, 1:2], 1.0 / DM))
            drain("vector")
            S.emit("vector", lambda e, g=g: e.tensor_scalar_mul(MN[:, g:g + 1],
                                                                SC[:, 12:13], -1.0))
            S.emit("vector", lambda e: e.tensor_mul(SC[:, 13:14], SC[:, 12:13], SC[:, 12:13]))
            S.emit("vector", lambda e: e.tensor_scalar_mul(SC[:, 14:15], SC[:, 11:12], 1.0 / DM))
            drain("vector")
            tk_var = inc("vector", lambda e: e.tensor_sub(SC[:, 15:16], SC[:, 14:15],
                                                          SC[:, 13:14]), "v")
            if 4 <= g < 12:
                j = g - 4
                w("vector", tk_wfl[j])
                S.emit("vector", lambda e, j=j: e.tensor_reduce(
                    SC[:, 9:10], XN[j % 2][:], axis=mybir.AxisListType.X, op=OP.max,
                    apply_absolute_value=True))
                drain("vector")
                wco, wci = (3, 6) if j % 2 == 0 else (6, 3)
                tk_wm = inc("vector", lambda e, wco=wco, wci=wci: e.tensor_max(
                    SC[:, wco:wco + 1], SC[:, wci:wci + 1], SC[:, 9:10]), "a")
                if j >= 6:
                    xnm_free_w[j % 2] = tk_wm
                drain("vector")
            w("scalar", tk_var)
            tk_sd = inc("scalar", lambda e: e.activation(out=SC[:, 11:12], in_=SC[:, 15:16],
                                                         func=AF.Sqrt, bias=cst[:, 0:1]), "s")
            w("vector", tk_sd)
            S.emit("vector", lambda e, g=g: e.reciprocal(MI[:, g:g + 1], SC[:, 11:12]))
            drain("vector")
            tk_xn_v = inc("vector", lambda e, g=g, sl=sl: e.tensor_scalar(
                QB[:], XT[sl][:], MN[:, g:g + 1], MI[:, g:g + 1],
                OP.add, OP.mult), "v")
            drain("vector")
            S.emit("vector", lambda e: e.tensor_reduce(
                SC[:, 1:2], QB[:], axis=mybir.AxisListType.X, op=OP.max,
                apply_absolute_value=True))
            drain("vector")
            xco, xci = (2, 0) if g % 2 == 0 else (0, 2)
            tk_ax = inc("vector", lambda e, xco=xco, xci=xci: e.tensor_max(
                SC[:, xco:xco + 1], SC[:, xci:xci + 1], SC[:, 1:2]), "b")
            drain("vector")
            if g == 5:
                w("scalar", tk_hm)
                tk_he = inc("scalar", lambda e: e.copy(out=XPT[:],
                                                       in_=P[3][:, 0:EC * 3]), "s")
            if g >= 12:
                for qq in range(8):
                    qt = (g - 12) * 8 + qq
                    kc, q4 = qt // 4, qt % 4
                    w("scalar", tk_wql[qt], tk_sw)
                    tk_q = inc("scalar", lambda e, kc=kc, q4=q4, qt=qt: e.activation(
                        out=Wq[:, kc * DM + q4 * 256: kc * DM + (q4 + 1) * 256],
                        in_=qslot[qt % 8], func=AF.Identity,
                        scale=SC[:, 8:9]), "q")
            emit_dg(6 if g < 12 else 0)
            xt_free[sl] = (tk_xn_v, tk_sq)
            if g == 11:
                w("tensor", tk_wm, tk_idf)
                tk_tw = inc("tensor", lambda e: e.transpose(
                    P[0][0:1, 128:256], SC[:, 6:7], identf[:]), "t")
                w("vector", tk_tw)
                S.emit("vector", lambda e: e.tensor_reduce(
                    SC[0:1, 1:2], P[0][0:1, 128:256], axis=mybir.AxisListType.X, op=OP.max))
                drain("vector")
                tk_aw = inc("vector", lambda e: e.tensor_copy(VARR[0:1, 0:1], SC[0:1, 1:2]), "v")
                w("tensor", tk_aw)
                tk_bc2 = inc("tensor", lambda e: e.matmul(P[1][:, 1:2], onesr[:],
                                                          VARR[0:1, 0:1],
                                                          start=True, stop=True), "t")
                w("vector", tk_bc2)
                tk_aw128 = inc("vector", lambda e: e.tensor_copy(SC[:, 7:8], P[1][:, 1:2]), "v")
                drain("vector")
                S.emit("vector", lambda e: e.tensor_scalar_max(SC[:, 1:2], SC[:, 7:8], 1e-12))
                drain("vector")
                S.emit("vector", lambda e: e.reciprocal(SC[:, 11:12], SC[:, 1:2]))
                drain("vector")
                S.emit("vector", lambda e: e.tensor_scalar(SC[:, 12:13], SC[:, 11:12],
                                                           448.0, 448.0, OP.mult, OP.min))
                drain("vector")
                S.emit("vector", lambda e: e.tensor_scalar_mul(SC[:, 8:9], SC[:, 12:13], 0.5))
                tk_sw = inc("vector", lambda e: e.reciprocal(SC[:, 9:10], SC[:, 12:13]), "v")
                drain("vector")
            if g == 0:
                w("vector", tk_hx, tk_hsq)
                S.emit("vector", lambda e: e.reduce_sum(HCOL[0:3, 0:1], XS[0:3, hx0:hx0 + DM],
                                                        axis=mybir.AxisListType.X))
                S.emit("vector", lambda e: e.reduce_sum(HCOL[0:3, 1:2], XS[0:3, hs0:hs0 + DM],
                                                        axis=mybir.AxisListType.X))
                drain("vector")
                S.emit("vector", lambda e: e.tensor_scalar_mul(HCOL[0:3, 2:3], HCOL[0:3, 0:1],
                                                               1.0 / DM))
                drain("vector")
                S.emit("vector", lambda e: e.tensor_mul(HCOL[0:3, 0:1], HCOL[0:3, 2:3],
                                                        HCOL[0:3, 2:3]))
                S.emit("vector", lambda e: e.tensor_scalar_mul(HCOL[0:3, 3:4], HCOL[0:3, 1:2],
                                                               1.0 / DM))
                drain("vector")
                tk_hv = inc("vector", lambda e: e.tensor_sub(HCOL[0:3, 1:2], HCOL[0:3, 3:4],
                                                             HCOL[0:3, 0:1]), "v")
                w("scalar", tk_hv)
                tk_hsd = inc("scalar", lambda e: e.activation(
                    out=HCOL[0:3, 3:4], in_=HCOL[0:3, 1:2],
                    func=AF.Sqrt, bias=cst[0:3, 0:1]), "s")
                w("vector", tk_hsd)
                S.emit("vector", lambda e: e.reciprocal(HCOL[0:3, 0:1], HCOL[0:3, 3:4]))
                drain("vector")
                S.emit("vector", lambda e: e.tensor_scalar_mul(HCOL[0:3, 1:2],
                                                               HCOL[0:3, 2:3], -1.0))
                drain("vector")
                tk_hap = inc("vector", lambda e: e.tensor_scalar(
                    XN[1][0:3, :], XS[0:3, hx0:hx0 + DM],
                    HCOL[0:3, 1:2], HCOL[0:3, 0:1], OP.add, OP.mult), "v")
                drain("vector")
                w("tensor", tk_hap, tk_idf)
                for f in range(KD):
                    tk_ht = inc("tensor", lambda e, f=f: e.transpose(
                        P[2][:, f * 3:(f + 1) * 3], XN[1][0:3, f * 128:(f + 1) * 128],
                        identf[0:3, 0:3]), "h")
                w("scalar", tk_ht)
                tk_hf = inc("scalar", lambda e: e.copy(out=HFM[:], in_=P[2][:, 0:24]), "s")
                w("tensor", tk_hf, tk_w1)
                for ec in range(EC):
                    for kc in range(KD):
                        tk_hm = inc("tensor", lambda e, ec=ec, kc=kc: e.matmul(
                            P[3][:, ec * 3:(ec + 1) * 3],
                            W1[:, kc * E + ec * 128: kc * E + ec * 128 + 128],
                            HFM[:, kc * 3:(kc + 1) * 3],
                            start=(ec == 0 and kc == 0), stop=(kc == 7),
                            skip_group_check=True), "t")
        tk_x_pre = [None, None]
        for s_ in range(2):
            w("sync", xt_free[s_])
            tk_x_pre[s_] = dma("sync", XT[s_][:], x_in[s_ * 128:(s_ + 1) * 128, :], "x")
            xt_free[s_] = None
        w("tensor", tk_ax, tk_idf)
        tk_tr = inc("tensor", lambda e: e.transpose(P[0][0:1, 0:128], SC[:, 0:1], identf[:]), "t")
        w("vector", tk_tr)
        S.emit("vector", lambda e: e.tensor_reduce(SC[0:1, 1:2], P[0][0:1, 0:128],
                                                   axis=mybir.AxisListType.X, op=OP.max))
        drain("vector")
        tk_axs = inc("vector", lambda e: e.tensor_copy(VARR[0:1, 0:1], SC[0:1, 1:2]), "v")
        w("tensor", tk_axs)
        tk_bc = inc("tensor", lambda e: e.matmul(P[1][:, 0:1], onesr[:], VARR[0:1, 0:1],
                                                 start=True, stop=True), "t")
        w("vector", tk_bc, tk_small)
        tk_ax128 = inc("vector", lambda e: e.tensor_copy(SC[:, 2:3], P[1][:, 0:1]), "v")
        drain("vector")
        tk_p1 = inc("vector", lambda e: e.tensor_scalar(PAY1[:], M8, SC[:, 2:3],
                                                        None, OP.mult), "v")
        w("sync", tk_p1)
        tk_ccin = dma("sync", cc1i.ap(), PAY1[:], "x")
        w("gpsimd", tk_ccin)
        tk_cc1 = inc("gpsimd", lambda e: e.collective_compute(
            "AllReduce", OP.add, replica_groups=[list(range(NCORE))],
            ins=[cc1i.ap().opt()], outs=[cc1o.ap().opt()]), "c")
        w("gpsimd", tk_cc1)
        w("sync", tk_cc1)
        tk_g1 = dma("sync", G1[:], cc1o.ap(), "x")
        xnm_free = [xnm_free_w[0], (xnm_free_w[1], tk_ht)]

        tl = {"g2": None, "cc2": None, "cs": [None] * NTI, "p2": None}

        def emit_tail_prefix():
            w("vector", tk_scan_done[NTI - 2])
            S.emit("vector", lambda e: e.memset(LL[:, 0:1], 0.0))
            S.emit("vector", lambda e: e.memset(PP[:, 0:1], 1.0))
            for q in range(NTI - 1):
                ec_ = (q + 1) * TT - 1
                S.emit("vector", lambda e, q=q, ec_=ec_: e.tensor_copy(
                    EQF[:, 2 * q:2 * q + 1], A_loc[:, ec_:ec_ + 1]))
                S.emit("vector", lambda e, q=q, ec_=ec_: e.tensor_copy(
                    EQF[:, 2 * q + 1:2 * q + 2], B_loc[:, ec_:ec_ + 1]))
            drain("vector")
            for q in range(NTI - 1):
                S.emit("vector", lambda e, q=q: e.scalar_tensor_tensor(
                    out=LL[:, q + 1:q + 2], in0=LL[:, q:q + 1],
                    scalar=EQF[:, 2 * q:2 * q + 1],
                    in1=EQF[:, 2 * q + 1:2 * q + 2], op0=OP.mult, op1=OP.add))
                S.emit("vector", lambda e, q=q: e.tensor_scalar(
                    PP[:, q + 1:q + 2], PP[:, q:q + 1], EQF[:, 2 * q:2 * q + 1],
                    None, OP.mult))
                drain("vector")

        def emit_tail_cc():
            q = NTI - 1
            ec_ = (q + 1) * TT - 1
            w("vector", tk_scan_done[NTI - 1])
            S.emit("vector", lambda e, q=q, ec_=ec_: e.tensor_copy(
                EQF[:, 2 * q:2 * q + 1], A_loc[:, ec_:ec_ + 1]))
            S.emit("vector", lambda e, q=q, ec_=ec_: e.tensor_copy(
                EQF[:, 2 * q + 1:2 * q + 2], B_loc[:, ec_:ec_ + 1]))
            drain("vector")
            S.emit("vector", lambda e, q=q: e.scalar_tensor_tensor(
                out=LL[:, q + 1:q + 2], in0=LL[:, q:q + 1],
                scalar=EQF[:, 2 * q:2 * q + 1],
                in1=EQF[:, 2 * q + 1:2 * q + 2], op0=OP.mult, op1=OP.add))
            S.emit("vector", lambda e, q=q: e.tensor_scalar(
                PP[:, q + 1:q + 2], PP[:, q:q + 1], EQF[:, 2 * q:2 * q + 1],
                None, OP.mult))
            drain("vector")
            for j in range(NCORE):
                S.emit("vector", lambda e, j=j: e.tensor_copy(
                    PAY2[:, 2 * j:2 * j + 1], PP[:, NTI:NTI + 1]))
                S.emit("vector", lambda e, j=j: e.tensor_copy(
                    PAY2[:, 2 * j + 1:2 * j + 2], LL[:, NTI:NTI + 1]))
            drain("vector")
            tl["p2"] = inc("vector", lambda e: e.tensor_mul(PAY2b[:], PAY2[:], M16), "v")
            w("sync", tl["p2"])
            tk_cc2in = dma("sync", cc2i.ap(), PAY2b[:], "x")
            w("gpsimd", tk_cc2in)
            tl["cc2"] = inc("gpsimd", lambda e: e.collective_compute(
                "AllReduce", OP.add, replica_groups=[list(range(NCORE))],
                ins=[cc2i.ap().opt()], outs=[cc2o.ap().opt()]), "c")
            w("gpsimd", tl["cc2"])
            w("sync", tl["cc2"])
            tl["g2"] = dma("sync", G2[:], cc2o.ap(), "x")

        def emit_tail_schain():
            w("vector", tl["g2"])
            S.emit("vector", lambda e: e.memset(SS[:, 0:1], 0.0))
            drain("vector")
            for j in range(NCORE - 1):
                S.emit("vector", lambda e, j=j: e.scalar_tensor_tensor(
                    out=SS[:, j + 1:j + 2], in0=SS[:, j:j + 1],
                    scalar=G2[:, 2 * j:2 * j + 1],
                    in1=G2[:, 2 * j + 1:2 * j + 2], op0=OP.mult, op1=OP.add))
                drain("vector")
            S.emit("vector", lambda e: e.tensor_mul(SELT[:], SS[:, 0:NCORE], SEL))
            drain("vector")
            S.emit("vector", lambda e: e.tensor_add(SELT[:, 0:4], SELT[:, 0:4], SELT[:, 4:8]))
            drain("vector")
            S.emit("vector", lambda e: e.tensor_add(SELT[:, 0:2], SELT[:, 0:2], SELT[:, 2:4]))
            drain("vector")
            S.emit("vector", lambda e: e.tensor_add(SIN[:], SELT[:, 0:1], SELT[:, 1:2]))
            drain("vector")
            for q in range(NTI):
                S.emit("vector", lambda e, q=q: e.scalar_tensor_tensor(
                    out=SST[:, q:q + 1], in0=PP[:, q:q + 1], scalar=SIN[:],
                    in1=LL[:, q:q + 1], op0=OP.mult, op1=OP.add))
            drain("vector")

        def emit_tail_q(q):
            sl0 = q * TT
            S.emit("vector", lambda e, q=q, sl0=sl0: e.scalar_tensor_tensor(
                out=SCB[:], in0=A_loc[:, sl0:sl0 + TT], scalar=SST[:, q:q + 1],
                in1=B_loc[:, sl0:sl0 + TT], op0=OP.mult, op1=OP.add))
            drain("vector")
            tl["cs"][q] = inc("vector", lambda e, sl0=sl0: e.tensor_mul(
                A_loc[:, sl0:sl0 + TT], SCB[:], C_loc[:, sl0:sl0 + TT]), "v")
            drain("vector")

        def emit_tail_ymm(tk_pf):
            regions = [(0, 0), (0, TT), (1, 0), (1, TT)]
            tk_ymm = [None] * NTI
            tk_ye = [None, None, None, None]
            for rnd in range(2):
                w("tensor", tk_pf)
                if rnd == 1:
                    w("tensor", tk_ye[0], tk_ye[1], tk_ye[2], tk_ye[3])
                for i4 in range(4):
                    q = rnd * 4 + i4
                    b, c0 = regions[i4]
                    w("tensor", tl["cs"][q])
                    tk_ymm[q] = inc("tensor", lambda e, q=q, b=b, c0=c0: e.matmul(
                        P[b][0:1, c0:c0 + TT], onesbf[:], A_loc[:, q * TT:(q + 1) * TT],
                        start=(i4 % 2 == 0), stop=True, skip_group_check=True), "t")
                for i4 in range(4):
                    q = rnd * 4 + i4
                    b, c0 = regions[i4]
                    w("scalar", tk_ymm[q])
                    tk_ye[i4] = inc("scalar", lambda e, q=q, b=b, c0=c0: e.copy(
                        out=C_loc[0:1, q * TT:(q + 1) * TT],
                        in_=P[b][0:1, c0:c0 + TT]), "s")
        pfree = {0: tk_axs, 1: tk_ax128, 2: tk_hf, 3: tk_he,
                 4: None, 5: None, 6: None, 7: None}
        tk_store = [None] * 16
        tk_xs_free = None
        tk_xnb_free = None
        tk_xq_free = None
        tk_drb_free = None
        tk_szb_free = [None, None]
        tk_ynbf_free = [None, None]
        tk_ygb_free = [None, None, None, None]
        tk_xsq_free = [None, None]
        tk_gt_free = [None, None]
        tk_gf_free = [None, None]
        tk_sgb_free = [None, None]
        tk_sca_free = None
        tk_qab_prev = None
        tk_scan_done = [None] * NTI

        tk_xn_next = [None, None]
        for s_ in range(2):
            w("vector", tk_x_pre[s_], xnm_free[s_])
            tk_xn_next[s_] = inc("vector", lambda e, s_=s_: e.tensor_scalar(
                XN[s_][:], XT[s_][:], MN[:, s_:s_ + 1], MI[:, s_:s_ + 1],
                OP.add, OP.mult), "v")
            xt_free[s_] = tk_xn_next[s_]
        tk_x_next = [None, None]

        for ti in range(NTI):
            g0 = ti * 2
            tk_xn = list(tk_xn_next)
            tk_acc = [None, None]
            tk_tp = [None] * 4
            for s_ in range(2):
                w("tensor", tk_xn[s_])
                for half in range(2):
                    bank = 4 + s_ * 2 + half
                    w("tensor", pfree[bank])
                    for f4 in range(4):
                        f = half * 4 + f4
                        tk_tp[bank - 4] = inc("tensor", lambda e, s_=s_, f=f, bank=bank, f4=f4:
                                              e.transpose(P[bank][:, f4 * 128:(f4 + 1) * 128],
                                                          XN[s_][:, f * 128:(f + 1) * 128],
                                                          identf[:]), "t")
            def emit_sx_chain():
                w("vector", tk_g1)
                S.emit("vector", lambda e: e.tensor_reduce(
                    SC[:, 3:4], G1[:], axis=mybir.AxisListType.X, op=OP.max))
                drain("vector")
                S.emit("vector", lambda e: e.tensor_scalar_max(SC[:, 1:2], SC[:, 3:4], 1e-12))
                drain("vector")
                S.emit("vector", lambda e: e.reciprocal(SC[:, 11:12], SC[:, 1:2]))
                drain("vector")
                S.emit("vector", lambda e: e.tensor_scalar(SC[:, 12:13], SC[:, 11:12],
                                                           448.0, 448.0, OP.mult, OP.min))
                drain("vector")
                S.emit("vector", lambda e: e.tensor_scalar_mul(SC[:, 4:5], SC[:, 12:13], 0.5))
                tk = inc("vector", lambda e: e.reciprocal(SC[:, 5:6], SC[:, 12:13]), "sx")
                drain("vector")
                S.emit("vector", lambda e: e.tensor_scalar_mul(SC[:, 11:12], SC[:, 5:6], 4.0))
                drain("vector")
                tk4 = inc("vector", lambda e: e.tensor_mul(SC[:, 10:11], SC[:, 11:12],
                                                           SC[:, 9:10]), "v")
                drain("vector")
                return tk, tk4
            tk_xnb = [None, None]
            tk_xq = [None, None]
            for s_ in range(2):
                w("scalar", tk_tp[s_ * 2], tk_tp[s_ * 2 + 1], tk_xnb_free)
                for half in range(2):
                    bank = 4 + s_ * 2 + half
                    tk_xnb[s_] = inc("scalar", lambda e, s_=s_, half=half, bank=bank: e.copy(
                        out=XNB[:, s_ * DM + half * 512: s_ * DM + half * 512 + 512],
                        in_=P[bank][:]), "s")
            def emit_xq():
                for s_ in range(2):
                    w("scalar", ("sx", semsx, 1), tk_xq_free)
                    for half in range(2):
                        bank = 4 + s_ * 2 + half
                        tk_xq[s_] = inc("scalar", lambda e, s_=s_, half=half, bank=bank:
                                        e.activation(
                            out=XQ[:, s_ * DM + half * 512: s_ * DM + half * 512 + 512],
                            in_=P[bank][:], func=AF.Identity, scale=SC[:, 4:5]), "s")
                return (tk_xq[0], tk_xq[1])
            if ti > 0:
                pf47 = emit_xq()

            w("tensor", tk_xnb[0], tk_xnb[1], pfree[2], tk_wb0)
            for kc in range(KD):
                tk_b0 = inc("tensor", lambda e, kc=kc: e.matmul(
                    P[2][0:1, 0:TT], wb0[:, kc:kc + 1], xnb3[:, :, kc * 128:(kc + 1) * 128],
                    start=(kc == 0), stop=(kc == 7)), "t")
            w("scalar", tk_b0)
            tk_b0r = inc("scalar", lambda e: e.copy(out=B0R[:], in_=P[2][0:1, 0:TT]), "s")
            pfree[2] = tk_b0r

            tk_proj = [None] * EC
            tk_xpev = [None] * EC
            tk_silu = [None] * EC
            tk_xsq = [None] * EC
            tk_convd = [None] * EC
            tk_cdr = [None] * EC

            def emit_proj(e_, ti=ti):
                bank = e_ % 2
                w("tensor", pfree[bank])
                if e_ >= 2:
                    w("tensor", tk_xpev[e_ - 2])
                for kc in range(KD):
                    tk = inc("tensor", lambda e, kc=kc, e_=e_, bank=bank: e.matmul(
                        P[bank][0:128, 0:TT],
                        W1[:, kc * E + e_ * 128: kc * E + e_ * 128 + 128],
                        xnb3[:, :, kc * 128:(kc + 1) * 128],
                        start=(kc == 0), stop=(kc == 7)), "t")
                tk_proj[e_] = tk

            def emit_xpev(e_):
                hb = e_ % 2
                base = hb * 260
                w("vector", tk_proj[e_])
                if e_ >= 2:
                    w("vector", tk_convd[e_ - 2])
                if ti == 0:
                    w("vector", tk_he)
                S.emit("vector", lambda e, e_=e_, base=base: e.tensor_copy(
                    out=XPB[:, base + 1:base + 4], in_=XPT[:, e_ * 3:(e_ + 1) * 3]))
                S.emit("vector", lambda e, e_=e_, base=base, hb=hb: e.tensor_copy(
                    out=XPB[:, base + 4:base + 260], in_=P[hb][0:128, 0:TT]))
                drain("vector")
                tk_xpev[e_] = inc("vector", lambda e, e_=e_, base=base: e.tensor_copy(
                    out=XPT[:, e_ * 3:(e_ + 1) * 3], in_=XPB[:, base + 257:base + 260]), "v")
                drain("vector")

            def emit_conv(e_):
                hb = e_ % 2
                base = hb * 260
                bank = 2 + e_ % 2
                w("tensor", tk_xpev[e_], pfree[bank])
                for k in range(4):
                    j = e_ * 4 + k
                    tk = inc("tensor", lambda e, j=j, k=k, base=base, bank=bank: e.matmul(
                        P[bank][0:128, 0:TT], DG[:, j * 128:(j + 1) * 128],
                        XPB[:, base + 1 + k: base + 1 + k + TT],
                        start=(k == 0), stop=(k == 3)), "t")
                tk_convd[e_] = tk

            def emit_silu(e_, ti=ti):
                bank = 2 + e_ % 2
                w("scalar", tk_convd[e_], tk_xs_free)
                if ti == 0 and e_ == 8:
                    w("scalar", tk_hap)
                tk = inc("scalar", lambda e, e_=e_, bank=bank: e.activation(
                    out=XS[:, e_ * TT:(e_ + 1) * TT], in_=P[bank][0:128, 0:TT],
                    func=AF.Silu, bias=CB[:, e_:e_ + 1]), "s")
                if e_ == 0:
                    tk = inc("scalar", lambda e, bank=bank: e.activation(
                        out=XS0R[:], in_=P[bank][0:1, 0:TT],
                        func=AF.Silu, bias=CB[0:1, 0:1]), "s")
                tk_silu[e_] = tk
                pfree[bank] = tk

            def emit_xsq(e_):
                w("vector", tk_silu[e_])
                if e_ >= 4:
                    w("vector", tk_stats_q[(e_ - 4) // 4])
                elif ti > 0:
                    w("vector", tk_xsq_free[0])
                tk_xsq[e_] = inc("vector", lambda e, e_=e_: e.tensor_mul(
                    XSQ2[:, (e_ % 4) * TT:(e_ % 4 + 1) * TT], XS[:, e_ * TT:(e_ + 1) * TT],
                    XS[:, e_ * TT:(e_ + 1) * TT]), "v")

            def emit_stats_half(r0, r1):
                w("tensor", tk_silu[r1 - 1], tk_xsq[r1 - 1], tk_wx)
                if r0 == 0:
                    w("tensor", pf47[0], pf47[1])
                tk = None
                for e_ in range(r0, r1):
                    inc("tensor", lambda e, e_=e_: e.matmul(
                        P[6][0:1, 0:TT], Dm1[:, e_:e_ + 1], XS[:, e_ * TT:(e_ + 1) * TT],
                        start=(e_ == 0), stop=(e_ == EC - 1)), "t")
                    tk = inc("tensor", lambda e, e_=e_: e.matmul(
                        P[7][0:1, 0:TT], Dm2[:, e_:e_ + 1],
                        XSQ2[:, (e_ % 4) * TT:(e_ % 4 + 1) * TT],
                        start=(e_ == 0), stop=(e_ == EC - 1)), "t")
                    inc("tensor", lambda e, e_=e_: e.matmul(
                        P[5][0:128, 0:TT], Wxs[:, e_ * 192 + 64:(e_ + 1) * 192],
                        XS[:, e_ * TT:(e_ + 1) * TT],
                        start=(e_ == 0), stop=(e_ == EC - 1)), "t")
                    tk_cdr[e_] = inc("tensor", lambda e, e_=e_: e.matmul(
                        P[4][0:64, 0:TT], Wxs[:, e_ * 192: e_ * 192 + 64],
                        XS[:, e_ * TT:(e_ + 1) * TT],
                        start=(e_ == 0), stop=(e_ == EC - 1)), "t")
                return tk
            tk_stats_q = [None, None, None, None]
            for step in range(EC + 4):
                if ti == 0 and step == 5:
                    tk_sx, tk_s4 = emit_sx_chain()
                    pf47 = emit_xq()
                if ti == NTI - 1 and step == 10:
                    emit_tail_prefix()
                if ti < NTI - 1 and step == 8:
                    for s_ in range(2):
                        g = g0 + 2 + s_
                        sl = g % 2
                        w("sync", xt_free[sl])
                        tk_x_next[s_] = dma("sync", XT[sl][:],
                                            x_in[g * 128:(g + 1) * 128, :], "x")
                if ti < NTI - 1 and step == 12:
                    for s_ in range(2):
                        g = g0 + 2 + s_
                        sl = g % 2
                        w("vector", tk_x_next[s_], tk_tp[s_ * 2], tk_tp[s_ * 2 + 1])
                        tk_xn_next[s_] = inc("vector", lambda e, g=g, sl=sl, s_=s_:
                                             e.tensor_scalar(
                            XN[s_][:], XT[sl][:], MN[:, g:g + 1], MI[:, g:g + 1],
                            OP.add, OP.mult), "v")
                        xt_free[sl] = tk_xn_next[s_]
                if step < EC:
                    emit_proj(step)
                    emit_xpev(step)
                if 1 <= step <= EC:
                    emit_conv(step - 1)
                    emit_silu(step - 1)
                if 4 <= step <= EC + 3:
                    emit_xsq(step - 4)
                if step in (7, 11, 15):
                    tk_stats_q[(step - 7) // 4] = emit_stats_half(step - 7, step - 3)
            emit_stats_half(12, EC)
            tk_eloop = tk_cdr[EC - 1]
            tk_xsq_free[0] = tk_eloop

            w("scalar", tk_eloop, tk_drb_free)
            tk_drb = inc("scalar", lambda e: e.copy(out=DRB[:], in_=P[4][0:64, 0:TT]), "s")
            pfree[4] = tk_drb
            tk_cl = inc("scalar", lambda e, ti=ti: e.copy(
                out=C_loc[:, ti * TT:(ti + 1) * TT], in_=P[5][0:128, 0:TT]), "s")
            pfree[5] = tk_cl
            tk_m1r = inc("scalar", lambda e: e.copy(out=M1R[:], in_=P[6][0:1, 0:TT]), "s")
            tk_m2r = inc("scalar", lambda e: e.copy(out=M2R[:], in_=P[7][0:1, 0:TT]), "s")
            pfree[6] = tk_m1r
            pfree[7] = tk_m2r
            w("vector", tk_m1r, tk_m2r)
            S.emit("vector", lambda e: e.tensor_mul(RSTR[:], M1R[:], M1R[:]))
            drain("vector")
            tk_var2 = inc("vector", lambda e: e.tensor_sub(VARR[:], M2R[:], RSTR[:]), "v")
            w("tensor", tk_drb, pfree[2], tk_wdt)
            tk_dmm = inc("tensor", lambda e: e.matmul(
                P[2][0:1, 0:TT], wdt0b[:], DRB[:], start=True, stop=True), "t")
            w("scalar", tk_dmm)
            tk_dltev = inc("scalar", lambda e: e.copy(out=T0R[:], in_=P[2][0:1, 0:TT]), "s")
            pfree[2] = tk_dltev
            drain("scalar")
            S.emit("scalar", lambda e: e.activation(out=T1R[:], in_=T0R[:], func=AF.Abs,
                                                    bias=BDT))
            w("scalar", tk_b0r)
            tk_habs = inc("scalar", lambda e: e.activation(out=BM0R[:], in_=B0R[:],
                                                           func=AF.Abs), "s")
            drain("scalar")
            S.emit("scalar", lambda e: e.activation(out=DLTR[:], in_=T1R[:], func=AF.Exp,
                                                    scale=-1.0))
            S.emit("scalar", lambda e: e.activation(out=BDR[:], in_=BM0R[:], func=AF.Exp,
                                                    scale=-1.0))
            w("scalar", tk_var2)
            S.emit("scalar", lambda e: e.activation(out=M2R[:], in_=VARR[:], func=AF.Ln,
                                                    bias=cst[0:1, 0:1]))
            drain("scalar")
            tk_sp1 = inc("scalar", lambda e: e.activation(out=T1R[:], in_=DLTR[:],
                                                          func=AF.Ln, bias=1.0), "s")
            tk_sp2 = inc("scalar", lambda e: e.activation(out=BM0R[:], in_=BDR[:],
                                                          func=AF.Ln, bias=1.0), "s")
            tk_rst = inc("scalar", lambda e: e.activation(out=RSTR[:], in_=M2R[:], func=AF.Exp,
                                                          scale=cst[0:1, 1:2]), "s")
            drain("scalar")

            tk_zmm = [None] * EC
            tk_sz = [None] * EC
            tk_yg = [None] * EC
            tk_wout = [None] * EC
            tk_m1b = tk_rsb = None
            tk_dltev = tk_dl1 = tk_bd = tk_aev = tk_bev = None
            tk_ynf1_free = [None]

            def emit_z(e_):
                bank = e_ % 2
                if e_ < 2:
                    w("tensor", tk_xpev[EC - 2 + e_], pfree[e_ % 2])
                else:
                    w("tensor", tk_sz[e_ - 2])
                w("tensor", tk_w2)
                for kc in range(KD):
                    tk = inc("tensor", lambda e, kc=kc, e_=e_, bank=bank: e.matmul(
                        P[bank][0:128, 0:TT],
                        W2[:, kc * E + e_ * 128: kc * E + e_ * 128 + 128],
                        xnb3[:, :, kc * 128:(kc + 1) * 128],
                        start=(kc == 0), stop=(kc == 7)), "t")
                tk_zmm[e_] = tk

            def emit_ychain(e_):
                bank = e_ % 2
                sl = e_ % 2
                gsl = e_ % 4
                w("scalar", tk_zmm[e_], tk_szb_free[sl])
                S.emit("scalar", lambda e, bank=bank: e.activation(
                    out=TNH[:], in_=P[bank][0:128, 0:TT], func=AF.Tanh,
                    scale=cst[:, 2:3]))
                drain("scalar")
                tk_sz[e_] = inc("scalar", lambda e, sl=sl: e.activation(
                    out=SZB[:, sl * TT:(sl + 1) * TT], in_=TNH[:],
                    func=AF.Identity, scale=cst[:, 2:3], bias=cst[:, 2:3]), "s")
                drain("scalar")
                w("vector", tk_m1b, tk_rsb, tk_eloop, tk_ln)
                S.emit("vector", lambda e, e_=e_: e.scalar_tensor_tensor(
                    out=YNF[0][:], in0=XS[:, e_ * TT:(e_ + 1) * TT], scalar=DD[:, e_:e_ + 1],
                    in1=M1B[:], op0=OP.mult, op1=OP.subtract))
                drain("vector")
                w("vector", tk_ynf1_free[0])
                tk_stt2 = inc("vector", lambda e, e_=e_: e.scalar_tensor_tensor(
                    out=YNF[1][:], in0=YNF[0][:], scalar=1.0,
                    in1=RSB[:], op0=OP.mult, op1=OP.mult), "v")
                drain("vector")
                w("scalar", tk_stt2, tk_ynbf_free[sl])
                tk_yn = inc("scalar", lambda e, e_=e_, sl=sl: e.activation(
                    out=YNBF[:, sl * TT:(sl + 1) * TT], in_=YNF[1][:], func=AF.Identity,
                    scale=LNG[:, e_:e_ + 1], bias=LNB[:, e_:e_ + 1]), "s")
                w("vector", tk_sz[e_], tk_yn, tk_ygb_free[gsl])
                tk_yg[e_] = inc("vector", lambda e, sl=sl, gsl=gsl: e.tensor_mul(
                    YGB[:, gsl * TT:(gsl + 1) * TT], YNBF[:, sl * TT:(sl + 1) * TT],
                    SZB[:, sl * TT:(sl + 1) * TT]), "v")
                tk_ynbf_free[sl] = tk_yg[e_]
                tk_szb_free[sl] = tk_yg[e_]
                tk_ynf1_free[0] = tk_yn

            def emit_wout(e_):
                gsl = e_ % 4
                w("tensor", tk_yg[e_], tk_w3)
                for d_ in range(KD):
                    bank = 4 + d_ // 2
                    col = (d_ % 2) * TT
                    st = (e_ == 0 and d_ % 2 == 0)
                    tk = inc("tensor", lambda e, e_=e_, d_=d_, bank=bank, col=col, st=st, gsl=gsl:
                             e.matmul(P[bank][0:128, col:col + TT],
                                      W3[:, e_ * DM + d_ * 128: e_ * DM + d_ * 128 + 128],
                                      YGB[:, gsl * TT:(gsl + 1) * TT],
                                      start=st, stop=(e_ == EC - 1),
                                      skip_group_check=True), "t")
                tk_wout[e_] = tk
                tk_ygb_free[gsl] = tk

            def emit_scan_level(li, ti=ti):
                sl0 = ti * TT
                d = 1 << li
                if li == 0:
                    w("vector", tk_aev, tk_bev)
                    drain("vector")
                S.emit("vector", lambda e, sl0=sl0: e.tensor_copy(SCA[:], A_loc[:, sl0:sl0 + TT]))
                S.emit("vector", lambda e, sl0=sl0: e.tensor_copy(SCB[:], B_loc[:, sl0:sl0 + TT]))
                drain("vector")
                S.emit("vector", lambda e, sl0=sl0, d=d: e.tensor_mul(
                    A_loc[:, sl0 + d:sl0 + TT], SCA[:, d:TT], SCA[:, 0:TT - d]))
                S.emit("vector", lambda e, d=d: e.tensor_mul(
                    SCC[:, 0:TT - d], SCA[:, d:TT], SCB[:, 0:TT - d]))
                drain("vector")
                S.emit("vector", lambda e, sl0=sl0, d=d: e.tensor_add(
                    B_loc[:, sl0 + d:sl0 + TT], SCB[:, d:TT], SCC[:, 0:TT - d]))
                drain("vector")
                if li == 7:
                    tk_scan_done[ti] = inc("vector", lambda e, sl0=sl0: e.tensor_copy(
                        SCA[:, 0:1], A_loc[:, sl0:sl0 + 1]), "v")

            for step in range(EC + 3):
                if 2 <= step < EC + 2:
                    emit_ychain(step - 2)
                if 11 <= step < 19:
                    emit_scan_level(step - 11)
                if step < EC:
                    emit_z(step)
                if step == 0:
                    w("tensor", tk_m1r, pfree[2])
                    tk_m1bc = inc("tensor", lambda e: e.matmul(
                        P[2][0:128, 0:TT], onesr[:], M1R[:], start=True, stop=True), "t")
                    w("vector", tk_m1bc)
                    tk_m1b = inc("vector", lambda e: e.tensor_copy(M1B[:], P[2][0:128, 0:TT]), "v")
                    pfree[2] = tk_m1b
                if step == 1:
                    w("tensor", tk_rst, pfree[3])
                    tk_rsbc = inc("tensor", lambda e: e.matmul(
                        P[3][0:128, 0:TT], onesr[:], RSTR[:], start=True, stop=True), "t")
                    w("vector", tk_rsbc)
                    tk_rsb = inc("vector", lambda e: e.tensor_copy(RSB[:], P[3][0:128, 0:TT]), "v")
                    pfree[3] = tk_rsb
                if step == 5:
                    w("vector", tk_sp1)
                    S.emit("vector", lambda e: e.tensor_scalar(DLTR[:], T0R[:], BDT,
                                                               0.0, OP.add, OP.max))
                    drain("vector")
                    S.emit("vector", lambda e: e.tensor_add(T0R[:], DLTR[:], T1R[:]))
                    drain("vector")
                    tk_dl1 = inc("vector", lambda e: e.tensor_scalar(DLTR[:], T0R[:], 1e-6,
                                                                     100.0, OP.max, OP.min), "v")
                    drain("vector")
                if step == 9:
                    w("vector", tk_sp2, tk_dl1)
                    S.emit("vector", lambda e: e.tensor_scalar_max(BDR[:], B0R[:], 0.0))
                    drain("vector")
                    S.emit("vector", lambda e: e.tensor_add(T0R[:], BDR[:], BM0R[:]))
                    drain("vector")
                    w("vector", tk_silu[0])
                    S.emit("vector", lambda e: e.tensor_mul(T1R[:], DLTR[:], T0R[:]))
                    drain("vector")
                    tk_bd = inc("vector", lambda e: e.tensor_mul(BDR[:], T1R[:], XS0R[:]), "v")
                    drain("vector")
                if step == 8:
                    w("tensor", tk_dl1, pfree[2])
                    tk_abc = inc("tensor", lambda e: e.matmul(P[2][0:128, 0:TT], onesrb[:],
                                                              DLTR[:], start=True, stop=True), "t")
                    w("scalar", tk_abc, tk_a0)
                    tk_aev = inc("scalar", lambda e, ti=ti: e.activation(
                        out=A_loc[:, ti * TT:(ti + 1) * TT], in_=P[2][0:128, 0:TT],
                        func=AF.Exp, scale=A0c[:]), "s")
                    pfree[2] = tk_aev
                if step == 10:
                    w("tensor", tk_bd, pfree[3])
                    tk_bbc = inc("tensor", lambda e: e.matmul(P[3][0:128, 0:TT], onesrb[:],
                                                              BDR[:], start=True, stop=True), "t")
                    w("scalar", tk_bbc)
                    tk_bev = inc("scalar", lambda e, ti=ti: e.copy(
                        out=B_loc[:, ti * TT:(ti + 1) * TT], in_=P[3][0:128, 0:TT]), "s")
                    pfree[3] = tk_bev
                if step >= 3 and step - 3 < EC:
                    emit_wout(step - 3)
            tk_youts = tk_wout[EC - 1]
            tk_xs_free = tk_yg[EC - 1]



            tk_fmm = [None] * KD
            tk_epi = [None] * KD
            tk_gfo = [None] * KD
            tk_sg = [None] * KD
            tk_tps0 = {2: None, 3: None}
            tk_tps1 = {4: None, 5: None}
            tk_c1s = [None, None]
            tk_adds0 = [None, None]
            tk_adds1 = [None, None]

            def emit_ffn(o):
                bank = o % 2
                w("tensor", tk_xq[0], tk_xq[1], tk_q)
                if o < 2:
                    w("tensor", tk_sz[EC - 2 + o])
                else:
                    w("tensor", tk_epi[o - 2])
                for kc in range(KD):
                    tk = inc("tensor", lambda e, kc=kc, o=o, bank=bank: e.matmul(
                        P[bank][0:128, 0:TT],
                        Wq[:, kc * DM + o * 128: kc * DM + o * 128 + 128],
                        xq3[:, :, kc * 128:(kc + 1) * 128],
                        start=(kc == 0), stop=(kc == 7)), "t")
                tk_fmm[o] = tk

            def emit_epi(o):
                bank = o % 2
                sl = o % 2
                w("vector", tk_fmm[o], tk_s4, tk_gt_free[sl])
                tk_epi[o] = inc("vector", lambda e, o=o, bank=bank, sl=sl: e.tensor_scalar(
                    GT[:, sl * TT:(sl + 1) * TT], P[bank][0:128, 0:TT],
                    SC[:, 10:11], FFB[:, o:o + 1], OP.mult, OP.add), "v")
                w("scalar", tk_epi[o], tk_gf_free[sl])
                tk_gfo[o] = inc("scalar", lambda e, sl=sl: e.activation(
                    out=GF[:, sl * TT:(sl + 1) * TT], in_=GT[:, sl * TT:(sl + 1) * TT],
                    func=AF.Silu), "s")
                sbank = 4 + o // 2
                scol = (o % 2) * TT
                w("vector", tk_gfo[o], tk_youts, tk_sgb_free[sl])
                tk_sg[o] = inc("vector", lambda e, sl=sl, sbank=sbank, scol=scol: e.tensor_mul(
                    SGB[:, sl * TT:(sl + 1) * TT], GF[:, sl * TT:(sl + 1) * TT],
                    P[sbank][0:128, scol:scol + TT]), "v")
                tk_gt_free[sl] = tk_gfo[o]
                tk_gf_free[sl] = tk_sg[o]

            def emit_sgt(o):
                sl = o % 2
                b0_ = 2 if o < 4 else 3
                w("tensor", tk_sg[o])
                if o == 0:
                    w("tensor", pfree[2])
                if o == 4:
                    w("tensor", pfree[3])
                tk_tps0[b0_] = inc("tensor", lambda e, o=o, sl=sl, b0_=b0_: e.transpose(
                    P[b0_][:, (o % 4) * 128:(o % 4 + 1) * 128],
                    SGB[:, sl * TT: sl * TT + 128], identf[:]), "t")
                if o < 2:
                    w("vector", tk_tps0[b0_])
                    tk_c1 = inc("vector", lambda e, o=o, sl=sl: e.tensor_copy(
                        SG1[:, o * 128:(o + 1) * 128],
                        SGB[:, sl * TT + 128: sl * TT + 256]), "v")
                    tk_c1s[o] = tk_c1
                    tk_sgb_free[sl] = (tk_tps0[b0_], tk_c1)
                else:
                    b1_ = 4 if o < 4 else 5
                    if o == 2:
                        w("tensor", tk_sg[1], tk_c1s[0], tk_c1s[1])
                        for oo in range(2):
                            inc("tensor", lambda e, oo=oo: e.transpose(
                                P[4][:, oo * 128:(oo + 1) * 128],
                                SG1[:, oo * 128:(oo + 1) * 128], identf[:]), "t")
                    if o == 4:
                        w("tensor", tk_sg[3])
                    tk_tps1[b1_] = inc("tensor", lambda e, o=o, sl=sl, b1_=b1_: e.transpose(
                        P[b1_][:, (o % 4) * 128:(o % 4 + 1) * 128],
                        SGB[:, sl * TT + 128: sl * TT + 256], identf[:]), "t")
                    tk_sgb_free[sl] = tk_tps1[b1_]

            for step in range(KD + 2):
                if ti == NTI - 1:
                    if step == 0:
                        emit_tail_cc()
                    elif step == 6:
                        emit_tail_schain()
                    elif step == 7:
                        emit_tail_q(0)
                        emit_tail_q(1)
                    elif step == 8:
                        emit_tail_q(2)
                        emit_tail_q(3)
                    elif step == 9:
                        for q_ in range(4, NTI):
                            emit_tail_q(q_)
                if step < KD:
                    emit_ffn(step)
                if 1 <= step <= KD:
                    emit_epi(step - 1)
                if step >= 2:
                    emit_sgt(step - 2)
                if ti < NTI - 1:
                    if step - 2 == 3:
                        w("scalar", tk_tps0[2], tk_qab_prev)
                        tk_adds0[0] = inc("scalar", lambda e: e.copy(
                            out=QA[:, 0:512], in_=P[2][:, 0:512]), "s")
                        pfree[2] = tk_adds0[0]
                    if step - 2 == 7:
                        w("scalar", tk_tps0[3])
                        tk_adds0[1] = inc("scalar", lambda e: e.copy(
                            out=QA[:, 512:1024], in_=P[3][:, 0:512]), "s")
                        pfree[3] = tk_adds0[1]
                        w("scalar", tk_tps1[4])
                        tk_adds1[0] = inc("scalar", lambda e: e.copy(
                            out=QB[:, 0:512], in_=P[4][:, 0:512]), "s")
                        pfree[4] = tk_adds1[0]
                        w("scalar", tk_tps1[5])
                        tk_adds1[1] = inc("scalar", lambda e: e.copy(
                            out=QB[:, 512:1024], in_=P[5][:, 0:512]), "s")
                        pfree[5] = tk_adds1[1]
                        w("gpsimd", tk_adds0[0], tk_adds0[1])
                        tk_acc[0] = dma("gpsimd", QA[:],
                                        x_in[g0 * 128:(g0 + 1) * 128, :], "w",
                                        accum=OP.add)
                        w("gpsimd", tk_adds1[0], tk_adds1[1])
                        tk_acc[1] = dma("gpsimd", QB[:],
                                        x_in[(g0 + 1) * 128:(g0 + 2) * 128, :], "w",
                                        accum=OP.add)
                else:
                    if step - 2 == 3:
                        w("vector", tk_tps0[2], tk_qab_prev)
                        tk_adds0[0] = inc("vector", lambda e, g0=g0: e.tensor_add(
                            QA[:, 0:512], P[2][:, 0:512], XT[g0 % 2][:, 0:512]), "v")
                        pfree[2] = tk_adds0[0]
                    if step - 2 == 7:
                        w("vector", tk_tps0[3])
                        tk_adds0[1] = inc("vector", lambda e, g0=g0: e.tensor_add(
                            QA[:, 512:1024], P[3][:, 0:512], XT[g0 % 2][:, 512:1024]), "v")
                        pfree[3] = tk_adds0[1]
                        w("vector", tk_tps1[4], tk_qab_prev)
                        tk_adds1[0] = inc("vector", lambda e, g0=g0: e.tensor_add(
                            QB[:, 0:512], P[4][:, 0:512], XT[(g0 + 1) % 2][:, 0:512]), "v")
                        pfree[4] = tk_adds1[0]
                        w("vector", tk_tps1[5])
                        tk_adds1[1] = inc("vector", lambda e, g0=g0: e.tensor_add(
                            QB[:, 512:1024], P[5][:, 0:512], XT[(g0 + 1) % 2][:, 512:1024]), "v")
                        pfree[5] = tk_adds1[1]
                        tk_acc[0] = (tk_adds0[0], tk_adds0[1])
                        tk_acc[1] = (tk_adds1[0], tk_adds1[1])
            for s_ in range(2):
                g = g0 + s_
                w("sync", tk_acc[s_])
                tk_store[g] = dma("sync", y_out[g * 128:(g + 1) * 128, :],
                                  (QA if s_ == 0 else QB)[:], "o")
            if ti == NTI - 1:
                xt_free[0] = (tk_adds0[0], tk_adds0[1])
                xt_free[1] = (tk_adds1[0], tk_adds1[1])
            tk_qab_prev = (tk_store[g0], tk_store[g0 + 1],
                           tk_acc[0], tk_acc[1])
            tk_xnb_free = tk_zmm[EC - 1]
            tk_xq_free = tk_fmm[KD - 1]
            tk_drb_free = tk_dmm
            pfree[0] = tk_epi[KD - 2]
            pfree[1] = tk_epi[KD - 1]

        emit_tail_ymm((tk_epi[KD - 2], tk_epi[KD - 1]))

        for g in range(16):
            w("sync", tk_store[g])

        block = ctx.enter_context(nc.Block())
        S.run(block)
    return nc


def kernel(**inputs):
    x = np.asarray(inputs["L0_attn_x"])[0]
    Win = np.asarray(inputs["L0_attn_Win"])
    conv_w = np.asarray(inputs["L0_attn_conv_w"])
    conv_b = np.asarray(inputs["L0_attn_conv_b"])
    Wx = np.asarray(inputs["L0_attn_Wx"])
    Wdt = np.asarray(inputs["L0_attn_Wdt"])
    bdt = np.asarray(inputs["L0_attn_bdt"])
    A_log = np.asarray(inputs["L0_attn_A_log"])
    D = np.asarray(inputs["L0_attn_D"])
    ln_g = np.asarray(inputs["L0_attn_ln_g"])
    ln_b = np.asarray(inputs["L0_attn_ln_b"])
    Wout = np.asarray(inputs["L0_attn_Wout"])
    res_scale = np.asarray(inputs["L0_attn_res_scale"])
    Wf = np.asarray(inputs["L0_ffn_weight"])
    ffb = np.asarray(inputs["L0_ffn_bias"])

    C = np.ascontiguousarray
    w_xp = C(Win[:, :E])
    w_z = C(Win[:, E:2 * E])
    w_b0 = C(Win[:, 2 * E].reshape(KD, 128).T)
    w_ft = C(Wf.T)
    wdt0 = C(Wdt[:, 0:1])
    alog0 = C(A_log[0].reshape(H, 1))
    cw = C(conv_w.reshape(EC, 128, 4).transpose(1, 0, 2).reshape(128, EC * 4))
    col = lambda v: C(v.reshape(EC, 128).T)
    ffbc = C(ffb.reshape(KD, 128).T)
    rs128 = C(np.broadcast_to(res_scale.reshape(1, 1), (128, 1)))
    bdt0 = C(bdt[0].reshape(1, 1))

    in_maps = []
    for i in range(NCORE):
        t0 = i * T
        xh = x[t0 - 3:t0] if i > 0 else np.zeros((3, DM), np.float32)
        sel = np.zeros((128, NCORE), np.float32)
        sel[:, i] = 1.0
        m8 = np.zeros((128, NCORE), np.float32)
        m8[:, i] = 1.0
        m16 = np.zeros((128, 2 * NCORE), np.float32)
        m16[:, 2 * i:2 * i + 2] = 1.0
        cst_all = np.concatenate([
            cw, col(conv_b), col(D), col(ln_g), col(ln_b), ffbc, rs128,
            sel, m8, m16, np.broadcast_to(bdt0, (128, 1)), alog0], axis=1)
        in_maps.append({
            "x": C(x[t0:t0 + T]), "xh": C(xh),
            "w_xp": w_xp, "w_z": w_z, "w_b0": w_b0, "w_x": Wx, "w_dt0": wdt0,
            "w_out": Wout, "w_ft": w_ft,
            "cst_all": C(cst_all.astype(np.float32)),
        })
    nc = build_nc()
    res = run_bass_kernel_spmd(nc, in_maps, core_ids=list(range(NCORE)))
    out = np.concatenate([r["y"] for r in res.results], axis=0)
    return out.reshape(1, NCORE * T, DM)


# version 27
# speedup vs baseline: 1.0716x; 1.0251x over previous
import numpy as np
import concourse.bass as bass
import concourse.mybir as mybir
from concourse.bass_utils import run_bass_kernel_spmd
from contextlib import ExitStack

F32 = mybir.dt.float32
BF16 = mybir.dt.bfloat16
F8 = mybir.dt.float8e4
AF = mybir.ActivationFunctionType
OP = mybir.AluOpType

NCORE = 8
T = 2048
DM = 1024
E = 2048
EC = 16
KD = 8
TT = 256
NTI = T // TT
H = 128
ENGS = ("sync", "gpsimd", "tensor", "vector", "scalar")


class Sched:
    def __init__(self):
        self.st = {e: [] for e in ENGS}
        self.cnt = {}
        self.lastw = {}

    def inc(self, eng, fn, sem, semname, n=1):
        v = self.cnt.get(semname, 0) + n
        self.cnt[semname] = v
        self.st[eng].append(lambda e, fn=fn, sem=sem, n=n: fn(e).then_inc(sem, n))
        return (semname, sem, v)

    def emit(self, eng, fn):
        self.st[eng].append(fn)

    def wait(self, eng, ticket):
        if ticket is None:
            return
        if isinstance(ticket, (tuple, list)) and ticket and isinstance(ticket[0], (tuple, list, type(None))):
            for t in ticket:
                self.wait(eng, t)
            return
        semname, sem, v = ticket
        key = (eng, semname)
        if self.lastw.get(key, -1) >= v:
            return
        self.lastw[key] = v
        self.st[eng].append(lambda e, sem=sem, v=v: e.wait_ge(sem, v))

    def run(self, block):
        for name in ENGS:
            def body(e, name=name):
                for fn in self.st[name]:
                    fn(e)
            getattr(block, name)(body)


def build_nc():
    nc = bass.Bass("TRN2", target_bir_lowering=False)
    dt = nc.dram_tensor
    x_in = dt("x", [T, DM], F32, kind="ExternalInput").ap()
    xh_in = dt("xh", [3, DM], F32, kind="ExternalInput").ap()
    w_xp = dt("w_xp", [DM, E], F32, kind="ExternalInput").ap()
    w_z = dt("w_z", [DM, E], F32, kind="ExternalInput").ap()
    w_b0 = dt("w_b0", [128, KD], F32, kind="ExternalInput").ap()
    w_x = dt("w_x", [E, 192], F32, kind="ExternalInput").ap()
    w_dt0 = dt("w_dt0", [64, 1], F32, kind="ExternalInput").ap()
    w_out = dt("w_out", [E, DM], F32, kind="ExternalInput").ap()
    w_ft = dt("w_ft", [DM, DM], F32, kind="ExternalInput").ap()
    cst_in = dt("cst_all", [128, 171], F32, kind="ExternalInput").ap()
    y_out = dt("y", [T, DM], F32, kind="ExternalOutput").ap()
    cc1i = dt("cc1i", [128, NCORE], F32)
    cc1o = dt("cc1o", [128, NCORE], F32)
    cc2i = dt("cc2i", [128, 2 * NCORE], F32)
    cc2o = dt("cc2o", [128, 2 * NCORE], F32)
    identf_d = nc.inline_tensor(np.eye(128, dtype=np.float32), "identf_dram")

    ctx = ExitStack()
    with ctx:
        sb = lambda n, s, d: ctx.enter_context(nc.sbuf_tensor(n, s, d))
        pst = lambda n: ctx.enter_context(nc.psum_tensor(n, [128, 512], F32))
        mksem = lambda n: ctx.enter_context(nc.semaphore(n))

        W1 = sb("W1", [128, KD * E], BF16)
        W2 = sb("W2", [128, KD * E], BF16)
        W3 = sb("W3", [128, EC * DM], BF16)
        Wq = sb("Wq", [128, KD * DM], F8)
        Wxs = sb("Wxs", [128, EC * 192], BF16)
        DG = sb("DG", [128, EC * 4 * 128], BF16)
        wb0 = sb("wb0", [128, KD], BF16)
        wdt0b = sb("wdt0b", [64, 1], BF16)
        identf = sb("identf", [128, 128], F32)
        identb = sb("identb", [128, 128], BF16)
        onesr = sb("onesr", [1, 128], F32)
        onesbf = sb("onesbf", [128, 1], BF16)
        onesrb = sb("onesrb", [1, 128], BF16)
        cst = sb("cst", [128, 4], F32)
        XS = sb("XS", [128, EC * TT], BF16)
        XSQ2 = sb("XSQ2", [128, 4 * TT], BF16)
        XT = [sb(f"XT{i}", [128, DM], F32) for i in range(2)]
        XN = [sb(f"XN{i}", [128, DM], F32) for i in range(2)]
        XNB = sb("XNB", [128, 2 * DM], BF16)
        XQ = sb("XQ", [128, 2 * DM], F8)
        QA = sb("QA", [128, DM], F32)
        QB = sb("QB", [128, DM], F32)
        XPB = sb("XPB", [128, 2 * 260], BF16)
        XPT = sb("XPT", [128, EC * 3], BF16)
        A_loc = sb("A_loc", [128, T], BF16)
        B_loc = sb("B_loc", [128, T], BF16)
        C_loc = sb("C_loc", [128, T], BF16)
        SCA = sb("SCA", [128, TT], BF16)
        SCB = sb("SCB", [128, TT], BF16)
        SCC = sb("SCC", [128, TT], BF16)
        DRB = sb("DRB", [64, TT], BF16)
        M1B = sb("M1B", [128, TT], F32)
        RSB = sb("RSB", [128, TT], F32)
        SZB = sb("SZB", [128, 2 * TT], BF16)
        TNH = sb("TNH", [128, TT], F32)
        YNF = [sb(f"YNF{i}", [128, TT], F32) for i in range(2)]
        YNBF = sb("YNBF", [128, 2 * TT], BF16)
        YGB = sb("YGB", [128, 4 * TT], BF16)
        GT = sb("GT", [128, 2 * TT], BF16)
        GF = sb("GF", [128, 2 * TT], F32)
        SGB = sb("SGB", [128, 2 * TT], F32)
        SG1 = sb("SG1", [128, 256], F32)
        HFM = sb("HFM", [128, KD * 3], BF16)
        MI = sb("MI", [128, 16], F32)
        MN = sb("MN", [128, 16], F32)
        MNI = sb("MNI", [128, 16], F32)
        A0c = sb("A0c", [128, 1], F32)
        Dm1 = sb("Dm1", [128, EC], BF16)
        Dm2 = sb("Dm2", [128, EC], BF16)
        CONSTS = sb("CONSTS", [128, 171], F32)
        CW = CONSTS.ap()[:, 0:64]
        CB = CONSTS.ap()[:, 64:80]
        DD = CONSTS.ap()[:, 80:96]
        LNG = CONSTS.ap()[:, 96:112]
        LNB = CONSTS.ap()[:, 112:128]
        FFB = CONSTS.ap()[:, 128:136]
        RS1 = CONSTS.ap()[:, 136:137]
        SEL = CONSTS.ap()[:, 137:145]
        M8 = CONSTS.ap()[:, 145:153]
        M16 = CONSTS.ap()[:, 153:169]
        BDT = CONSTS.ap()[0:1, 169:170]
        ALG = CONSTS.ap()[:, 170:171]
        SC = sb("SC", [128, 16], F32)
        LL = sb("LL", [128, 9], F32)
        PP = sb("PP", [128, 9], F32)
        SS = sb("SS", [128, 9], F32)
        SST = sb("SST", [128, NTI], F32)
        SIN = sb("SIN", [128, 1], F32)
        EQF = sb("EQF", [128, 2 * NTI], F32)
        SELT = sb("SELT", [128, NCORE], F32)
        PAY1 = sb("PAY1", [128, NCORE], F32)
        PAY2 = sb("PAY2", [128, 2 * NCORE], F32)
        PAY2b = sb("PAY2b", [128, 2 * NCORE], F32)
        G1 = sb("G1", [128, NCORE], F32)
        G2 = sb("G2", [128, 2 * NCORE], F32)
        HCOL = sb("HCOL", [3, 4], F32)
        B0R = sb("B0R", [1, TT], BF16)
        XS0R = sb("XS0R", [1, TT], BF16)
        T0R = sb("T0R", [1, TT], BF16)
        T1R = sb("T1R", [1, TT], BF16)
        DLTR = sb("DLTR", [1, TT], BF16)
        BM0R = sb("BM0R", [1, TT], BF16)
        BDR = sb("BDR", [1, TT], BF16)
        M1R = sb("M1R", [1, TT], F32)
        M2R = sb("M2R", [1, TT], F32)
        VARR = sb("VARR", [1, TT], F32)
        RSTR = sb("RSTR", [1, TT], F32)

        P = [pst(f"p{i}") for i in range(8)]

        semx = mksem("semx")
        semw = mksem("semw")
        semt = mksem("semt")
        semv = mksem("semv")
        sems = mksem("sems")
        semc = mksem("semc")
        semo = mksem("semo")
        semx2 = mksem("semx2")
        semw2 = mksem("semw2")
        semo2 = mksem("semo2")
        semg = mksem("semg")
        sema = mksem("sema")
        semb = mksem("semb")
        semh = mksem("semh")
        semq = mksem("semq")
        semsx = mksem("semsx")
        S = Sched()
        SEM = {"x": semx, "w": semw, "t": semt, "v": semv, "s": sems,
               "c": semc, "o": semo, "x2": semx2, "w2": semw2, "o2": semo2,
               "g": semg, "a": sema, "b": semb, "h": semh, "q": semq,
               "sx": semsx}
        _rot = {"x": ["x", "x2"], "w": ["w", "w2"], "o": ["o", "o2"]}
        _ri = {"x": 0, "w": 0, "o": 0}

        def inc(eng, fn, s, n=1):
            return S.inc(eng, fn, SEM[s], s, n)

        def w(eng, *tickets):
            for tk in tickets:
                S.wait(eng, tk)

        def dma(eng, out, in_, s, accum=None):
            name = _rot[s][_ri[s]]
            _ri[s] ^= 1
            prior = S.cnt.get(name, 0)
            if prior:
                w(eng, (name, SEM[name], prior))
            if accum is not None:
                return inc(eng, lambda e, o=out, i=in_, a=accum: e.dma_start(
                    out=o, in_=i, accum_op=a), name, 16)
            return inc(eng, lambda e, o=out, i=in_: e.dma_start(out=o, in_=i), name, 16)

        def drain(eng):
            S.emit(eng, lambda e: e.drain())

        xnb3 = XNB.ap().rearrange("p (s c) -> p s c", c=DM)
        xq3 = XQ.ap().rearrange("p (s c) -> p s c", c=DM)

        tk_idf = dma("sync", identf[:], identf_d.ap(), "x")
        tk_idb = dma("gpsimd", identb[:], identf_d.ap(), "w")
        tk_hx = dma("gpsimd", XS[0:3, 8 * TT:8 * TT + DM], xh_in, "w")
        tk_small = dma("sync", CONSTS[:], cst_in, "x")
        tk_w1 = dma("gpsimd", W1.ap().rearrange("p (kc c) -> p kc c", c=E),
                    w_xp.rearrange("(kc p) c -> p kc c", p=128), "w")
        tk_wfl = [None] * KD
        for g_ in range(KD):
            if g_ == 1:
                w("gpsimd", ("h", semh, KD))
            if g_ >= 2:
                w("gpsimd", ("a", sema, g_ - 1))
            tk_wfl[g_] = dma("gpsimd", XN[g_ % 2][:],
                             w_ft[g_ * 128:(g_ + 1) * 128, :], "w")
        tk_wx = dma("gpsimd", Wxs.ap().rearrange("p (ec c) -> p ec c", c=192),
                    w_x.rearrange("(ec p) c -> p ec c", p=128), "w")
        tk_wb0 = dma("gpsimd", wb0[:], w_b0, "w")
        tk_wdt = dma("gpsimd", wdt0b[:], w_dt0, "w")
        qslot = [GF.ap()[:, 0:256], GF.ap()[:, 256:512],
                 SGB.ap()[:, 0:256], SGB.ap()[:, 256:512],
                 M1B.ap(), RSB.ap(), TNH.ap(), YNF[0].ap()]
        tk_wql = [None] * 32
        for qt in range(32):
            if qt >= 8:
                w("gpsimd", ("q", semq, qt - 7))
            kc, q4 = qt // 4, qt % 4
            tk_wql[qt] = dma("gpsimd", qslot[qt % 8],
                             w_ft[kc * 128:(kc + 1) * 128,
                                  q4 * 256:(q4 + 1) * 256], "w")
        tk_w2 = dma("gpsimd", W2.ap().rearrange("p (kc c) -> p kc c", c=E),
                    w_z.rearrange("(kc p) c -> p kc c", p=128), "w")
        tk_w3 = dma("gpsimd", W3.ap().rearrange("p (ec c) -> p ec c", c=DM),
                    w_out.rearrange("(ec p) c -> p ec c", p=128), "w")

        S.emit("vector", lambda e: e.memset(cst[:, 0:1], 1e-5))
        S.emit("vector", lambda e: e.memset(cst[:, 1:2], -0.5))
        S.emit("vector", lambda e: e.memset(cst[:, 2:3], 0.5))
        S.emit("vector", lambda e: e.memset(onesbf[:], 1.0))
        S.emit("vector", lambda e: e.memset(onesr[:], 1.0))
        S.emit("vector", lambda e: e.memset(onesrb[:], 1.0))
        S.emit("vector", lambda e: e.memset(XPT[:], 0.0))
        S.emit("vector", lambda e: e.memset(SC[:, 0:1], 0.0))
        S.emit("vector", lambda e: e.memset(SC[:, 6:7], 0.0))
        tk_cst = inc("vector", lambda e: e.memset(XPB[:], 0.0), "v")
        w("scalar", tk_small)
        S.emit("scalar", lambda e: e.activation(out=SC[:, 11:12], in_=ALG, func=AF.Exp))
        drain("scalar")
        tk_a0 = inc("scalar", lambda e: e.mul(A0c[:], SC[:, 11:12], -1.0), "s")
        w("vector", tk_small)
        S.emit("vector", lambda e: e.tensor_scalar_mul(Dm1[:], DD, 1.0 / 2048.0))
        S.emit("vector", lambda e: e.tensor_mul(QA[:, 0:EC], DD, DD))
        drain("vector")
        S.emit("vector", lambda e: e.tensor_scalar_mul(Dm2[:], QA[:, 0:EC], 1.0 / 2048.0))
        drain("vector")
        S.emit("vector", lambda e: e.tensor_scalar(QA[:, 0:EC], LNG, RS1, None, OP.mult))
        S.emit("vector", lambda e: e.tensor_scalar(QA[:, EC:2 * EC], LNB, RS1, None, OP.mult))
        drain("vector")
        S.emit("vector", lambda e: e.tensor_copy(LNG, QA[:, 0:EC]))
        tk_ln = inc("vector", lambda e: e.tensor_copy(LNB, QA[:, EC:2 * EC]), "v")
        drain("vector")
        w("scalar", tk_idb)
        dgj = [0]

        def emit_dg(n):
            tk = None
            for _ in range(n):
                j = dgj[0]
                if j >= EC * 4:
                    return None
                dgj[0] += 1
                tk = inc("scalar", lambda e, j=j: e.activation(
                    out=DG[:, j * 128:(j + 1) * 128], in_=identb[:], func=AF.Copy,
                    scale=CW[:, j:j + 1]), "s")
            return tk

        hx0, hs0 = 8 * TT, 8 * TT + DM
        w("scalar", tk_hx)
        tk_hsq = inc("scalar", lambda e: e.square(XS[0:3, hs0:hs0 + DM],
                                                  XS[0:3, hx0:hx0 + DM]), "s")

        xt_free = [None, None]
        w("vector", tk_a0)
        tk_ax = tk_sw = tk_q = tk_aw = tk_aw128 = None
        tk_sum_prev = None
        xnm_free_w = [None, None]
        for g in range(16):
            sl = g % 2
            w("sync", xt_free[sl])
            tk_xt = dma("sync", XT[sl][:], x_in[g * 128:(g + 1) * 128, :], "x")
            w("scalar", tk_xt, tk_sum_prev, tk_ln)
            tk_sq = inc("scalar", lambda e, sl=sl: e.square(QA[:], XT[sl][:]), "s")
            w("vector", tk_xt, tk_cst)
            tk_vs = inc("vector", lambda e, sl=sl: e.reduce_sum(
                SC[:, 1:2], XT[sl][:], axis=mybir.AxisListType.X), "v")
            w("vector", tk_sq)
            tk_sum_prev = inc("vector", lambda e: e.reduce_sum(
                SC[:, 11:12], QA[:], axis=mybir.AxisListType.X), "v")
            drain("vector")
            S.emit("vector", lambda e: e.tensor_scalar_mul(SC[:, 12:13], SC[:, 1:2], 1.0 / DM))
            drain("vector")
            S.emit("vector", lambda e, g=g: e.tensor_scalar_mul(MN[:, g:g + 1],
                                                                SC[:, 12:13], -1.0))
            S.emit("vector", lambda e: e.tensor_mul(SC[:, 13:14], SC[:, 12:13], SC[:, 12:13]))
            S.emit("vector", lambda e: e.tensor_scalar_mul(SC[:, 14:15], SC[:, 11:12], 1.0 / DM))
            drain("vector")
            tk_var = inc("vector", lambda e: e.tensor_sub(SC[:, 15:16], SC[:, 14:15],
                                                          SC[:, 13:14]), "v")
            if 4 <= g < 12:
                j = g - 4
                w("vector", tk_wfl[j])
                S.emit("vector", lambda e, j=j: e.tensor_reduce(
                    SC[:, 9:10], XN[j % 2][:], axis=mybir.AxisListType.X, op=OP.max,
                    apply_absolute_value=True))
                drain("vector")
                wco, wci = (3, 6) if j % 2 == 0 else (6, 3)
                tk_wm = inc("vector", lambda e, wco=wco, wci=wci: e.tensor_max(
                    SC[:, wco:wco + 1], SC[:, wci:wci + 1], SC[:, 9:10]), "a")
                if j >= 6:
                    xnm_free_w[j % 2] = tk_wm
                drain("vector")
            w("scalar", tk_var)
            tk_sd = inc("scalar", lambda e: e.activation(out=SC[:, 11:12], in_=SC[:, 15:16],
                                                         func=AF.Sqrt, bias=cst[:, 0:1]), "s")
            w("vector", tk_sd)
            S.emit("vector", lambda e, g=g: e.reciprocal(MI[:, g:g + 1], SC[:, 11:12]))
            drain("vector")
            tk_xn_v = inc("vector", lambda e, g=g, sl=sl: e.tensor_scalar(
                QB[:], XT[sl][:], MN[:, g:g + 1], MI[:, g:g + 1],
                OP.add, OP.mult), "v")
            drain("vector")
            S.emit("vector", lambda e: e.tensor_reduce(
                SC[:, 1:2], QB[:], axis=mybir.AxisListType.X, op=OP.max,
                apply_absolute_value=True))
            drain("vector")
            xco, xci = (2, 0) if g % 2 == 0 else (0, 2)
            tk_ax = inc("vector", lambda e, xco=xco, xci=xci: e.tensor_max(
                SC[:, xco:xco + 1], SC[:, xci:xci + 1], SC[:, 1:2]), "b")
            drain("vector")
            if g == 5:
                w("scalar", tk_hm)
                tk_he = inc("scalar", lambda e: e.copy(out=XPT[:],
                                                       in_=P[3][:, 0:EC * 3]), "s")
            if g >= 12:
                for qq in range(8):
                    qt = (g - 12) * 8 + qq
                    kc, q4 = qt // 4, qt % 4
                    w("scalar", tk_wql[qt], tk_sw)
                    tk_q = inc("scalar", lambda e, kc=kc, q4=q4, qt=qt: e.activation(
                        out=Wq[:, kc * DM + q4 * 256: kc * DM + (q4 + 1) * 256],
                        in_=qslot[qt % 8], func=AF.Identity,
                        scale=SC[:, 8:9]), "q")
            emit_dg(6 if g < 12 else 0)
            xt_free[sl] = (tk_xn_v, tk_sq)
            if g == 11:
                w("tensor", tk_wm, tk_idf)
                tk_tw = inc("tensor", lambda e: e.transpose(
                    P[0][0:1, 128:256], SC[:, 6:7], identf[:]), "t")
                w("vector", tk_tw)
                S.emit("vector", lambda e: e.tensor_reduce(
                    SC[0:1, 1:2], P[0][0:1, 128:256], axis=mybir.AxisListType.X, op=OP.max))
                drain("vector")
                tk_aw = inc("vector", lambda e: e.tensor_copy(VARR[0:1, 0:1], SC[0:1, 1:2]), "v")
                w("tensor", tk_aw)
                tk_bc2 = inc("tensor", lambda e: e.matmul(P[1][:, 1:2], onesr[:],
                                                          VARR[0:1, 0:1],
                                                          start=True, stop=True), "t")
                w("vector", tk_bc2)
                tk_aw128 = inc("vector", lambda e: e.tensor_copy(SC[:, 7:8], P[1][:, 1:2]), "v")
                drain("vector")
                S.emit("vector", lambda e: e.tensor_scalar_max(SC[:, 1:2], SC[:, 7:8], 1e-12))
                drain("vector")
                S.emit("vector", lambda e: e.reciprocal(SC[:, 11:12], SC[:, 1:2]))
                drain("vector")
                S.emit("vector", lambda e: e.tensor_scalar(SC[:, 12:13], SC[:, 11:12],
                                                           448.0, 448.0, OP.mult, OP.min))
                drain("vector")
                S.emit("vector", lambda e: e.tensor_scalar_mul(SC[:, 8:9], SC[:, 12:13], 0.5))
                tk_sw = inc("vector", lambda e: e.reciprocal(SC[:, 9:10], SC[:, 12:13]), "v")
                drain("vector")
            if g == 0:
                w("vector", tk_hx, tk_hsq)
                S.emit("vector", lambda e: e.reduce_sum(HCOL[0:3, 0:1], XS[0:3, hx0:hx0 + DM],
                                                        axis=mybir.AxisListType.X))
                S.emit("vector", lambda e: e.reduce_sum(HCOL[0:3, 1:2], XS[0:3, hs0:hs0 + DM],
                                                        axis=mybir.AxisListType.X))
                drain("vector")
                S.emit("vector", lambda e: e.tensor_scalar_mul(HCOL[0:3, 2:3], HCOL[0:3, 0:1],
                                                               1.0 / DM))
                drain("vector")
                S.emit("vector", lambda e: e.tensor_mul(HCOL[0:3, 0:1], HCOL[0:3, 2:3],
                                                        HCOL[0:3, 2:3]))
                S.emit("vector", lambda e: e.tensor_scalar_mul(HCOL[0:3, 3:4], HCOL[0:3, 1:2],
                                                               1.0 / DM))
                drain("vector")
                tk_hv = inc("vector", lambda e: e.tensor_sub(HCOL[0:3, 1:2], HCOL[0:3, 3:4],
                                                             HCOL[0:3, 0:1]), "v")
                w("scalar", tk_hv)
                tk_hsd = inc("scalar", lambda e: e.activation(
                    out=HCOL[0:3, 3:4], in_=HCOL[0:3, 1:2],
                    func=AF.Sqrt, bias=cst[0:3, 0:1]), "s")
                w("vector", tk_hsd)
                S.emit("vector", lambda e: e.reciprocal(HCOL[0:3, 0:1], HCOL[0:3, 3:4]))
                drain("vector")
                S.emit("vector", lambda e: e.tensor_scalar_mul(HCOL[0:3, 1:2],
                                                               HCOL[0:3, 2:3], -1.0))
                drain("vector")
                tk_hap = inc("vector", lambda e: e.tensor_scalar(
                    XN[1][0:3, :], XS[0:3, hx0:hx0 + DM],
                    HCOL[0:3, 1:2], HCOL[0:3, 0:1], OP.add, OP.mult), "v")
                drain("vector")
                w("tensor", tk_hap, tk_idf)
                for f in range(KD):
                    tk_ht = inc("tensor", lambda e, f=f: e.transpose(
                        P[2][:, f * 3:(f + 1) * 3], XN[1][0:3, f * 128:(f + 1) * 128],
                        identf[0:3, 0:3]), "h")
                w("scalar", tk_ht)
                tk_hf = inc("scalar", lambda e: e.copy(out=HFM[:], in_=P[2][:, 0:24]), "s")
                w("tensor", tk_hf, tk_w1)
                for ec in range(EC):
                    for kc in range(KD):
                        tk_hm = inc("tensor", lambda e, ec=ec, kc=kc: e.matmul(
                            P[3][:, ec * 3:(ec + 1) * 3],
                            W1[:, kc * E + ec * 128: kc * E + ec * 128 + 128],
                            HFM[:, kc * 3:(kc + 1) * 3],
                            start=(ec == 0 and kc == 0), stop=(kc == 7),
                            skip_group_check=True), "t")
        tk_x_pre = [None, None]
        for s_ in range(2):
            w("sync", xt_free[s_])
            tk_x_pre[s_] = dma("sync", XT[s_][:], x_in[s_ * 128:(s_ + 1) * 128, :], "x")
            xt_free[s_] = None
        w("tensor", tk_ax, tk_idf)
        tk_tr = inc("tensor", lambda e: e.transpose(P[0][0:1, 0:128], SC[:, 0:1], identf[:]), "t")
        w("vector", tk_tr)
        S.emit("vector", lambda e: e.tensor_reduce(SC[0:1, 1:2], P[0][0:1, 0:128],
                                                   axis=mybir.AxisListType.X, op=OP.max))
        drain("vector")
        tk_axs = inc("vector", lambda e: e.tensor_copy(VARR[0:1, 0:1], SC[0:1, 1:2]), "v")
        w("tensor", tk_axs)
        tk_bc = inc("tensor", lambda e: e.matmul(P[1][:, 0:1], onesr[:], VARR[0:1, 0:1],
                                                 start=True, stop=True), "t")
        w("vector", tk_bc, tk_small)
        tk_ax128 = inc("vector", lambda e: e.tensor_copy(SC[:, 2:3], P[1][:, 0:1]), "v")
        drain("vector")
        tk_p1 = inc("vector", lambda e: e.tensor_scalar(PAY1[:], M8, SC[:, 2:3],
                                                        None, OP.mult), "v")
        w("sync", tk_p1)
        tk_ccin = dma("sync", cc1i.ap(), PAY1[:], "x")
        w("gpsimd", tk_ccin)
        tk_cc1 = inc("gpsimd", lambda e: e.collective_compute(
            "AllReduce", OP.add, replica_groups=[list(range(NCORE))],
            ins=[cc1i.ap().opt()], outs=[cc1o.ap().opt()]), "c")
        w("gpsimd", tk_cc1)
        w("sync", tk_cc1)
        tk_g1 = dma("sync", G1[:], cc1o.ap(), "x")
        xnm_free = [xnm_free_w[0], (xnm_free_w[1], tk_ht)]

        tl = {"g2": None, "cc2": None, "cs": [None] * NTI, "p2": None}

        def emit_tail_prefix():
            w("vector", tk_scan_done[NTI - 2])
            S.emit("vector", lambda e: e.memset(LL[:, 0:1], 0.0))
            S.emit("vector", lambda e: e.memset(PP[:, 0:1], 1.0))
            for q in range(NTI - 1):
                ec_ = (q + 1) * TT - 1
                S.emit("vector", lambda e, q=q, ec_=ec_: e.tensor_copy(
                    EQF[:, 2 * q:2 * q + 1], A_loc[:, ec_:ec_ + 1]))
                S.emit("vector", lambda e, q=q, ec_=ec_: e.tensor_copy(
                    EQF[:, 2 * q + 1:2 * q + 2], B_loc[:, ec_:ec_ + 1]))
            drain("vector")
            for q in range(NTI - 1):
                S.emit("vector", lambda e, q=q: e.scalar_tensor_tensor(
                    out=LL[:, q + 1:q + 2], in0=LL[:, q:q + 1],
                    scalar=EQF[:, 2 * q:2 * q + 1],
                    in1=EQF[:, 2 * q + 1:2 * q + 2], op0=OP.mult, op1=OP.add))
                S.emit("vector", lambda e, q=q: e.tensor_scalar(
                    PP[:, q + 1:q + 2], PP[:, q:q + 1], EQF[:, 2 * q:2 * q + 1],
                    None, OP.mult))
                drain("vector")

        def emit_tail_cc():
            q = NTI - 1
            ec_ = (q + 1) * TT - 1
            w("vector", tk_scan_done[NTI - 1])
            S.emit("vector", lambda e, q=q, ec_=ec_: e.tensor_copy(
                EQF[:, 2 * q:2 * q + 1], A_loc[:, ec_:ec_ + 1]))
            S.emit("vector", lambda e, q=q, ec_=ec_: e.tensor_copy(
                EQF[:, 2 * q + 1:2 * q + 2], B_loc[:, ec_:ec_ + 1]))
            drain("vector")
            S.emit("vector", lambda e, q=q: e.scalar_tensor_tensor(
                out=LL[:, q + 1:q + 2], in0=LL[:, q:q + 1],
                scalar=EQF[:, 2 * q:2 * q + 1],
                in1=EQF[:, 2 * q + 1:2 * q + 2], op0=OP.mult, op1=OP.add))
            S.emit("vector", lambda e, q=q: e.tensor_scalar(
                PP[:, q + 1:q + 2], PP[:, q:q + 1], EQF[:, 2 * q:2 * q + 1],
                None, OP.mult))
            drain("vector")
            for j in range(NCORE):
                S.emit("vector", lambda e, j=j: e.tensor_copy(
                    PAY2[:, 2 * j:2 * j + 1], PP[:, NTI:NTI + 1]))
                S.emit("vector", lambda e, j=j: e.tensor_copy(
                    PAY2[:, 2 * j + 1:2 * j + 2], LL[:, NTI:NTI + 1]))
            drain("vector")
            tl["p2"] = inc("vector", lambda e: e.tensor_mul(PAY2b[:], PAY2[:], M16), "v")
            w("sync", tl["p2"])
            tk_cc2in = dma("sync", cc2i.ap(), PAY2b[:], "x")
            w("gpsimd", tk_cc2in)
            tl["cc2"] = inc("gpsimd", lambda e: e.collective_compute(
                "AllReduce", OP.add, replica_groups=[list(range(NCORE))],
                ins=[cc2i.ap().opt()], outs=[cc2o.ap().opt()]), "c")
            w("gpsimd", tl["cc2"])
            w("sync", tl["cc2"])
            tl["g2"] = dma("sync", G2[:], cc2o.ap(), "x")

        def emit_tail_schain():
            w("vector", tl["g2"])
            S.emit("vector", lambda e: e.memset(SS[:, 0:1], 0.0))
            drain("vector")
            for j in range(NCORE - 1):
                S.emit("vector", lambda e, j=j: e.scalar_tensor_tensor(
                    out=SS[:, j + 1:j + 2], in0=SS[:, j:j + 1],
                    scalar=G2[:, 2 * j:2 * j + 1],
                    in1=G2[:, 2 * j + 1:2 * j + 2], op0=OP.mult, op1=OP.add))
                drain("vector")
            S.emit("vector", lambda e: e.tensor_mul(SELT[:], SS[:, 0:NCORE], SEL))
            drain("vector")
            S.emit("vector", lambda e: e.tensor_add(SELT[:, 0:4], SELT[:, 0:4], SELT[:, 4:8]))
            drain("vector")
            S.emit("vector", lambda e: e.tensor_add(SELT[:, 0:2], SELT[:, 0:2], SELT[:, 2:4]))
            drain("vector")
            S.emit("vector", lambda e: e.tensor_add(SIN[:], SELT[:, 0:1], SELT[:, 1:2]))
            drain("vector")
            for q in range(NTI):
                S.emit("vector", lambda e, q=q: e.scalar_tensor_tensor(
                    out=SST[:, q:q + 1], in0=PP[:, q:q + 1], scalar=SIN[:],
                    in1=LL[:, q:q + 1], op0=OP.mult, op1=OP.add))
            drain("vector")

        def emit_tail_q(q):
            sl0 = q * TT
            S.emit("vector", lambda e, q=q, sl0=sl0: e.scalar_tensor_tensor(
                out=SCB[:], in0=A_loc[:, sl0:sl0 + TT], scalar=SST[:, q:q + 1],
                in1=B_loc[:, sl0:sl0 + TT], op0=OP.mult, op1=OP.add))
            drain("vector")
            tl["cs"][q] = inc("vector", lambda e, sl0=sl0: e.tensor_mul(
                A_loc[:, sl0:sl0 + TT], SCB[:], C_loc[:, sl0:sl0 + TT]), "v")
            drain("vector")

        def emit_tail_ymm(tk_pf):
            regions = [(0, 0), (0, TT), (1, 0), (1, TT)]
            tk_ymm = [None] * NTI
            tk_ye = [None, None, None, None]
            for rnd in range(2):
                w("tensor", tk_pf)
                if rnd == 1:
                    w("tensor", tk_ye[0], tk_ye[1], tk_ye[2], tk_ye[3])
                for i4 in range(4):
                    q = rnd * 4 + i4
                    b, c0 = regions[i4]
                    w("tensor", tl["cs"][q])
                    tk_ymm[q] = inc("tensor", lambda e, q=q, b=b, c0=c0: e.matmul(
                        P[b][0:1, c0:c0 + TT], onesbf[:], A_loc[:, q * TT:(q + 1) * TT],
                        start=(i4 % 2 == 0), stop=True, skip_group_check=True), "t")
                for i4 in range(4):
                    q = rnd * 4 + i4
                    b, c0 = regions[i4]
                    w("scalar", tk_ymm[q])
                    tk_ye[i4] = inc("scalar", lambda e, q=q, b=b, c0=c0: e.copy(
                        out=C_loc[0:1, q * TT:(q + 1) * TT],
                        in_=P[b][0:1, c0:c0 + TT]), "s")
        pfree = {0: tk_axs, 1: tk_ax128, 2: tk_hf, 3: tk_he,
                 4: None, 5: None, 6: None, 7: None}
        tk_store = [None] * 16
        tk_xs_free = None
        tk_xnb_free = None
        tk_xq_free = None
        tk_drb_free = None
        tk_szb_free = [None, None]
        tk_ynbf_free = [None, None]
        tk_ygb_free = [None, None, None, None]
        tk_xsq_free = [None, None]
        tk_gt_free = [None, None]
        tk_gf_free = [None, None]
        tk_sgb_free = [None, None]
        tk_sca_free = None
        tk_qab_prev = None
        tk_scan_done = [None] * NTI

        tk_xn_next = [None, None]
        for s_ in range(2):
            w("vector", tk_x_pre[s_], xnm_free[s_])
            tk_xn_next[s_] = inc("vector", lambda e, s_=s_: e.tensor_scalar(
                XN[s_][:], XT[s_][:], MN[:, s_:s_ + 1], MI[:, s_:s_ + 1],
                OP.add, OP.mult), "v")
            xt_free[s_] = tk_xn_next[s_]
        tk_x_next = [None, None]

        for ti in range(NTI):
            g0 = ti * 2
            tk_xn = list(tk_xn_next)
            tk_acc = [None, None]
            tk_tp = [None] * 4
            for s_ in range(2):
                w("tensor", tk_xn[s_])
                for half in range(2):
                    bank = 4 + s_ * 2 + half
                    w("tensor", pfree[bank])
                    for f4 in range(4):
                        f = half * 4 + f4
                        tk_tp[bank - 4] = inc("tensor", lambda e, s_=s_, f=f, bank=bank, f4=f4:
                                              e.transpose(P[bank][:, f4 * 128:(f4 + 1) * 128],
                                                          XN[s_][:, f * 128:(f + 1) * 128],
                                                          identf[:]), "t")
            def emit_sx_chain():
                w("vector", tk_g1)
                S.emit("vector", lambda e: e.tensor_reduce(
                    SC[:, 3:4], G1[:], axis=mybir.AxisListType.X, op=OP.max))
                drain("vector")
                S.emit("vector", lambda e: e.tensor_scalar_max(SC[:, 1:2], SC[:, 3:4], 1e-12))
                drain("vector")
                S.emit("vector", lambda e: e.reciprocal(SC[:, 11:12], SC[:, 1:2]))
                drain("vector")
                S.emit("vector", lambda e: e.tensor_scalar(SC[:, 12:13], SC[:, 11:12],
                                                           448.0, 448.0, OP.mult, OP.min))
                drain("vector")
                S.emit("vector", lambda e: e.tensor_scalar_mul(SC[:, 4:5], SC[:, 12:13], 0.5))
                tk = inc("vector", lambda e: e.reciprocal(SC[:, 5:6], SC[:, 12:13]), "sx")
                drain("vector")
                S.emit("vector", lambda e: e.tensor_scalar_mul(SC[:, 11:12], SC[:, 5:6], 4.0))
                drain("vector")
                tk4 = inc("vector", lambda e: e.tensor_mul(SC[:, 10:11], SC[:, 11:12],
                                                           SC[:, 9:10]), "v")
                drain("vector")
                return tk, tk4
            tk_xnb = [None, None]
            tk_xq = [None, None]
            for s_ in range(2):
                w("scalar", tk_tp[s_ * 2], tk_tp[s_ * 2 + 1], tk_xnb_free)
                for half in range(2):
                    bank = 4 + s_ * 2 + half
                    tk_xnb[s_] = inc("scalar", lambda e, s_=s_, half=half, bank=bank: e.copy(
                        out=XNB[:, s_ * DM + half * 512: s_ * DM + half * 512 + 512],
                        in_=P[bank][:]), "s")
            def emit_xq():
                for s_ in range(2):
                    w("scalar", ("sx", semsx, 1), tk_xq_free)
                    for half in range(2):
                        bank = 4 + s_ * 2 + half
                        tk_xq[s_] = inc("scalar", lambda e, s_=s_, half=half, bank=bank:
                                        e.activation(
                            out=XQ[:, s_ * DM + half * 512: s_ * DM + half * 512 + 512],
                            in_=P[bank][:], func=AF.Identity, scale=SC[:, 4:5]), "s")
                return (tk_xq[0], tk_xq[1])
            if ti > 0:
                pf47 = emit_xq()

            w("tensor", tk_xnb[0], tk_xnb[1], pfree[2], tk_wb0)
            for kc in range(KD):
                tk_b0 = inc("tensor", lambda e, kc=kc: e.matmul(
                    P[2][0:1, 0:TT], wb0[:, kc:kc + 1], xnb3[:, :, kc * 128:(kc + 1) * 128],
                    start=(kc == 0), stop=(kc == 7)), "t")
            w("scalar", tk_b0)
            tk_b0r = inc("scalar", lambda e: e.copy(out=B0R[:], in_=P[2][0:1, 0:TT]), "s")
            pfree[2] = tk_b0r

            tk_proj = [None] * EC
            tk_xpev = [None] * EC
            tk_silu = [None] * EC
            tk_xsq = [None] * EC
            tk_convd = [None] * EC
            tk_cdr = [None] * EC

            def emit_proj(e_, ti=ti):
                bank = e_ % 2
                w("tensor", pfree[bank])
                if e_ >= 2:
                    w("tensor", tk_xpev[e_ - 2])
                for kc in range(KD):
                    tk = inc("tensor", lambda e, kc=kc, e_=e_, bank=bank: e.matmul(
                        P[bank][0:128, 0:TT],
                        W1[:, kc * E + e_ * 128: kc * E + e_ * 128 + 128],
                        xnb3[:, :, kc * 128:(kc + 1) * 128],
                        start=(kc == 0), stop=(kc == 7)), "t")
                tk_proj[e_] = tk

            def emit_xpev(e_):
                hb = e_ % 2
                base = hb * 260
                w("vector", tk_proj[e_])
                if e_ >= 2:
                    w("vector", tk_convd[e_ - 2])
                if ti == 0:
                    w("vector", tk_he)
                S.emit("vector", lambda e, e_=e_, base=base: e.tensor_copy(
                    out=XPB[:, base + 1:base + 4], in_=XPT[:, e_ * 3:(e_ + 1) * 3]))
                S.emit("vector", lambda e, e_=e_, base=base, hb=hb: e.tensor_copy(
                    out=XPB[:, base + 4:base + 260], in_=P[hb][0:128, 0:TT]))
                drain("vector")
                tk_xpev[e_] = inc("vector", lambda e, e_=e_, base=base: e.tensor_copy(
                    out=XPT[:, e_ * 3:(e_ + 1) * 3], in_=XPB[:, base + 257:base + 260]), "v")
                drain("vector")

            def emit_conv(e_):
                hb = e_ % 2
                base = hb * 260
                bank = 2 + e_ % 2
                w("tensor", tk_xpev[e_], pfree[bank])
                for k in range(4):
                    j = e_ * 4 + k
                    tk = inc("tensor", lambda e, j=j, k=k, base=base, bank=bank: e.matmul(
                        P[bank][0:128, 0:TT], DG[:, j * 128:(j + 1) * 128],
                        XPB[:, base + 1 + k: base + 1 + k + TT],
                        start=(k == 0), stop=(k == 3)), "t")
                tk_convd[e_] = tk

            def emit_silu(e_, ti=ti):
                bank = 2 + e_ % 2
                w("scalar", tk_convd[e_], tk_xs_free)
                if ti == 0 and e_ == 8:
                    w("scalar", tk_hap)
                tk = inc("scalar", lambda e, e_=e_, bank=bank: e.activation(
                    out=XS[:, e_ * TT:(e_ + 1) * TT], in_=P[bank][0:128, 0:TT],
                    func=AF.Silu, bias=CB[:, e_:e_ + 1]), "s")
                if e_ == 0:
                    tk = inc("scalar", lambda e, bank=bank: e.activation(
                        out=XS0R[:], in_=P[bank][0:1, 0:TT],
                        func=AF.Silu, bias=CB[0:1, 0:1]), "s")
                tk_silu[e_] = tk
                pfree[bank] = tk

            def emit_xsq(e_):
                w("vector", tk_silu[e_])
                if e_ >= 4:
                    w("vector", tk_stats_q[(e_ - 4) // 4])
                elif ti > 0:
                    w("vector", tk_xsq_free[0])
                tk_xsq[e_] = inc("vector", lambda e, e_=e_: e.tensor_mul(
                    XSQ2[:, (e_ % 4) * TT:(e_ % 4 + 1) * TT], XS[:, e_ * TT:(e_ + 1) * TT],
                    XS[:, e_ * TT:(e_ + 1) * TT]), "v")

            def emit_stats_half(r0, r1):
                w("tensor", tk_silu[r1 - 1], tk_xsq[r1 - 1], tk_wx)
                if r0 == 0:
                    w("tensor", pf47[0], pf47[1])
                tk = None
                for e_ in range(r0, r1):
                    inc("tensor", lambda e, e_=e_: e.matmul(
                        P[6][0:1, 0:TT], Dm1[:, e_:e_ + 1], XS[:, e_ * TT:(e_ + 1) * TT],
                        start=(e_ == 0), stop=(e_ == EC - 1)), "t")
                    tk = inc("tensor", lambda e, e_=e_: e.matmul(
                        P[7][0:1, 0:TT], Dm2[:, e_:e_ + 1],
                        XSQ2[:, (e_ % 4) * TT:(e_ % 4 + 1) * TT],
                        start=(e_ == 0), stop=(e_ == EC - 1)), "t")
                    inc("tensor", lambda e, e_=e_: e.matmul(
                        P[5][0:128, 0:TT], Wxs[:, e_ * 192 + 64:(e_ + 1) * 192],
                        XS[:, e_ * TT:(e_ + 1) * TT],
                        start=(e_ == 0), stop=(e_ == EC - 1)), "t")
                    tk_cdr[e_] = inc("tensor", lambda e, e_=e_: e.matmul(
                        P[4][0:64, 0:TT], Wxs[:, e_ * 192: e_ * 192 + 64],
                        XS[:, e_ * TT:(e_ + 1) * TT],
                        start=(e_ == 0), stop=(e_ == EC - 1)), "t")
                return tk
            tk_stats_q = [None, None, None, None]
            for step in range(EC + 4):
                if ti == 0 and step == 5:
                    tk_sx, tk_s4 = emit_sx_chain()
                    pf47 = emit_xq()
                if ti == NTI - 1 and step == 10:
                    emit_tail_prefix()
                if ti < NTI - 1 and step == 8:
                    for s_ in range(2):
                        g = g0 + 2 + s_
                        sl = g % 2
                        w("sync", xt_free[sl])
                        tk_x_next[s_] = dma("sync", XT[sl][:],
                                            x_in[g * 128:(g + 1) * 128, :], "x")
                if ti < NTI - 1 and step == 12:
                    for s_ in range(2):
                        g = g0 + 2 + s_
                        sl = g % 2
                        w("vector", tk_x_next[s_], tk_tp[s_ * 2], tk_tp[s_ * 2 + 1])
                        tk_xn_next[s_] = inc("vector", lambda e, g=g, sl=sl, s_=s_:
                                             e.tensor_scalar(
                            XN[s_][:], XT[sl][:], MN[:, g:g + 1], MI[:, g:g + 1],
                            OP.add, OP.mult), "v")
                        xt_free[sl] = tk_xn_next[s_]
                if step < EC:
                    emit_proj(step)
                    emit_xpev(step)
                if 1 <= step <= EC:
                    emit_conv(step - 1)
                    emit_silu(step - 1)
                if 4 <= step <= EC + 3:
                    emit_xsq(step - 4)
                if step in (7, 11, 15):
                    tk_stats_q[(step - 7) // 4] = emit_stats_half(step - 7, step - 3)
            emit_stats_half(12, EC)
            tk_eloop = tk_cdr[EC - 1]
            tk_xsq_free[0] = tk_eloop

            w("scalar", tk_eloop, tk_drb_free)
            tk_drb = inc("scalar", lambda e: e.copy(out=DRB[:], in_=P[4][0:64, 0:TT]), "s")
            pfree[4] = tk_drb
            tk_cl = inc("scalar", lambda e, ti=ti: e.copy(
                out=C_loc[:, ti * TT:(ti + 1) * TT], in_=P[5][0:128, 0:TT]), "s")
            pfree[5] = tk_cl
            tk_m1r = inc("scalar", lambda e: e.copy(out=M1R[:], in_=P[6][0:1, 0:TT]), "s")
            tk_m2r = inc("scalar", lambda e: e.copy(out=M2R[:], in_=P[7][0:1, 0:TT]), "s")
            pfree[6] = tk_m1r
            pfree[7] = tk_m2r
            w("vector", tk_m1r, tk_m2r)
            S.emit("vector", lambda e: e.tensor_mul(RSTR[:], M1R[:], M1R[:]))
            drain("vector")
            tk_var2 = inc("vector", lambda e: e.tensor_sub(VARR[:], M2R[:], RSTR[:]), "v")
            w("tensor", tk_drb, pfree[2], tk_wdt)
            tk_dmm = inc("tensor", lambda e: e.matmul(
                P[2][0:1, 0:TT], wdt0b[:], DRB[:], start=True, stop=True), "t")
            w("scalar", tk_dmm)
            tk_dltev = inc("scalar", lambda e: e.copy(out=T0R[:], in_=P[2][0:1, 0:TT]), "s")
            pfree[2] = tk_dltev
            drain("scalar")
            S.emit("scalar", lambda e: e.activation(out=T1R[:], in_=T0R[:], func=AF.Abs,
                                                    bias=BDT))
            w("scalar", tk_b0r)
            tk_habs = inc("scalar", lambda e: e.activation(out=BM0R[:], in_=B0R[:],
                                                           func=AF.Abs), "s")
            drain("scalar")
            S.emit("scalar", lambda e: e.activation(out=DLTR[:], in_=T1R[:], func=AF.Exp,
                                                    scale=-1.0))
            S.emit("scalar", lambda e: e.activation(out=BDR[:], in_=BM0R[:], func=AF.Exp,
                                                    scale=-1.0))
            w("scalar", tk_var2)
            S.emit("scalar", lambda e: e.activation(out=M2R[:], in_=VARR[:], func=AF.Ln,
                                                    bias=cst[0:1, 0:1]))
            drain("scalar")
            tk_sp1 = inc("scalar", lambda e: e.activation(out=T1R[:], in_=DLTR[:],
                                                          func=AF.Ln, bias=1.0), "s")
            tk_sp2 = inc("scalar", lambda e: e.activation(out=BM0R[:], in_=BDR[:],
                                                          func=AF.Ln, bias=1.0), "s")
            tk_rst = inc("scalar", lambda e: e.activation(out=RSTR[:], in_=M2R[:], func=AF.Exp,
                                                          scale=cst[0:1, 1:2]), "s")
            drain("scalar")

            tk_zmm = [None] * EC
            tk_sz = [None] * EC
            tk_yg = [None] * EC
            tk_wout = [None] * EC
            tk_m1b = tk_rsb = None
            tk_dltev = tk_dl1 = tk_bd = tk_aev = tk_bev = None
            tk_ynf1_free = [None]

            def emit_z(e_):
                bank = e_ % 2
                if e_ < 2:
                    w("tensor", tk_xpev[EC - 2 + e_], pfree[e_ % 2])
                else:
                    w("tensor", tk_sz[e_ - 2])
                w("tensor", tk_w2)
                for kc in range(KD):
                    tk = inc("tensor", lambda e, kc=kc, e_=e_, bank=bank: e.matmul(
                        P[bank][0:128, 0:TT],
                        W2[:, kc * E + e_ * 128: kc * E + e_ * 128 + 128],
                        xnb3[:, :, kc * 128:(kc + 1) * 128],
                        start=(kc == 0), stop=(kc == 7)), "t")
                tk_zmm[e_] = tk

            def emit_ychain(e_):
                bank = e_ % 2
                sl = e_ % 2
                gsl = e_ % 4
                w("scalar", tk_zmm[e_], tk_szb_free[sl])
                S.emit("scalar", lambda e, bank=bank: e.activation(
                    out=TNH[:], in_=P[bank][0:128, 0:TT], func=AF.Tanh,
                    scale=cst[:, 2:3]))
                drain("scalar")
                tk_sz[e_] = inc("scalar", lambda e, sl=sl: e.activation(
                    out=SZB[:, sl * TT:(sl + 1) * TT], in_=TNH[:],
                    func=AF.Identity, scale=cst[:, 2:3], bias=cst[:, 2:3]), "s")
                drain("scalar")
                w("vector", tk_m1b, tk_rsb, tk_eloop, tk_ln)
                S.emit("vector", lambda e, e_=e_: e.scalar_tensor_tensor(
                    out=YNF[0][:], in0=XS[:, e_ * TT:(e_ + 1) * TT], scalar=DD[:, e_:e_ + 1],
                    in1=M1B[:], op0=OP.mult, op1=OP.subtract))
                drain("vector")
                w("vector", tk_ynf1_free[0])
                tk_stt2 = inc("vector", lambda e, e_=e_: e.scalar_tensor_tensor(
                    out=YNF[1][:], in0=YNF[0][:], scalar=1.0,
                    in1=RSB[:], op0=OP.mult, op1=OP.mult), "v")
                drain("vector")
                w("scalar", tk_stt2, tk_ynbf_free[sl])
                tk_yn = inc("scalar", lambda e, e_=e_, sl=sl: e.activation(
                    out=YNBF[:, sl * TT:(sl + 1) * TT], in_=YNF[1][:], func=AF.Identity,
                    scale=LNG[:, e_:e_ + 1], bias=LNB[:, e_:e_ + 1]), "s")
                w("vector", tk_sz[e_], tk_yn, tk_ygb_free[gsl])
                tk_yg[e_] = inc("vector", lambda e, sl=sl, gsl=gsl: e.tensor_mul(
                    YGB[:, gsl * TT:(gsl + 1) * TT], YNBF[:, sl * TT:(sl + 1) * TT],
                    SZB[:, sl * TT:(sl + 1) * TT]), "v")
                tk_ynbf_free[sl] = tk_yg[e_]
                tk_szb_free[sl] = tk_yg[e_]
                tk_ynf1_free[0] = tk_yn

            def emit_wout(e_):
                gsl = e_ % 4
                w("tensor", tk_yg[e_], tk_w3)
                for d_ in range(KD):
                    bank = 4 + d_ // 2
                    col = (d_ % 2) * TT
                    st = (e_ == 0 and d_ % 2 == 0)
                    tk = inc("tensor", lambda e, e_=e_, d_=d_, bank=bank, col=col, st=st, gsl=gsl:
                             e.matmul(P[bank][0:128, col:col + TT],
                                      W3[:, e_ * DM + d_ * 128: e_ * DM + d_ * 128 + 128],
                                      YGB[:, gsl * TT:(gsl + 1) * TT],
                                      start=st, stop=(e_ == EC - 1),
                                      skip_group_check=True), "t")
                tk_wout[e_] = tk
                tk_ygb_free[gsl] = tk

            def emit_scan_level(li, ti=ti):
                sl0 = ti * TT
                d = 1 << li
                if li == 0:
                    w("vector", tk_aev, tk_bev)
                    drain("vector")
                S.emit("vector", lambda e, sl0=sl0: e.tensor_copy(SCA[:], A_loc[:, sl0:sl0 + TT]))
                S.emit("vector", lambda e, sl0=sl0: e.tensor_copy(SCB[:], B_loc[:, sl0:sl0 + TT]))
                drain("vector")
                S.emit("vector", lambda e, sl0=sl0, d=d: e.tensor_mul(
                    A_loc[:, sl0 + d:sl0 + TT], SCA[:, d:TT], SCA[:, 0:TT - d]))
                S.emit("vector", lambda e, d=d: e.tensor_mul(
                    SCC[:, 0:TT - d], SCA[:, d:TT], SCB[:, 0:TT - d]))
                drain("vector")
                S.emit("vector", lambda e, sl0=sl0, d=d: e.tensor_add(
                    B_loc[:, sl0 + d:sl0 + TT], SCB[:, d:TT], SCC[:, 0:TT - d]))
                drain("vector")
                if li == 7:
                    tk_scan_done[ti] = inc("vector", lambda e, sl0=sl0: e.tensor_copy(
                        SCA[:, 0:1], A_loc[:, sl0:sl0 + 1]), "v")

            for step in range(EC + 3):
                if 2 <= step < EC + 2:
                    emit_ychain(step - 2)
                if 11 <= step < 19:
                    emit_scan_level(step - 11)
                if step < EC:
                    emit_z(step)
                if step == 0:
                    w("tensor", tk_m1r, pfree[2])
                    tk_m1bc = inc("tensor", lambda e: e.matmul(
                        P[2][0:128, 0:TT], onesr[:], M1R[:], start=True, stop=True), "t")
                    w("vector", tk_m1bc)
                    tk_m1b = inc("vector", lambda e: e.tensor_copy(M1B[:], P[2][0:128, 0:TT]), "v")
                    pfree[2] = tk_m1b
                if step == 1:
                    w("tensor", tk_rst, pfree[3])
                    tk_rsbc = inc("tensor", lambda e: e.matmul(
                        P[3][0:128, 0:TT], onesr[:], RSTR[:], start=True, stop=True), "t")
                    w("vector", tk_rsbc)
                    tk_rsb = inc("vector", lambda e: e.tensor_copy(RSB[:], P[3][0:128, 0:TT]), "v")
                    pfree[3] = tk_rsb
                if step == 5:
                    w("vector", tk_sp1)
                    S.emit("vector", lambda e: e.tensor_scalar(DLTR[:], T0R[:], BDT,
                                                               0.0, OP.add, OP.max))
                    drain("vector")
                    S.emit("vector", lambda e: e.tensor_add(T0R[:], DLTR[:], T1R[:]))
                    drain("vector")
                    tk_dl1 = inc("vector", lambda e: e.tensor_scalar(DLTR[:], T0R[:], 1e-6,
                                                                     100.0, OP.max, OP.min), "v")
                    drain("vector")
                if step == 9:
                    w("vector", tk_sp2, tk_dl1)
                    S.emit("vector", lambda e: e.tensor_scalar_max(BDR[:], B0R[:], 0.0))
                    drain("vector")
                    S.emit("vector", lambda e: e.tensor_add(T0R[:], BDR[:], BM0R[:]))
                    drain("vector")
                    w("vector", tk_silu[0])
                    S.emit("vector", lambda e: e.tensor_mul(T1R[:], DLTR[:], T0R[:]))
                    drain("vector")
                    tk_bd = inc("vector", lambda e: e.tensor_mul(BDR[:], T1R[:], XS0R[:]), "v")
                    drain("vector")
                if step == 8:
                    w("tensor", tk_dl1, pfree[2])
                    tk_abc = inc("tensor", lambda e: e.matmul(P[2][0:128, 0:TT], onesrb[:],
                                                              DLTR[:], start=True, stop=True), "t")
                    w("scalar", tk_abc, tk_a0)
                    tk_aev = inc("scalar", lambda e, ti=ti: e.activation(
                        out=A_loc[:, ti * TT:(ti + 1) * TT], in_=P[2][0:128, 0:TT],
                        func=AF.Exp, scale=A0c[:]), "s")
                    pfree[2] = tk_aev
                if step == 10:
                    w("tensor", tk_bd, pfree[3])
                    tk_bbc = inc("tensor", lambda e: e.matmul(P[3][0:128, 0:TT], onesrb[:],
                                                              BDR[:], start=True, stop=True), "t")
                    w("scalar", tk_bbc)
                    tk_bev = inc("scalar", lambda e, ti=ti: e.copy(
                        out=B_loc[:, ti * TT:(ti + 1) * TT], in_=P[3][0:128, 0:TT]), "s")
                    pfree[3] = tk_bev
                if step >= 3 and step - 3 < EC:
                    emit_wout(step - 3)
            tk_youts = tk_wout[EC - 1]
            tk_xs_free = tk_yg[EC - 1]



            tk_fmm = [None] * KD
            tk_epi = [None] * KD
            tk_gfo = [None] * KD
            tk_sg = [None] * KD
            tk_tps0 = {2: None, 3: None}
            tk_tps1 = {4: None, 5: None}
            tk_c1s = [None, None]
            tk_adds0 = [None, None]
            tk_adds1 = [None, None]

            def emit_ffn(o):
                bank = o % 2
                w("tensor", tk_xq[0], tk_xq[1], tk_q)
                if o < 2:
                    w("tensor", tk_sz[EC - 2 + o])
                else:
                    w("tensor", tk_epi[o - 2])
                for kc in range(KD):
                    tk = inc("tensor", lambda e, kc=kc, o=o, bank=bank: e.matmul(
                        P[bank][0:128, 0:TT],
                        Wq[:, kc * DM + o * 128: kc * DM + o * 128 + 128],
                        xq3[:, :, kc * 128:(kc + 1) * 128],
                        start=(kc == 0), stop=(kc == 7)), "t")
                tk_fmm[o] = tk

            def emit_epi(o):
                bank = o % 2
                sl = o % 2
                w("vector", tk_fmm[o], tk_s4, tk_gt_free[sl])
                tk_epi[o] = inc("vector", lambda e, o=o, bank=bank, sl=sl: e.tensor_scalar(
                    GT[:, sl * TT:(sl + 1) * TT], P[bank][0:128, 0:TT],
                    SC[:, 10:11], FFB[:, o:o + 1], OP.mult, OP.add), "v")
                w("scalar", tk_epi[o], tk_gf_free[sl])
                tk_gfo[o] = inc("scalar", lambda e, sl=sl: e.activation(
                    out=GF[:, sl * TT:(sl + 1) * TT], in_=GT[:, sl * TT:(sl + 1) * TT],
                    func=AF.Silu), "s")
                sbank = 4 + o // 2
                scol = (o % 2) * TT
                w("vector", tk_gfo[o], tk_youts, tk_sgb_free[sl])
                tk_sg[o] = inc("vector", lambda e, sl=sl, sbank=sbank, scol=scol: e.tensor_mul(
                    SGB[:, sl * TT:(sl + 1) * TT], GF[:, sl * TT:(sl + 1) * TT],
                    P[sbank][0:128, scol:scol + TT]), "v")
                tk_gt_free[sl] = tk_gfo[o]
                tk_gf_free[sl] = tk_sg[o]

            def emit_sgt(o):
                sl = o % 2
                b0_ = 2 if o < 4 else 3
                w("tensor", tk_sg[o])
                if o == 0:
                    w("tensor", pfree[2])
                if o == 4:
                    w("tensor", pfree[3])
                tk_tps0[b0_] = inc("tensor", lambda e, o=o, sl=sl, b0_=b0_: e.transpose(
                    P[b0_][:, (o % 4) * 128:(o % 4 + 1) * 128],
                    SGB[:, sl * TT: sl * TT + 128], identf[:]), "t")
                if o < 2:
                    w("vector", tk_tps0[b0_])
                    tk_c1 = inc("vector", lambda e, o=o, sl=sl: e.tensor_copy(
                        SG1[:, o * 128:(o + 1) * 128],
                        SGB[:, sl * TT + 128: sl * TT + 256]), "v")
                    tk_c1s[o] = tk_c1
                    tk_sgb_free[sl] = (tk_tps0[b0_], tk_c1)
                else:
                    b1_ = 4 if o < 4 else 5
                    if o == 2:
                        w("tensor", tk_sg[1], tk_c1s[0], tk_c1s[1])
                        for oo in range(2):
                            inc("tensor", lambda e, oo=oo: e.transpose(
                                P[4][:, oo * 128:(oo + 1) * 128],
                                SG1[:, oo * 128:(oo + 1) * 128], identf[:]), "t")
                    if o == 4:
                        w("tensor", tk_sg[3])
                    tk_tps1[b1_] = inc("tensor", lambda e, o=o, sl=sl, b1_=b1_: e.transpose(
                        P[b1_][:, (o % 4) * 128:(o % 4 + 1) * 128],
                        SGB[:, sl * TT + 128: sl * TT + 256], identf[:]), "t")
                    tk_sgb_free[sl] = tk_tps1[b1_]

            for step in range(KD + 2):
                if ti == NTI - 1:
                    if step == 0:
                        emit_tail_cc()
                    elif step == 6:
                        emit_tail_schain()
                    elif step == 7:
                        emit_tail_q(0)
                        emit_tail_q(1)
                    elif step == 8:
                        emit_tail_q(2)
                        emit_tail_q(3)
                    elif step == 9:
                        for q_ in range(4, NTI):
                            emit_tail_q(q_)
                if step < KD:
                    emit_ffn(step)
                if 1 <= step <= KD:
                    emit_epi(step - 1)
                if step >= 2:
                    emit_sgt(step - 2)
                if ti < NTI - 1:
                    if step - 2 == 3:
                        w("scalar", tk_tps0[2], tk_qab_prev)
                        tk_adds0[0] = inc("scalar", lambda e: e.copy(
                            out=QA[:, 0:512], in_=P[2][:, 0:512]), "s")
                        pfree[2] = tk_adds0[0]
                    if step - 2 == 7:
                        w("scalar", tk_tps0[3])
                        tk_adds0[1] = inc("scalar", lambda e: e.copy(
                            out=QA[:, 512:1024], in_=P[3][:, 0:512]), "s")
                        pfree[3] = tk_adds0[1]
                        w("scalar", tk_tps1[4])
                        tk_adds1[0] = inc("scalar", lambda e: e.copy(
                            out=QB[:, 0:512], in_=P[4][:, 0:512]), "s")
                        pfree[4] = tk_adds1[0]
                        w("scalar", tk_tps1[5])
                        tk_adds1[1] = inc("scalar", lambda e: e.copy(
                            out=QB[:, 512:1024], in_=P[5][:, 0:512]), "s")
                        pfree[5] = tk_adds1[1]
                        w("gpsimd", tk_adds0[0], tk_adds0[1])
                        tk_acc[0] = dma("gpsimd", QA[:],
                                        x_in[g0 * 128:(g0 + 1) * 128, :], "w",
                                        accum=OP.add)
                        w("gpsimd", tk_adds1[0], tk_adds1[1])
                        tk_acc[1] = dma("gpsimd", QB[:],
                                        x_in[(g0 + 1) * 128:(g0 + 2) * 128, :], "w",
                                        accum=OP.add)
                else:
                    if step - 2 == 3:
                        w("vector", tk_tps0[2], tk_qab_prev)
                        tk_adds0[0] = inc("vector", lambda e, g0=g0: e.tensor_add(
                            QA[:, 0:512], P[2][:, 0:512], XT[g0 % 2][:, 0:512]), "v")
                        pfree[2] = tk_adds0[0]
                    if step - 2 == 7:
                        w("vector", tk_tps0[3])
                        tk_adds0[1] = inc("vector", lambda e, g0=g0: e.tensor_add(
                            QA[:, 512:1024], P[3][:, 0:512], XT[g0 % 2][:, 512:1024]), "v")
                        pfree[3] = tk_adds0[1]
                        w("vector", tk_tps1[4], tk_qab_prev)
                        tk_adds1[0] = inc("vector", lambda e, g0=g0: e.tensor_add(
                            QB[:, 0:512], P[4][:, 0:512], XT[(g0 + 1) % 2][:, 0:512]), "v")
                        pfree[4] = tk_adds1[0]
                        w("vector", tk_tps1[5])
                        tk_adds1[1] = inc("vector", lambda e, g0=g0: e.tensor_add(
                            QB[:, 512:1024], P[5][:, 0:512], XT[(g0 + 1) % 2][:, 512:1024]), "v")
                        pfree[5] = tk_adds1[1]
                        tk_acc[0] = (tk_adds0[0], tk_adds0[1])
                        tk_acc[1] = (tk_adds1[0], tk_adds1[1])
            for s_ in range(2):
                g = g0 + s_
                w("sync", tk_acc[s_])
                tk_store[g] = dma("sync", y_out[g * 128:(g + 1) * 128, :],
                                  (QA if s_ == 0 else QB)[:], "o")
            if ti == NTI - 1:
                xt_free[0] = (tk_adds0[0], tk_adds0[1])
                xt_free[1] = (tk_adds1[0], tk_adds1[1])
            tk_qab_prev = (tk_store[g0], tk_store[g0 + 1],
                           tk_acc[0], tk_acc[1])
            tk_xnb_free = tk_zmm[EC - 1]
            tk_xq_free = tk_fmm[KD - 1]
            tk_drb_free = tk_dmm
            pfree[0] = tk_epi[KD - 2]
            pfree[1] = tk_epi[KD - 1]

        emit_tail_ymm((tk_epi[KD - 2], tk_epi[KD - 1]))

        for g in range(16):
            w("sync", tk_store[g])

        block = ctx.enter_context(nc.Block())
        S.run(block)
    return nc


def kernel(**inputs):
    x = np.asarray(inputs["L0_attn_x"])[0]
    Win = np.asarray(inputs["L0_attn_Win"])
    conv_w = np.asarray(inputs["L0_attn_conv_w"])
    conv_b = np.asarray(inputs["L0_attn_conv_b"])
    Wx = np.asarray(inputs["L0_attn_Wx"])
    Wdt = np.asarray(inputs["L0_attn_Wdt"])
    bdt = np.asarray(inputs["L0_attn_bdt"])
    A_log = np.asarray(inputs["L0_attn_A_log"])
    D = np.asarray(inputs["L0_attn_D"])
    ln_g = np.asarray(inputs["L0_attn_ln_g"])
    ln_b = np.asarray(inputs["L0_attn_ln_b"])
    Wout = np.asarray(inputs["L0_attn_Wout"])
    res_scale = np.asarray(inputs["L0_attn_res_scale"])
    Wf = np.asarray(inputs["L0_ffn_weight"])
    ffb = np.asarray(inputs["L0_ffn_bias"])

    C = np.ascontiguousarray
    w_xp = C(Win[:, :E])
    w_z = C(Win[:, E:2 * E])
    w_b0 = C(Win[:, 2 * E].reshape(KD, 128).T)
    w_ft = C(Wf.T)
    wdt0 = C(Wdt[:, 0:1])
    alog0 = C(A_log[0].reshape(H, 1))
    cw = C(conv_w.reshape(EC, 128, 4).transpose(1, 0, 2).reshape(128, EC * 4))
    col = lambda v: C(v.reshape(EC, 128).T)
    ffbc = C(ffb.reshape(KD, 128).T)
    rs128 = C(np.broadcast_to(res_scale.reshape(1, 1), (128, 1)))
    bdt0 = C(bdt[0].reshape(1, 1))

    in_maps = []
    for i in range(NCORE):
        t0 = i * T
        xh = x[t0 - 3:t0] if i > 0 else np.zeros((3, DM), np.float32)
        sel = np.zeros((128, NCORE), np.float32)
        sel[:, i] = 1.0
        m8 = np.zeros((128, NCORE), np.float32)
        m8[:, i] = 1.0
        m16 = np.zeros((128, 2 * NCORE), np.float32)
        m16[:, 2 * i:2 * i + 2] = 1.0
        cst_all = np.concatenate([
            cw, col(conv_b), col(D), col(ln_g), col(ln_b), ffbc, rs128,
            sel, m8, m16, np.broadcast_to(bdt0, (128, 1)), alog0], axis=1)
        in_maps.append({
            "x": C(x[t0:t0 + T]), "xh": C(xh),
            "w_xp": w_xp, "w_z": w_z, "w_b0": w_b0, "w_x": Wx, "w_dt0": wdt0,
            "w_out": Wout, "w_ft": w_ft,
            "cst_all": C(cst_all.astype(np.float32)),
        })
    nc = build_nc()
    res = run_bass_kernel_spmd(nc, in_maps, core_ids=list(range(NCORE)))
    out = np.concatenate([r["y"] for r in res.results], axis=0)
    return out.reshape(1, NCORE * T, DM)


# version 29
# speedup vs baseline: 1.1051x; 1.0313x over previous
import numpy as np
import concourse.bass as bass
import concourse.mybir as mybir
from concourse.bass_utils import run_bass_kernel_spmd
from contextlib import ExitStack

F32 = mybir.dt.float32
BF16 = mybir.dt.bfloat16
F8 = mybir.dt.float8e4
AF = mybir.ActivationFunctionType
OP = mybir.AluOpType

NCORE = 8
T = 2048
DM = 1024
E = 2048
EC = 16
KD = 8
TT = 256
NTI = T // TT
H = 128
ENGS = ("sync", "gpsimd", "tensor", "vector", "scalar")


class Sched:
    def __init__(self):
        self.st = {e: [] for e in ENGS}
        self.cnt = {}
        self.lastw = {}

    def inc(self, eng, fn, sem, semname, n=1):
        v = self.cnt.get(semname, 0) + n
        self.cnt[semname] = v
        self.st[eng].append(lambda e, fn=fn, sem=sem, n=n: fn(e).then_inc(sem, n))
        return (semname, sem, v)

    def emit(self, eng, fn):
        self.st[eng].append(fn)

    def wait(self, eng, ticket):
        if ticket is None:
            return
        if isinstance(ticket, (tuple, list)) and ticket and isinstance(ticket[0], (tuple, list, type(None))):
            for t in ticket:
                self.wait(eng, t)
            return
        semname, sem, v = ticket
        key = (eng, semname)
        if self.lastw.get(key, -1) >= v:
            return
        self.lastw[key] = v
        self.st[eng].append(lambda e, sem=sem, v=v: e.wait_ge(sem, v))

    def run(self, block):
        for name in ENGS:
            def body(e, name=name):
                for fn in self.st[name]:
                    fn(e)
            getattr(block, name)(body)


def build_nc():
    nc = bass.Bass("TRN2", target_bir_lowering=False)
    dt = nc.dram_tensor
    x_in = dt("x", [T, DM], F32, kind="ExternalInput").ap()
    xh_in = dt("xh", [3, DM], F32, kind="ExternalInput").ap()
    w_xp = dt("w_xp", [DM, E], F32, kind="ExternalInput").ap()
    w_z = dt("w_z", [DM, E], F32, kind="ExternalInput").ap()
    w_b0 = dt("w_b0", [128, KD], F32, kind="ExternalInput").ap()
    w_x = dt("w_x", [E, 192], F32, kind="ExternalInput").ap()
    w_dt0 = dt("w_dt0", [64, 1], F32, kind="ExternalInput").ap()
    w_out = dt("w_out", [E, DM], F32, kind="ExternalInput").ap()
    w_ft = dt("w_ft", [DM, DM], F32, kind="ExternalInput").ap()
    cst_in = dt("cst_all", [128, 171], F32, kind="ExternalInput").ap()
    y_out = dt("y", [T, DM], F32, kind="ExternalOutput").ap()
    cc1i = dt("cc1i", [128, NCORE], F32)
    cc1o = dt("cc1o", [128, NCORE], F32)
    cc2i = dt("cc2i", [128, 2 * NCORE], F32)
    cc2o = dt("cc2o", [128, 2 * NCORE], F32)
    identf_d = nc.inline_tensor(np.eye(128, dtype=np.float32), "identf_dram")

    ctx = ExitStack()
    with ctx:
        sb = lambda n, s, d: ctx.enter_context(nc.sbuf_tensor(n, s, d))
        pst = lambda n: ctx.enter_context(nc.psum_tensor(n, [128, 512], F32))
        mksem = lambda n: ctx.enter_context(nc.semaphore(n))

        W1 = sb("W1", [128, KD * E], BF16)
        W2 = sb("W2", [128, KD * E], BF16)
        W3 = sb("W3", [128, EC * DM], BF16)
        Wq = sb("Wq", [128, KD * DM], F8)
        Wxs = sb("Wxs", [128, EC * 192], BF16)
        DG = sb("DG", [128, EC * 4 * 128], BF16)
        wb0 = sb("wb0", [128, KD], BF16)
        wdt0b = sb("wdt0b", [64, 1], BF16)
        identf = sb("identf", [128, 128], F32)
        identb = sb("identb", [128, 128], BF16)
        onesr = sb("onesr", [1, 128], F32)
        onesbf = sb("onesbf", [128, 1], BF16)
        onesrb = sb("onesrb", [1, 128], BF16)
        cst = sb("cst", [128, 4], F32)
        XS = sb("XS", [128, EC * TT], BF16)
        XSQ2 = sb("XSQ2", [128, 4 * TT], BF16)
        XT = [sb(f"XT{i}", [128, DM], F32) for i in range(2)]
        XN = [sb(f"XN{i}", [128, DM], F32) for i in range(2)]
        XNB = sb("XNB", [128, 2 * DM], BF16)
        XQ = sb("XQ", [128, 2 * DM], F8)
        QA = sb("QA", [128, DM], F32)
        QB = sb("QB", [128, DM], F32)
        XPB = sb("XPB", [128, 2 * 260], BF16)
        XPT = sb("XPT", [128, EC * 3], BF16)
        A_loc = sb("A_loc", [128, T], BF16)
        B_loc = sb("B_loc", [128, T], BF16)
        C_loc = sb("C_loc", [128, T], BF16)
        SCA = sb("SCA", [128, TT], BF16)
        SCB = sb("SCB", [128, TT], BF16)
        SCC = sb("SCC", [128, TT], BF16)
        DRB = sb("DRB", [64, TT], BF16)
        M1B = sb("M1B", [128, TT], F32)
        RSB = sb("RSB", [128, TT], F32)
        SZB = sb("SZB", [128, 2 * TT], BF16)
        TNH = sb("TNH", [128, TT], F32)
        YNF = [sb(f"YNF{i}", [128, TT], F32) for i in range(2)]
        YNBF = sb("YNBF", [128, 2 * TT], BF16)
        YGB = sb("YGB", [128, 4 * TT], BF16)
        GT = sb("GT", [128, 2 * TT], BF16)
        GF = sb("GF", [128, 2 * TT], F32)
        SGB = sb("SGB", [128, 2 * TT], F32)
        SG1 = sb("SG1", [128, 256], F32)
        HFM = sb("HFM", [128, KD * 3], BF16)
        MI = sb("MI", [128, 16], F32)
        MN = sb("MN", [128, 16], F32)
        MNI = sb("MNI", [128, 16], F32)
        A0c = sb("A0c", [128, 1], F32)
        Dm1 = sb("Dm1", [128, EC], BF16)
        Dm2 = sb("Dm2", [128, EC], BF16)
        CONSTS = sb("CONSTS", [128, 171], F32)
        CW = CONSTS.ap()[:, 0:64]
        CB = CONSTS.ap()[:, 64:80]
        DD = CONSTS.ap()[:, 80:96]
        LNG = CONSTS.ap()[:, 96:112]
        LNB = CONSTS.ap()[:, 112:128]
        FFB = CONSTS.ap()[:, 128:136]
        RS1 = CONSTS.ap()[:, 136:137]
        SEL = CONSTS.ap()[:, 137:145]
        M8 = CONSTS.ap()[:, 145:153]
        M16 = CONSTS.ap()[:, 153:169]
        BDT = CONSTS.ap()[0:1, 169:170]
        ALG = CONSTS.ap()[:, 170:171]
        SC = sb("SC", [128, 16], F32)
        LL = sb("LL", [128, 9], F32)
        PP = sb("PP", [128, 9], F32)
        SS = sb("SS", [128, 9], F32)
        SST = sb("SST", [128, NTI], F32)
        SIN = sb("SIN", [128, 1], F32)
        EQF = sb("EQF", [128, 2 * NTI], F32)
        SELT = sb("SELT", [128, NCORE], F32)
        PAY1 = sb("PAY1", [128, NCORE], F32)
        PAY2 = sb("PAY2", [128, 2 * NCORE], F32)
        PAY2b = sb("PAY2b", [128, 2 * NCORE], F32)
        G1 = sb("G1", [128, NCORE], F32)
        G2 = sb("G2", [128, 2 * NCORE], F32)
        HCOL = sb("HCOL", [3, 4], F32)
        B0R = sb("B0R", [1, TT], BF16)
        XS0R = sb("XS0R", [1, TT], BF16)
        T0R = sb("T0R", [1, TT], BF16)
        T1R = sb("T1R", [1, TT], BF16)
        DLTR = sb("DLTR", [1, TT], BF16)
        BM0R = sb("BM0R", [1, TT], BF16)
        BDR = sb("BDR", [1, TT], BF16)
        M1R = sb("M1R", [1, TT], F32)
        M2R = sb("M2R", [1, TT], F32)
        VARR = sb("VARR", [1, TT], F32)
        RSTR = sb("RSTR", [1, TT], F32)

        P = [pst(f"p{i}") for i in range(8)]

        semx = mksem("semx")
        semw = mksem("semw")
        semt = mksem("semt")
        semv = mksem("semv")
        sems = mksem("sems")
        semc = mksem("semc")
        semo = mksem("semo")
        semx2 = mksem("semx2")
        semw2 = mksem("semw2")
        semo2 = mksem("semo2")
        semg = mksem("semg")
        sema = mksem("sema")
        semb = mksem("semb")
        semh = mksem("semh")
        semq = mksem("semq")
        semsx = mksem("semsx")
        seml = mksem("seml")
        seml2 = mksem("seml2")
        S = Sched()
        SEM = {"x": semx, "w": semw, "t": semt, "v": semv, "s": sems,
               "c": semc, "o": semo, "x2": semx2, "w2": semw2, "o2": semo2,
               "g": semg, "a": sema, "b": semb, "h": semh, "q": semq,
               "sx": semsx, "l": seml, "l2": seml2}
        _rot = {"x": ["x", "x2"], "w": ["w", "w2"], "o": ["o", "o2"],
                "l": ["l", "l2"]}
        _ri = {"x": 0, "w": 0, "o": 0, "l": 0}

        def inc(eng, fn, s, n=1):
            return S.inc(eng, fn, SEM[s], s, n)

        def w(eng, *tickets):
            for tk in tickets:
                S.wait(eng, tk)

        def dma(eng, out, in_, s, accum=None):
            name = _rot[s][_ri[s]]
            _ri[s] ^= 1
            prior = S.cnt.get(name, 0)
            if prior:
                w(eng, (name, SEM[name], prior))
            if accum is not None:
                return inc(eng, lambda e, o=out, i=in_, a=accum: e.dma_start(
                    out=o, in_=i, accum_op=a), name, 16)
            return inc(eng, lambda e, o=out, i=in_: e.dma_start(out=o, in_=i), name, 16)

        def drain(eng):
            S.emit(eng, lambda e: e.drain())

        xnb3 = XNB.ap().rearrange("p (s c) -> p s c", c=DM)
        xq3 = XQ.ap().rearrange("p (s c) -> p s c", c=DM)

        tk_idf = dma("sync", identf[:], identf_d.ap(), "x")
        tk_idb = dma("gpsimd", identb[:], identf_d.ap(), "w")
        tk_hx = dma("gpsimd", XS[0:3, 8 * TT:8 * TT + DM], xh_in, "w")
        tk_small = dma("sync", CONSTS[:], cst_in, "x")
        tk_w1 = dma("gpsimd", W1.ap().rearrange("p (kc c) -> p kc c", c=E),
                    w_xp.rearrange("(kc p) c -> p kc c", p=128), "w")
        tk_wfl = [None] * KD
        for g_ in range(KD):
            if g_ == 1:
                w("gpsimd", ("h", semh, KD))
            if g_ >= 2:
                w("gpsimd", ("a", sema, g_ - 1))
            tk_wfl[g_] = dma("gpsimd", XN[g_ % 2][:],
                             w_ft[g_ * 128:(g_ + 1) * 128, :], "w")
        tk_wx = dma("gpsimd", Wxs.ap().rearrange("p (ec c) -> p ec c", c=192),
                    w_x.rearrange("(ec p) c -> p ec c", p=128), "w")
        tk_wb0 = dma("gpsimd", wb0[:], w_b0, "w")
        tk_wdt = dma("gpsimd", wdt0b[:], w_dt0, "w")
        qslot = [GF.ap()[:, 0:256], GF.ap()[:, 256:512],
                 SGB.ap()[:, 0:256], SGB.ap()[:, 256:512],
                 M1B.ap(), RSB.ap(), TNH.ap(), YNF[0].ap()]
        tk_wql = [None] * 32
        for qt in range(8):
            kc, q4 = qt // 4, qt % 4
            tk_wql[qt] = dma("gpsimd", qslot[qt % 8],
                             w_ft[kc * 128:(kc + 1) * 128,
                                  q4 * 256:(q4 + 1) * 256], "w")
        tk_w2 = dma("gpsimd", W2.ap().rearrange("p (kc c) -> p kc c", c=E),
                    w_z.rearrange("(kc p) c -> p kc c", p=128), "w")
        tk_w3a = dma("gpsimd", W3.ap()[:, 0:8 * DM].rearrange(
            "p (ec c) -> p ec c", c=DM),
            w_out[0:1024, :].rearrange("(ec p) c -> p ec c", p=128), "w")

        S.emit("vector", lambda e: e.memset(cst[:, 0:1], 1e-5))
        S.emit("vector", lambda e: e.memset(cst[:, 1:2], -0.5))
        S.emit("vector", lambda e: e.memset(cst[:, 2:3], 0.5))
        S.emit("vector", lambda e: e.memset(onesbf[:], 1.0))
        S.emit("vector", lambda e: e.memset(onesr[:], 1.0))
        S.emit("vector", lambda e: e.memset(onesrb[:], 1.0))
        S.emit("vector", lambda e: e.memset(XPT[:], 0.0))
        S.emit("vector", lambda e: e.memset(SC[:, 0:1], 0.0))
        S.emit("vector", lambda e: e.memset(SC[:, 6:7], 0.0))
        tk_cst = inc("vector", lambda e: e.memset(XPB[:], 0.0), "v")
        w("scalar", tk_small)
        S.emit("scalar", lambda e: e.activation(out=SC[:, 11:12], in_=ALG, func=AF.Exp))
        drain("scalar")
        tk_a0 = inc("scalar", lambda e: e.mul(A0c[:], SC[:, 11:12], -1.0), "s")
        w("vector", tk_small)
        S.emit("vector", lambda e: e.tensor_scalar_mul(Dm1[:], DD, 1.0 / 2048.0))
        S.emit("vector", lambda e: e.tensor_mul(QA[:, 0:EC], DD, DD))
        drain("vector")
        S.emit("vector", lambda e: e.tensor_scalar_mul(Dm2[:], QA[:, 0:EC], 1.0 / 2048.0))
        drain("vector")
        S.emit("vector", lambda e: e.tensor_scalar(QA[:, 0:EC], LNG, RS1, None, OP.mult))
        S.emit("vector", lambda e: e.tensor_scalar(QA[:, EC:2 * EC], LNB, RS1, None, OP.mult))
        drain("vector")
        S.emit("vector", lambda e: e.tensor_copy(LNG, QA[:, 0:EC]))
        tk_ln = inc("vector", lambda e: e.tensor_copy(LNB, QA[:, EC:2 * EC]), "v")
        drain("vector")
        w("scalar", tk_idb)
        dgj = [0]

        def emit_dg(n):
            tk = None
            for _ in range(n):
                j = dgj[0]
                if j >= EC * 4:
                    return None
                dgj[0] += 1
                tk = inc("scalar", lambda e, j=j: e.activation(
                    out=DG[:, j * 128:(j + 1) * 128], in_=identb[:], func=AF.Copy,
                    scale=CW[:, j:j + 1]), "s")
            return tk

        hx0, hs0 = 8 * TT, 8 * TT + DM
        w("scalar", tk_hx)
        tk_hsq = inc("scalar", lambda e: e.square(XS[0:3, hs0:hs0 + DM],
                                                  XS[0:3, hx0:hx0 + DM]), "s")

        xt_free = [None, None]
        w("vector", tk_a0)
        tk_ax = tk_sw = tk_q = tk_aw = tk_aw128 = None
        tk_sum_prev = None
        xnm_free_w = [None, None]
        for g in range(16):
            sl = g % 2
            w("sync", xt_free[sl])
            tk_xt = dma("sync", XT[sl][:], x_in[g * 128:(g + 1) * 128, :], "x")
            w("scalar", tk_xt, tk_sum_prev, tk_ln)
            tk_sq = inc("scalar", lambda e, sl=sl: e.square(QA[:], XT[sl][:]), "s")
            w("vector", tk_xt, tk_cst)
            tk_vs = inc("vector", lambda e, sl=sl: e.reduce_sum(
                SC[:, 1:2], XT[sl][:], axis=mybir.AxisListType.X), "v")
            w("vector", tk_sq)
            tk_sum_prev = inc("vector", lambda e: e.reduce_sum(
                SC[:, 11:12], QA[:], axis=mybir.AxisListType.X), "v")
            drain("vector")
            S.emit("vector", lambda e: e.tensor_scalar_mul(SC[:, 12:13], SC[:, 1:2], 1.0 / DM))
            drain("vector")
            S.emit("vector", lambda e, g=g: e.tensor_scalar_mul(MN[:, g:g + 1],
                                                                SC[:, 12:13], -1.0))
            S.emit("vector", lambda e: e.tensor_mul(SC[:, 13:14], SC[:, 12:13], SC[:, 12:13]))
            S.emit("vector", lambda e: e.tensor_scalar_mul(SC[:, 14:15], SC[:, 11:12], 1.0 / DM))
            drain("vector")
            tk_var = inc("vector", lambda e: e.tensor_sub(SC[:, 15:16], SC[:, 14:15],
                                                          SC[:, 13:14]), "v")
            if 4 <= g < 12:
                j = g - 4
                w("vector", tk_wfl[j])
                S.emit("vector", lambda e, j=j: e.tensor_reduce(
                    SC[:, 9:10], XN[j % 2][:], axis=mybir.AxisListType.X, op=OP.max,
                    apply_absolute_value=True))
                drain("vector")
                wco, wci = (3, 6) if j % 2 == 0 else (6, 3)
                tk_wm = inc("vector", lambda e, wco=wco, wci=wci: e.tensor_max(
                    SC[:, wco:wco + 1], SC[:, wci:wci + 1], SC[:, 9:10]), "a")
                if j >= 6:
                    xnm_free_w[j % 2] = tk_wm
                drain("vector")
            w("scalar", tk_var)
            tk_sd = inc("scalar", lambda e: e.activation(out=SC[:, 11:12], in_=SC[:, 15:16],
                                                         func=AF.Sqrt, bias=cst[:, 0:1]), "s")
            w("vector", tk_sd)
            S.emit("vector", lambda e, g=g: e.reciprocal(MI[:, g:g + 1], SC[:, 11:12]))
            drain("vector")
            tk_xn_v = inc("vector", lambda e, g=g, sl=sl: e.tensor_scalar(
                QB[:], XT[sl][:], MN[:, g:g + 1], MI[:, g:g + 1],
                OP.add, OP.mult), "v")
            drain("vector")
            S.emit("vector", lambda e: e.tensor_reduce(
                SC[:, 1:2], QB[:], axis=mybir.AxisListType.X, op=OP.max,
                apply_absolute_value=True))
            drain("vector")
            xco, xci = (2, 0) if g % 2 == 0 else (0, 2)
            tk_ax = inc("vector", lambda e, xco=xco, xci=xci: e.tensor_max(
                SC[:, xco:xco + 1], SC[:, xci:xci + 1], SC[:, 1:2]), "b")
            drain("vector")
            if g == 5:
                w("scalar", tk_hm)
                tk_he = inc("scalar", lambda e: e.copy(out=XPT[:],
                                                       in_=P[3][:, 0:EC * 3]), "s")
            emit_dg(6)
            xt_free[sl] = (tk_xn_v, tk_sq)
            if g == 11:
                w("tensor", tk_wm, tk_idf)
                tk_tw = inc("tensor", lambda e: e.transpose(
                    P[0][0:1, 128:256], SC[:, 6:7], identf[:]), "t")
                w("vector", tk_tw)
                S.emit("vector", lambda e: e.tensor_reduce(
                    SC[0:1, 1:2], P[0][0:1, 128:256], axis=mybir.AxisListType.X, op=OP.max))
                drain("vector")
                tk_aw = inc("vector", lambda e: e.tensor_copy(VARR[0:1, 0:1], SC[0:1, 1:2]), "v")
                w("tensor", tk_aw)
                tk_bc2 = inc("tensor", lambda e: e.matmul(P[1][:, 1:2], onesr[:],
                                                          VARR[0:1, 0:1],
                                                          start=True, stop=True), "t")
                w("vector", tk_bc2)
                tk_aw128 = inc("vector", lambda e: e.tensor_copy(SC[:, 7:8], P[1][:, 1:2]), "v")
                drain("vector")
                S.emit("vector", lambda e: e.tensor_scalar_max(SC[:, 1:2], SC[:, 7:8], 1e-12))
                drain("vector")
                S.emit("vector", lambda e: e.reciprocal(SC[:, 11:12], SC[:, 1:2]))
                drain("vector")
                S.emit("vector", lambda e: e.tensor_scalar(SC[:, 12:13], SC[:, 11:12],
                                                           448.0, 448.0, OP.mult, OP.min))
                drain("vector")
                S.emit("vector", lambda e: e.tensor_scalar_mul(SC[:, 8:9], SC[:, 12:13], 0.5))
                tk_sw = inc("vector", lambda e: e.reciprocal(SC[:, 9:10], SC[:, 12:13]), "v")
                drain("vector")
            if g == 0:
                w("vector", tk_hx, tk_hsq)
                S.emit("vector", lambda e: e.reduce_sum(HCOL[0:3, 0:1], XS[0:3, hx0:hx0 + DM],
                                                        axis=mybir.AxisListType.X))
                S.emit("vector", lambda e: e.reduce_sum(HCOL[0:3, 1:2], XS[0:3, hs0:hs0 + DM],
                                                        axis=mybir.AxisListType.X))
                drain("vector")
                S.emit("vector", lambda e: e.tensor_scalar_mul(HCOL[0:3, 2:3], HCOL[0:3, 0:1],
                                                               1.0 / DM))
                drain("vector")
                S.emit("vector", lambda e: e.tensor_mul(HCOL[0:3, 0:1], HCOL[0:3, 2:3],
                                                        HCOL[0:3, 2:3]))
                S.emit("vector", lambda e: e.tensor_scalar_mul(HCOL[0:3, 3:4], HCOL[0:3, 1:2],
                                                               1.0 / DM))
                drain("vector")
                tk_hv = inc("vector", lambda e: e.tensor_sub(HCOL[0:3, 1:2], HCOL[0:3, 3:4],
                                                             HCOL[0:3, 0:1]), "v")
                w("scalar", tk_hv)
                tk_hsd = inc("scalar", lambda e: e.activation(
                    out=HCOL[0:3, 3:4], in_=HCOL[0:3, 1:2],
                    func=AF.Sqrt, bias=cst[0:3, 0:1]), "s")
                w("vector", tk_hsd)
                S.emit("vector", lambda e: e.reciprocal(HCOL[0:3, 0:1], HCOL[0:3, 3:4]))
                drain("vector")
                S.emit("vector", lambda e: e.tensor_scalar_mul(HCOL[0:3, 1:2],
                                                               HCOL[0:3, 2:3], -1.0))
                drain("vector")
                tk_hap = inc("vector", lambda e: e.tensor_scalar(
                    XN[1][0:3, :], XS[0:3, hx0:hx0 + DM],
                    HCOL[0:3, 1:2], HCOL[0:3, 0:1], OP.add, OP.mult), "v")
                drain("vector")
                w("tensor", tk_hap, tk_idf)
                for f in range(KD):
                    tk_ht = inc("tensor", lambda e, f=f: e.transpose(
                        P[2][:, f * 3:(f + 1) * 3], XN[1][0:3, f * 128:(f + 1) * 128],
                        identf[0:3, 0:3]), "h")
                w("scalar", tk_ht)
                tk_hf = inc("scalar", lambda e: e.copy(out=HFM[:], in_=P[2][:, 0:24]), "s")
                w("tensor", tk_hf, tk_w1)
                for ec in range(EC):
                    for kc in range(KD):
                        tk_hm = inc("tensor", lambda e, ec=ec, kc=kc: e.matmul(
                            P[3][:, ec * 3:(ec + 1) * 3],
                            W1[:, kc * E + ec * 128: kc * E + ec * 128 + 128],
                            HFM[:, kc * 3:(kc + 1) * 3],
                            start=(ec == 0 and kc == 0), stop=(kc == 7),
                            skip_group_check=True), "t")
        tk_x_pre = [None, None]
        for s_ in range(2):
            w("sync", xt_free[s_])
            tk_x_pre[s_] = dma("sync", XT[s_][:], x_in[s_ * 128:(s_ + 1) * 128, :], "x")
            xt_free[s_] = None
        w("tensor", tk_ax, tk_idf)
        tk_tr = inc("tensor", lambda e: e.transpose(P[0][0:1, 0:128], SC[:, 0:1], identf[:]), "t")
        w("vector", tk_tr)
        S.emit("vector", lambda e: e.tensor_reduce(SC[0:1, 1:2], P[0][0:1, 0:128],
                                                   axis=mybir.AxisListType.X, op=OP.max))
        drain("vector")
        tk_axs = inc("vector", lambda e: e.tensor_copy(VARR[0:1, 0:1], SC[0:1, 1:2]), "v")
        w("tensor", tk_axs)
        tk_bc = inc("tensor", lambda e: e.matmul(P[1][:, 0:1], onesr[:], VARR[0:1, 0:1],
                                                 start=True, stop=True), "t")
        w("vector", tk_bc, tk_small)
        tk_ax128 = inc("vector", lambda e: e.tensor_copy(SC[:, 2:3], P[1][:, 0:1]), "v")
        drain("vector")
        tk_p1 = inc("vector", lambda e: e.tensor_scalar(PAY1[:], M8, SC[:, 2:3],
                                                        None, OP.mult), "v")
        w("sync", tk_p1)
        tk_ccin = dma("sync", cc1i.ap(), PAY1[:], "x")
        w("gpsimd", tk_ccin)
        tk_cc1 = inc("gpsimd", lambda e: e.collective_compute(
            "AllReduce", OP.add, replica_groups=[list(range(NCORE))],
            ins=[cc1i.ap().opt()], outs=[cc1o.ap().opt()]), "c")
        w("gpsimd", tk_cc1)
        w("sync", tk_cc1)
        tk_g1 = dma("sync", G1[:], cc1o.ap(), "x")
        tk_w3b = dma("gpsimd", W3.ap()[:, 8 * DM:16 * DM].rearrange(
            "p (ec c) -> p ec c", c=DM),
            w_out[1024:2048, :].rearrange("(ec p) c -> p ec c", p=128), "w")
        xnm_free = [xnm_free_w[0], (xnm_free_w[1], tk_ht)]

        tl = {"g2": None, "cc2": None, "cs": [None] * NTI, "p2": None}

        def emit_tail_prefix():
            w("vector", tk_scan_done[NTI - 2])
            S.emit("vector", lambda e: e.memset(LL[:, 0:1], 0.0))
            S.emit("vector", lambda e: e.memset(PP[:, 0:1], 1.0))
            for q in range(NTI - 1):
                ec_ = (q + 1) * TT - 1
                S.emit("vector", lambda e, q=q, ec_=ec_: e.tensor_copy(
                    EQF[:, 2 * q:2 * q + 1], A_loc[:, ec_:ec_ + 1]))
                S.emit("vector", lambda e, q=q, ec_=ec_: e.tensor_copy(
                    EQF[:, 2 * q + 1:2 * q + 2], B_loc[:, ec_:ec_ + 1]))
            drain("vector")
            for q in range(NTI - 1):
                S.emit("vector", lambda e, q=q: e.scalar_tensor_tensor(
                    out=LL[:, q + 1:q + 2], in0=LL[:, q:q + 1],
                    scalar=EQF[:, 2 * q:2 * q + 1],
                    in1=EQF[:, 2 * q + 1:2 * q + 2], op0=OP.mult, op1=OP.add))
                S.emit("vector", lambda e, q=q: e.tensor_scalar(
                    PP[:, q + 1:q + 2], PP[:, q:q + 1], EQF[:, 2 * q:2 * q + 1],
                    None, OP.mult))
                drain("vector")

        def emit_tail_cc():
            q = NTI - 1
            ec_ = (q + 1) * TT - 1
            w("vector", tk_scan_done[NTI - 1])
            S.emit("vector", lambda e, q=q, ec_=ec_: e.tensor_copy(
                EQF[:, 2 * q:2 * q + 1], A_loc[:, ec_:ec_ + 1]))
            S.emit("vector", lambda e, q=q, ec_=ec_: e.tensor_copy(
                EQF[:, 2 * q + 1:2 * q + 2], B_loc[:, ec_:ec_ + 1]))
            drain("vector")
            S.emit("vector", lambda e, q=q: e.scalar_tensor_tensor(
                out=LL[:, q + 1:q + 2], in0=LL[:, q:q + 1],
                scalar=EQF[:, 2 * q:2 * q + 1],
                in1=EQF[:, 2 * q + 1:2 * q + 2], op0=OP.mult, op1=OP.add))
            S.emit("vector", lambda e, q=q: e.tensor_scalar(
                PP[:, q + 1:q + 2], PP[:, q:q + 1], EQF[:, 2 * q:2 * q + 1],
                None, OP.mult))
            drain("vector")
            for j in range(NCORE):
                S.emit("vector", lambda e, j=j: e.tensor_copy(
                    PAY2[:, 2 * j:2 * j + 1], PP[:, NTI:NTI + 1]))
                S.emit("vector", lambda e, j=j: e.tensor_copy(
                    PAY2[:, 2 * j + 1:2 * j + 2], LL[:, NTI:NTI + 1]))
            drain("vector")
            tl["p2"] = inc("vector", lambda e: e.tensor_mul(PAY2b[:], PAY2[:], M16), "v")
            w("sync", tl["p2"])
            tk_cc2in = dma("sync", cc2i.ap(), PAY2b[:], "x")
            w("gpsimd", tk_cc2in)
            tl["cc2"] = inc("gpsimd", lambda e: e.collective_compute(
                "AllReduce", OP.add, replica_groups=[list(range(NCORE))],
                ins=[cc2i.ap().opt()], outs=[cc2o.ap().opt()]), "c")
            w("gpsimd", tl["cc2"])
            w("sync", tl["cc2"])
            tl["g2"] = dma("sync", G2[:], cc2o.ap(), "x")

        def emit_tail_schain():
            w("vector", tl["g2"])
            S.emit("vector", lambda e: e.memset(SS[:, 0:1], 0.0))
            drain("vector")
            for j in range(NCORE - 1):
                S.emit("vector", lambda e, j=j: e.scalar_tensor_tensor(
                    out=SS[:, j + 1:j + 2], in0=SS[:, j:j + 1],
                    scalar=G2[:, 2 * j:2 * j + 1],
                    in1=G2[:, 2 * j + 1:2 * j + 2], op0=OP.mult, op1=OP.add))
                drain("vector")
            S.emit("vector", lambda e: e.tensor_mul(SELT[:], SS[:, 0:NCORE], SEL))
            drain("vector")
            S.emit("vector", lambda e: e.tensor_add(SELT[:, 0:4], SELT[:, 0:4], SELT[:, 4:8]))
            drain("vector")
            S.emit("vector", lambda e: e.tensor_add(SELT[:, 0:2], SELT[:, 0:2], SELT[:, 2:4]))
            drain("vector")
            S.emit("vector", lambda e: e.tensor_add(SIN[:], SELT[:, 0:1], SELT[:, 1:2]))
            drain("vector")
            for q in range(NTI):
                S.emit("vector", lambda e, q=q: e.scalar_tensor_tensor(
                    out=SST[:, q:q + 1], in0=PP[:, q:q + 1], scalar=SIN[:],
                    in1=LL[:, q:q + 1], op0=OP.mult, op1=OP.add))
            drain("vector")

        def emit_tail_q(q):
            sl0 = q * TT
            S.emit("vector", lambda e, q=q, sl0=sl0: e.scalar_tensor_tensor(
                out=SCB[:], in0=A_loc[:, sl0:sl0 + TT], scalar=SST[:, q:q + 1],
                in1=B_loc[:, sl0:sl0 + TT], op0=OP.mult, op1=OP.add))
            drain("vector")
            tl["cs"][q] = inc("vector", lambda e, sl0=sl0: e.tensor_mul(
                A_loc[:, sl0:sl0 + TT], SCB[:], C_loc[:, sl0:sl0 + TT]), "v")
            drain("vector")

        def emit_tail_ymm(tk_pf):
            regions = [(0, 0), (0, TT), (1, 0), (1, TT)]
            tk_ymm = [None] * NTI
            tk_ye = [None, None, None, None]
            for rnd in range(2):
                w("tensor", tk_pf)
                if rnd == 1:
                    w("tensor", tk_ye[0], tk_ye[1], tk_ye[2], tk_ye[3])
                for i4 in range(4):
                    q = rnd * 4 + i4
                    b, c0 = regions[i4]
                    w("tensor", tl["cs"][q])
                    tk_ymm[q] = inc("tensor", lambda e, q=q, b=b, c0=c0: e.matmul(
                        P[b][0:1, c0:c0 + TT], onesbf[:], A_loc[:, q * TT:(q + 1) * TT],
                        start=(i4 % 2 == 0), stop=True, skip_group_check=True), "t")
                for i4 in range(4):
                    q = rnd * 4 + i4
                    b, c0 = regions[i4]
                    w("scalar", tk_ymm[q])
                    tk_ye[i4] = inc("scalar", lambda e, q=q, b=b, c0=c0: e.copy(
                        out=C_loc[0:1, q * TT:(q + 1) * TT],
                        in_=P[b][0:1, c0:c0 + TT]), "s")
        pfree = {0: tk_axs, 1: tk_ax128, 2: tk_hf, 3: tk_he,
                 4: None, 5: None, 6: None, 7: None}
        tk_store = [None] * 16
        tk_xs_free = None
        tk_xnb_free = None
        tk_xq_free = None
        tk_drb_free = None
        tk_szb_free = [None, None]
        tk_ynbf_free = [None, None]
        tk_ygb_free = [None, None, None, None]
        tk_xsq_free = [None, None]
        tk_gt_free = [None, None]
        tk_gf_free = [None, None]
        tk_sgb_free = [None, None]
        tk_sca_free = None
        tk_qab_prev = None
        tk_scan_done = [None] * NTI

        tk_xn_next = [None, None]
        for s_ in range(2):
            w("vector", tk_x_pre[s_], xnm_free[s_])
            tk_xn_next[s_] = inc("vector", lambda e, s_=s_: e.tensor_scalar(
                XN[s_][:], XT[s_][:], MN[:, s_:s_ + 1], MI[:, s_:s_ + 1],
                OP.add, OP.mult), "v")
            xt_free[s_] = tk_xn_next[s_]
        tk_x_next = [None, None]

        for ti in range(NTI):
            g0 = ti * 2
            tk_xn = list(tk_xn_next)
            tk_acc = [None, None]
            tk_tp = [None] * 4
            for s_ in range(2):
                w("tensor", tk_xn[s_])
                for half in range(2):
                    bank = 4 + s_ * 2 + half
                    w("tensor", pfree[bank])
                    for f4 in range(4):
                        f = half * 4 + f4
                        tk_tp[bank - 4] = inc("tensor", lambda e, s_=s_, f=f, bank=bank, f4=f4:
                                              e.transpose(P[bank][:, f4 * 128:(f4 + 1) * 128],
                                                          XN[s_][:, f * 128:(f + 1) * 128],
                                                          identf[:]), "t")
            def emit_sx_chain():
                w("vector", tk_g1)
                S.emit("vector", lambda e: e.tensor_reduce(
                    SC[:, 3:4], G1[:], axis=mybir.AxisListType.X, op=OP.max))
                drain("vector")
                S.emit("vector", lambda e: e.tensor_scalar_max(SC[:, 1:2], SC[:, 3:4], 1e-12))
                drain("vector")
                S.emit("vector", lambda e: e.reciprocal(SC[:, 11:12], SC[:, 1:2]))
                drain("vector")
                S.emit("vector", lambda e: e.tensor_scalar(SC[:, 12:13], SC[:, 11:12],
                                                           448.0, 448.0, OP.mult, OP.min))
                drain("vector")
                S.emit("vector", lambda e: e.tensor_scalar_mul(SC[:, 4:5], SC[:, 12:13], 0.5))
                tk = inc("vector", lambda e: e.reciprocal(SC[:, 5:6], SC[:, 12:13]), "sx")
                drain("vector")
                S.emit("vector", lambda e: e.tensor_scalar_mul(SC[:, 11:12], SC[:, 5:6], 4.0))
                drain("vector")
                tk4 = inc("vector", lambda e: e.tensor_mul(SC[:, 10:11], SC[:, 11:12],
                                                           SC[:, 9:10]), "v")
                drain("vector")
                return tk, tk4
            tk_xnb = [None, None]
            tk_xq = [None, None]
            for s_ in range(2):
                w("scalar", tk_tp[s_ * 2], tk_tp[s_ * 2 + 1], tk_xnb_free)
                for half in range(2):
                    bank = 4 + s_ * 2 + half
                    tk_xnb[s_] = inc("scalar", lambda e, s_=s_, half=half, bank=bank: e.copy(
                        out=XNB[:, s_ * DM + half * 512: s_ * DM + half * 512 + 512],
                        in_=P[bank][:]), "s")
            def emit_xq():
                for s_ in range(2):
                    w("scalar", ("sx", semsx, 1), tk_xq_free)
                    for half in range(2):
                        bank = 4 + s_ * 2 + half
                        tk_xq[s_] = inc("scalar", lambda e, s_=s_, half=half, bank=bank:
                                        e.activation(
                            out=XQ[:, s_ * DM + half * 512: s_ * DM + half * 512 + 512],
                            in_=P[bank][:], func=AF.Identity, scale=SC[:, 4:5]), "s")
                return (tk_xq[0], tk_xq[1])
            if ti > 0:
                pf47 = emit_xq()

            w("tensor", tk_xnb[0], tk_xnb[1], pfree[2], tk_wb0)
            for kc in range(KD):
                tk_b0 = inc("tensor", lambda e, kc=kc: e.matmul(
                    P[2][0:1, 0:TT], wb0[:, kc:kc + 1], xnb3[:, :, kc * 128:(kc + 1) * 128],
                    start=(kc == 0), stop=(kc == 7)), "t")
            w("scalar", tk_b0)
            tk_b0r = inc("scalar", lambda e: e.copy(out=B0R[:], in_=P[2][0:1, 0:TT]), "s")
            pfree[2] = tk_b0r

            tk_proj = [None] * EC
            tk_xpev = [None] * EC
            tk_silu = [None] * EC
            tk_xsq = [None] * EC
            tk_convd = [None] * EC
            tk_cdr = [None] * EC

            def emit_proj(e_, ti=ti):
                bank = e_ % 2
                w("tensor", pfree[bank])
                if e_ >= 2:
                    w("tensor", tk_xpev[e_ - 2])
                for kc in range(KD):
                    tk = inc("tensor", lambda e, kc=kc, e_=e_, bank=bank: e.matmul(
                        P[bank][0:128, 0:TT],
                        W1[:, kc * E + e_ * 128: kc * E + e_ * 128 + 128],
                        xnb3[:, :, kc * 128:(kc + 1) * 128],
                        start=(kc == 0), stop=(kc == 7)), "t")
                tk_proj[e_] = tk

            def emit_xpev(e_):
                hb = e_ % 2
                base = hb * 260
                w("vector", tk_proj[e_])
                if e_ >= 2:
                    w("vector", tk_convd[e_ - 2])
                if ti == 0:
                    w("vector", tk_he)
                S.emit("vector", lambda e, e_=e_, base=base: e.tensor_copy(
                    out=XPB[:, base + 1:base + 4], in_=XPT[:, e_ * 3:(e_ + 1) * 3]))
                S.emit("vector", lambda e, e_=e_, base=base, hb=hb: e.tensor_copy(
                    out=XPB[:, base + 4:base + 260], in_=P[hb][0:128, 0:TT]))
                drain("vector")
                tk_xpev[e_] = inc("vector", lambda e, e_=e_, base=base: e.tensor_copy(
                    out=XPT[:, e_ * 3:(e_ + 1) * 3], in_=XPB[:, base + 257:base + 260]), "v")
                drain("vector")

            def emit_conv(e_):
                hb = e_ % 2
                base = hb * 260
                bank = 2 + e_ % 2
                w("tensor", tk_xpev[e_], pfree[bank])
                for k in range(4):
                    j = e_ * 4 + k
                    tk = inc("tensor", lambda e, j=j, k=k, base=base, bank=bank: e.matmul(
                        P[bank][0:128, 0:TT], DG[:, j * 128:(j + 1) * 128],
                        XPB[:, base + 1 + k: base + 1 + k + TT],
                        start=(k == 0), stop=(k == 3)), "t")
                tk_convd[e_] = tk

            def emit_silu(e_, ti=ti):
                bank = 2 + e_ % 2
                w("scalar", tk_convd[e_], tk_xs_free)
                if ti == 0 and e_ == 8:
                    w("scalar", tk_hap)
                tk = inc("scalar", lambda e, e_=e_, bank=bank: e.activation(
                    out=XS[:, e_ * TT:(e_ + 1) * TT], in_=P[bank][0:128, 0:TT],
                    func=AF.Silu, bias=CB[:, e_:e_ + 1]), "s")
                if e_ == 0:
                    tk = inc("scalar", lambda e, bank=bank: e.activation(
                        out=XS0R[:], in_=P[bank][0:1, 0:TT],
                        func=AF.Silu, bias=CB[0:1, 0:1]), "s")
                tk_silu[e_] = tk
                pfree[bank] = tk

            def emit_xsq(e_):
                w("vector", tk_silu[e_])
                if e_ >= 4:
                    w("vector", tk_stats_q[(e_ - 4) // 4])
                elif ti > 0:
                    w("vector", tk_xsq_free[0])
                tk_xsq[e_] = inc("vector", lambda e, e_=e_: e.tensor_mul(
                    XSQ2[:, (e_ % 4) * TT:(e_ % 4 + 1) * TT], XS[:, e_ * TT:(e_ + 1) * TT],
                    XS[:, e_ * TT:(e_ + 1) * TT]), "v")

            def emit_stats_half(r0, r1):
                w("tensor", tk_silu[r1 - 1], tk_xsq[r1 - 1], tk_wx)
                if r0 == 0:
                    w("tensor", pf47[0], pf47[1])
                tk = None
                for e_ in range(r0, r1):
                    inc("tensor", lambda e, e_=e_: e.matmul(
                        P[6][0:1, 0:TT], Dm1[:, e_:e_ + 1], XS[:, e_ * TT:(e_ + 1) * TT],
                        start=(e_ == 0), stop=(e_ == EC - 1)), "t")
                    tk = inc("tensor", lambda e, e_=e_: e.matmul(
                        P[7][0:1, 0:TT], Dm2[:, e_:e_ + 1],
                        XSQ2[:, (e_ % 4) * TT:(e_ % 4 + 1) * TT],
                        start=(e_ == 0), stop=(e_ == EC - 1)), "t")
                    inc("tensor", lambda e, e_=e_: e.matmul(
                        P[5][0:128, 0:TT], Wxs[:, e_ * 192 + 64:(e_ + 1) * 192],
                        XS[:, e_ * TT:(e_ + 1) * TT],
                        start=(e_ == 0), stop=(e_ == EC - 1)), "t")
                    tk_cdr[e_] = inc("tensor", lambda e, e_=e_: e.matmul(
                        P[4][0:64, 0:TT], Wxs[:, e_ * 192: e_ * 192 + 64],
                        XS[:, e_ * TT:(e_ + 1) * TT],
                        start=(e_ == 0), stop=(e_ == EC - 1)), "t")
                return tk
            tk_stats_q = [None, None, None, None]
            for step in range(EC + 4):
                if ti == 0 and step == 5:
                    tk_sx, tk_s4 = emit_sx_chain()
                    pf47 = emit_xq()
                if ti == NTI - 1 and step == 10:
                    emit_tail_prefix()
                if ti == 0 and 1 <= step <= EC:
                    for qq in range(2):
                        qt = (step - 1) * 2 + qq
                        kc, q4 = qt // 4, qt % 4
                        w("scalar", tk_wql[qt], tk_sw)
                        tk_q = inc("scalar", lambda e, kc=kc, q4=q4, qt=qt:
                                   e.activation(
                            out=Wq[:, kc * DM + q4 * 256: kc * DM + (q4 + 1) * 256],
                            in_=qslot[qt % 8], func=AF.Identity,
                            scale=SC[:, 8:9]), "q")
                        if qt + 8 < 32:
                            kc2, q42 = (qt + 8) // 4, (qt + 8) % 4
                            w("scalar", tk_q)
                            tk_wql[qt + 8] = dma(
                                "scalar", qslot[(qt + 8) % 8],
                                w_ft[kc2 * 128:(kc2 + 1) * 128,
                                     q42 * 256:(q42 + 1) * 256], "l")
                if ti < NTI - 1 and step == 8:
                    for s_ in range(2):
                        g = g0 + 2 + s_
                        sl = g % 2
                        w("sync", xt_free[sl])
                        tk_x_next[s_] = dma("sync", XT[sl][:],
                                            x_in[g * 128:(g + 1) * 128, :], "x")
                if ti < NTI - 1 and step == 12:
                    for s_ in range(2):
                        g = g0 + 2 + s_
                        sl = g % 2
                        w("vector", tk_x_next[s_], tk_tp[s_ * 2], tk_tp[s_ * 2 + 1])
                        tk_xn_next[s_] = inc("vector", lambda e, g=g, sl=sl, s_=s_:
                                             e.tensor_scalar(
                            XN[s_][:], XT[sl][:], MN[:, g:g + 1], MI[:, g:g + 1],
                            OP.add, OP.mult), "v")
                        xt_free[sl] = tk_xn_next[s_]
                if step < EC:
                    emit_proj(step)
                    emit_xpev(step)
                if 1 <= step <= EC:
                    emit_conv(step - 1)
                    emit_silu(step - 1)
                if 4 <= step <= EC + 3:
                    emit_xsq(step - 4)
                if step in (7, 11, 15):
                    tk_stats_q[(step - 7) // 4] = emit_stats_half(step - 7, step - 3)
            emit_stats_half(12, EC)
            tk_eloop = tk_cdr[EC - 1]
            tk_xsq_free[0] = tk_eloop

            w("scalar", tk_eloop, tk_drb_free)
            tk_drb = inc("scalar", lambda e: e.copy(out=DRB[:], in_=P[4][0:64, 0:TT]), "s")
            pfree[4] = tk_drb
            tk_cl = inc("scalar", lambda e, ti=ti: e.copy(
                out=C_loc[:, ti * TT:(ti + 1) * TT], in_=P[5][0:128, 0:TT]), "s")
            pfree[5] = tk_cl
            tk_m1r = inc("scalar", lambda e: e.copy(out=M1R[:], in_=P[6][0:1, 0:TT]), "s")
            tk_m2r = inc("scalar", lambda e: e.copy(out=M2R[:], in_=P[7][0:1, 0:TT]), "s")
            pfree[6] = tk_m1r
            pfree[7] = tk_m2r
            w("vector", tk_m1r, tk_m2r)
            S.emit("vector", lambda e: e.tensor_mul(RSTR[:], M1R[:], M1R[:]))
            drain("vector")
            tk_var2 = inc("vector", lambda e: e.tensor_sub(VARR[:], M2R[:], RSTR[:]), "v")
            w("tensor", tk_drb, pfree[2], tk_wdt)
            tk_dmm = inc("tensor", lambda e: e.matmul(
                P[2][0:1, 0:TT], wdt0b[:], DRB[:], start=True, stop=True), "t")
            w("scalar", tk_dmm)
            tk_dltev = inc("scalar", lambda e: e.copy(out=T0R[:], in_=P[2][0:1, 0:TT]), "s")
            pfree[2] = tk_dltev
            drain("scalar")
            S.emit("scalar", lambda e: e.activation(out=T1R[:], in_=T0R[:], func=AF.Abs,
                                                    bias=BDT))
            w("scalar", tk_b0r)
            tk_habs = inc("scalar", lambda e: e.activation(out=BM0R[:], in_=B0R[:],
                                                           func=AF.Abs), "s")
            drain("scalar")
            S.emit("scalar", lambda e: e.activation(out=DLTR[:], in_=T1R[:], func=AF.Exp,
                                                    scale=-1.0))
            S.emit("scalar", lambda e: e.activation(out=BDR[:], in_=BM0R[:], func=AF.Exp,
                                                    scale=-1.0))
            w("scalar", tk_var2)
            S.emit("scalar", lambda e: e.activation(out=M2R[:], in_=VARR[:], func=AF.Ln,
                                                    bias=cst[0:1, 0:1]))
            drain("scalar")
            tk_sp1 = inc("scalar", lambda e: e.activation(out=T1R[:], in_=DLTR[:],
                                                          func=AF.Ln, bias=1.0), "s")
            tk_sp2 = inc("scalar", lambda e: e.activation(out=BM0R[:], in_=BDR[:],
                                                          func=AF.Ln, bias=1.0), "s")
            tk_rst = inc("scalar", lambda e: e.activation(out=RSTR[:], in_=M2R[:], func=AF.Exp,
                                                          scale=cst[0:1, 1:2]), "s")
            drain("scalar")

            tk_zmm = [None] * EC
            tk_sz = [None] * EC
            tk_yg = [None] * EC
            tk_wout = [None] * EC
            tk_m1b = tk_rsb = None
            tk_dltev = tk_dl1 = tk_bd = tk_aev = tk_bev = None
            tk_ynf1_free = [None]

            def emit_z(e_):
                bank = e_ % 2
                if e_ < 2:
                    w("tensor", tk_xpev[EC - 2 + e_], pfree[e_ % 2])
                else:
                    w("tensor", tk_sz[e_ - 2])
                w("tensor", tk_w2)
                for kc in range(KD):
                    tk = inc("tensor", lambda e, kc=kc, e_=e_, bank=bank: e.matmul(
                        P[bank][0:128, 0:TT],
                        W2[:, kc * E + e_ * 128: kc * E + e_ * 128 + 128],
                        xnb3[:, :, kc * 128:(kc + 1) * 128],
                        start=(kc == 0), stop=(kc == 7)), "t")
                tk_zmm[e_] = tk

            def emit_ychain(e_):
                bank = e_ % 2
                sl = e_ % 2
                gsl = e_ % 4
                w("scalar", tk_zmm[e_], tk_szb_free[sl])
                S.emit("scalar", lambda e, bank=bank: e.activation(
                    out=TNH[:], in_=P[bank][0:128, 0:TT], func=AF.Tanh,
                    scale=cst[:, 2:3]))
                drain("scalar")
                tk_sz[e_] = inc("scalar", lambda e, sl=sl: e.activation(
                    out=SZB[:, sl * TT:(sl + 1) * TT], in_=TNH[:],
                    func=AF.Identity, scale=cst[:, 2:3], bias=cst[:, 2:3]), "s")
                drain("scalar")
                w("vector", tk_m1b, tk_rsb, tk_eloop, tk_ln)
                S.emit("vector", lambda e, e_=e_: e.scalar_tensor_tensor(
                    out=YNF[0][:], in0=XS[:, e_ * TT:(e_ + 1) * TT], scalar=DD[:, e_:e_ + 1],
                    in1=M1B[:], op0=OP.mult, op1=OP.subtract))
                drain("vector")
                w("vector", tk_ynf1_free[0])
                tk_stt2 = inc("vector", lambda e, e_=e_: e.scalar_tensor_tensor(
                    out=YNF[1][:], in0=YNF[0][:], scalar=1.0,
                    in1=RSB[:], op0=OP.mult, op1=OP.mult), "v")
                drain("vector")
                w("scalar", tk_stt2, tk_ynbf_free[sl])
                tk_yn = inc("scalar", lambda e, e_=e_, sl=sl: e.activation(
                    out=YNBF[:, sl * TT:(sl + 1) * TT], in_=YNF[1][:], func=AF.Identity,
                    scale=LNG[:, e_:e_ + 1], bias=LNB[:, e_:e_ + 1]), "s")
                w("vector", tk_sz[e_], tk_yn, tk_ygb_free[gsl])
                tk_yg[e_] = inc("vector", lambda e, sl=sl, gsl=gsl: e.tensor_mul(
                    YGB[:, gsl * TT:(gsl + 1) * TT], YNBF[:, sl * TT:(sl + 1) * TT],
                    SZB[:, sl * TT:(sl + 1) * TT]), "v")
                tk_ynbf_free[sl] = tk_yg[e_]
                tk_szb_free[sl] = tk_yg[e_]
                tk_ynf1_free[0] = tk_yn

            def emit_wout(e_):
                gsl = e_ % 4
                w("tensor", tk_yg[e_], tk_w3a if e_ < 8 else tk_w3b)
                for d_ in range(KD):
                    bank = 4 + d_ // 2
                    col = (d_ % 2) * TT
                    st = (e_ == 0 and d_ % 2 == 0)
                    tk = inc("tensor", lambda e, e_=e_, d_=d_, bank=bank, col=col, st=st, gsl=gsl:
                             e.matmul(P[bank][0:128, col:col + TT],
                                      W3[:, e_ * DM + d_ * 128: e_ * DM + d_ * 128 + 128],
                                      YGB[:, gsl * TT:(gsl + 1) * TT],
                                      start=st, stop=(e_ == EC - 1),
                                      skip_group_check=True), "t")
                tk_wout[e_] = tk
                tk_ygb_free[gsl] = tk

            def emit_scan_level(li, ti=ti):
                sl0 = ti * TT
                d = 1 << li
                if li == 0:
                    w("vector", tk_aev, tk_bev)
                    drain("vector")
                S.emit("vector", lambda e, sl0=sl0: e.tensor_copy(SCA[:], A_loc[:, sl0:sl0 + TT]))
                S.emit("vector", lambda e, sl0=sl0: e.tensor_copy(SCB[:], B_loc[:, sl0:sl0 + TT]))
                drain("vector")
                S.emit("vector", lambda e, sl0=sl0, d=d: e.tensor_mul(
                    A_loc[:, sl0 + d:sl0 + TT], SCA[:, d:TT], SCA[:, 0:TT - d]))
                S.emit("vector", lambda e, d=d: e.tensor_mul(
                    SCC[:, 0:TT - d], SCA[:, d:TT], SCB[:, 0:TT - d]))
                drain("vector")
                S.emit("vector", lambda e, sl0=sl0, d=d: e.tensor_add(
                    B_loc[:, sl0 + d:sl0 + TT], SCB[:, d:TT], SCC[:, 0:TT - d]))
                drain("vector")
                if li == 7:
                    tk_scan_done[ti] = inc("vector", lambda e, sl0=sl0: e.tensor_copy(
                        SCA[:, 0:1], A_loc[:, sl0:sl0 + 1]), "v")

            for step in range(EC + 3):
                if 2 <= step < EC + 2:
                    emit_ychain(step - 2)
                if 11 <= step < 19:
                    emit_scan_level(step - 11)
                if step < EC:
                    emit_z(step)
                if step == 0:
                    w("tensor", tk_m1r, pfree[2])
                    tk_m1bc = inc("tensor", lambda e: e.matmul(
                        P[2][0:128, 0:TT], onesr[:], M1R[:], start=True, stop=True), "t")
                    w("vector", tk_m1bc)
                    tk_m1b = inc("vector", lambda e: e.tensor_copy(M1B[:], P[2][0:128, 0:TT]), "v")
                    pfree[2] = tk_m1b
                if step == 1:
                    w("tensor", tk_rst, pfree[3])
                    tk_rsbc = inc("tensor", lambda e: e.matmul(
                        P[3][0:128, 0:TT], onesr[:], RSTR[:], start=True, stop=True), "t")
                    w("vector", tk_rsbc)
                    tk_rsb = inc("vector", lambda e: e.tensor_copy(RSB[:], P[3][0:128, 0:TT]), "v")
                    pfree[3] = tk_rsb
                if step == 5:
                    w("vector", tk_sp1)
                    S.emit("vector", lambda e: e.tensor_scalar(DLTR[:], T0R[:], BDT,
                                                               0.0, OP.add, OP.max))
                    drain("vector")
                    S.emit("vector", lambda e: e.tensor_add(T0R[:], DLTR[:], T1R[:]))
                    drain("vector")
                    tk_dl1 = inc("vector", lambda e: e.tensor_scalar(DLTR[:], T0R[:], 1e-6,
                                                                     100.0, OP.max, OP.min), "v")
                    drain("vector")
                if step == 9:
                    w("vector", tk_sp2, tk_dl1)
                    S.emit("vector", lambda e: e.tensor_scalar_max(BDR[:], B0R[:], 0.0))
                    drain("vector")
                    S.emit("vector", lambda e: e.tensor_add(T0R[:], BDR[:], BM0R[:]))
                    drain("vector")
                    w("vector", tk_silu[0])
                    S.emit("vector", lambda e: e.tensor_mul(T1R[:], DLTR[:], T0R[:]))
                    drain("vector")
                    tk_bd = inc("vector", lambda e: e.tensor_mul(BDR[:], T1R[:], XS0R[:]), "v")
                    drain("vector")
                if step == 8:
                    w("tensor", tk_dl1, pfree[2])
                    tk_abc = inc("tensor", lambda e: e.matmul(P[2][0:128, 0:TT], onesrb[:],
                                                              DLTR[:], start=True, stop=True), "t")
                    w("scalar", tk_abc, tk_a0)
                    tk_aev = inc("scalar", lambda e, ti=ti: e.activation(
                        out=A_loc[:, ti * TT:(ti + 1) * TT], in_=P[2][0:128, 0:TT],
                        func=AF.Exp, scale=A0c[:]), "s")
                    pfree[2] = tk_aev
                if step == 10:
                    w("tensor", tk_bd, pfree[3])
                    tk_bbc = inc("tensor", lambda e: e.matmul(P[3][0:128, 0:TT], onesrb[:],
                                                              BDR[:], start=True, stop=True), "t")
                    w("scalar", tk_bbc)
                    tk_bev = inc("scalar", lambda e, ti=ti: e.copy(
                        out=B_loc[:, ti * TT:(ti + 1) * TT], in_=P[3][0:128, 0:TT]), "s")
                    pfree[3] = tk_bev
                if step >= 3 and step - 3 < EC:
                    emit_wout(step - 3)
            tk_youts = tk_wout[EC - 1]
            tk_xs_free = tk_yg[EC - 1]



            tk_fmm = [None] * KD
            tk_epi = [None] * KD
            tk_gfo = [None] * KD
            tk_sg = [None] * KD
            tk_tps0 = {2: None, 3: None}
            tk_tps1 = {4: None, 5: None}
            tk_c1s = [None, None]
            tk_adds0 = [None, None]
            tk_adds1 = [None, None]

            def emit_ffn(o):
                bank = o % 2
                w("tensor", tk_xq[0], tk_xq[1], tk_q)
                if o < 2:
                    w("tensor", tk_sz[EC - 2 + o])
                else:
                    w("tensor", tk_epi[o - 2])
                for kc in range(KD):
                    tk = inc("tensor", lambda e, kc=kc, o=o, bank=bank: e.matmul(
                        P[bank][0:128, 0:TT],
                        Wq[:, kc * DM + o * 128: kc * DM + o * 128 + 128],
                        xq3[:, :, kc * 128:(kc + 1) * 128],
                        start=(kc == 0), stop=(kc == 7)), "t")
                tk_fmm[o] = tk

            def emit_epi(o):
                bank = o % 2
                sl = o % 2
                w("vector", tk_fmm[o], tk_s4, tk_gt_free[sl])
                tk_epi[o] = inc("vector", lambda e, o=o, bank=bank, sl=sl: e.tensor_scalar(
                    GT[:, sl * TT:(sl + 1) * TT], P[bank][0:128, 0:TT],
                    SC[:, 10:11], FFB[:, o:o + 1], OP.mult, OP.add), "v")
                w("scalar", tk_epi[o], tk_gf_free[sl])
                tk_gfo[o] = inc("scalar", lambda e, sl=sl: e.activation(
                    out=GF[:, sl * TT:(sl + 1) * TT], in_=GT[:, sl * TT:(sl + 1) * TT],
                    func=AF.Silu), "s")
                sbank = 4 + o // 2
                scol = (o % 2) * TT
                w("vector", tk_gfo[o], tk_youts, tk_sgb_free[sl])
                tk_sg[o] = inc("vector", lambda e, sl=sl, sbank=sbank, scol=scol: e.tensor_mul(
                    SGB[:, sl * TT:(sl + 1) * TT], GF[:, sl * TT:(sl + 1) * TT],
                    P[sbank][0:128, scol:scol + TT]), "v")
                tk_gt_free[sl] = tk_gfo[o]
                tk_gf_free[sl] = tk_sg[o]

            def emit_sgt(o):
                sl = o % 2
                b0_ = 2 if o < 4 else 3
                w("tensor", tk_sg[o])
                if o == 0:
                    w("tensor", pfree[2])
                if o == 4:
                    w("tensor", pfree[3])
                tk_tps0[b0_] = inc("tensor", lambda e, o=o, sl=sl, b0_=b0_: e.transpose(
                    P[b0_][:, (o % 4) * 128:(o % 4 + 1) * 128],
                    SGB[:, sl * TT: sl * TT + 128], identf[:]), "t")
                if o < 2:
                    w("vector", tk_tps0[b0_])
                    tk_c1 = inc("vector", lambda e, o=o, sl=sl: e.tensor_copy(
                        SG1[:, o * 128:(o + 1) * 128],
                        SGB[:, sl * TT + 128: sl * TT + 256]), "v")
                    tk_c1s[o] = tk_c1
                    tk_sgb_free[sl] = (tk_tps0[b0_], tk_c1)
                else:
                    b1_ = 4 if o < 4 else 5
                    if o == 2:
                        w("tensor", tk_sg[1], tk_c1s[0], tk_c1s[1])
                        for oo in range(2):
                            inc("tensor", lambda e, oo=oo: e.transpose(
                                P[4][:, oo * 128:(oo + 1) * 128],
                                SG1[:, oo * 128:(oo + 1) * 128], identf[:]), "t")
                    if o == 4:
                        w("tensor", tk_sg[3])
                    tk_tps1[b1_] = inc("tensor", lambda e, o=o, sl=sl, b1_=b1_: e.transpose(
                        P[b1_][:, (o % 4) * 128:(o % 4 + 1) * 128],
                        SGB[:, sl * TT + 128: sl * TT + 256], identf[:]), "t")
                    tk_sgb_free[sl] = tk_tps1[b1_]

            for step in range(KD + 2):
                if ti == NTI - 1:
                    if step == 0:
                        emit_tail_cc()
                    elif step == 6:
                        emit_tail_schain()
                    elif step == 7:
                        emit_tail_q(0)
                        emit_tail_q(1)
                    elif step == 8:
                        emit_tail_q(2)
                        emit_tail_q(3)
                    elif step == 9:
                        for q_ in range(4, NTI):
                            emit_tail_q(q_)
                if step < KD:
                    emit_ffn(step)
                if 1 <= step <= KD:
                    emit_epi(step - 1)
                if step >= 2:
                    emit_sgt(step - 2)
                if ti < NTI - 1:
                    if step - 2 == 3:
                        w("scalar", tk_tps0[2], tk_qab_prev)
                        tk_adds0[0] = inc("scalar", lambda e: e.copy(
                            out=QA[:, 0:512], in_=P[2][:, 0:512]), "s")
                        pfree[2] = tk_adds0[0]
                    if step - 2 == 7:
                        w("scalar", tk_tps0[3])
                        tk_adds0[1] = inc("scalar", lambda e: e.copy(
                            out=QA[:, 512:1024], in_=P[3][:, 0:512]), "s")
                        pfree[3] = tk_adds0[1]
                        w("scalar", tk_tps1[4])
                        tk_adds1[0] = inc("scalar", lambda e: e.copy(
                            out=QB[:, 0:512], in_=P[4][:, 0:512]), "s")
                        pfree[4] = tk_adds1[0]
                        w("scalar", tk_tps1[5])
                        tk_adds1[1] = inc("scalar", lambda e: e.copy(
                            out=QB[:, 512:1024], in_=P[5][:, 0:512]), "s")
                        pfree[5] = tk_adds1[1]
                        w("gpsimd", tk_adds0[0], tk_adds0[1])
                        tk_acc[0] = dma("gpsimd", QA[:],
                                        x_in[g0 * 128:(g0 + 1) * 128, :], "w",
                                        accum=OP.add)
                        w("gpsimd", tk_adds1[0], tk_adds1[1])
                        tk_acc[1] = dma("gpsimd", QB[:],
                                        x_in[(g0 + 1) * 128:(g0 + 2) * 128, :], "w",
                                        accum=OP.add)
                else:
                    if step - 2 == 3:
                        w("vector", tk_tps0[2], tk_qab_prev)
                        tk_adds0[0] = inc("vector", lambda e, g0=g0: e.tensor_add(
                            QA[:, 0:512], P[2][:, 0:512], XT[g0 % 2][:, 0:512]), "v")
                        pfree[2] = tk_adds0[0]
                    if step - 2 == 7:
                        w("vector", tk_tps0[3])
                        tk_adds0[1] = inc("vector", lambda e, g0=g0: e.tensor_add(
                            QA[:, 512:1024], P[3][:, 0:512], XT[g0 % 2][:, 512:1024]), "v")
                        pfree[3] = tk_adds0[1]
                        w("vector", tk_tps1[4], tk_qab_prev)
                        tk_adds1[0] = inc("vector", lambda e, g0=g0: e.tensor_add(
                            QB[:, 0:512], P[4][:, 0:512], XT[(g0 + 1) % 2][:, 0:512]), "v")
                        pfree[4] = tk_adds1[0]
                        w("vector", tk_tps1[5])
                        tk_adds1[1] = inc("vector", lambda e, g0=g0: e.tensor_add(
                            QB[:, 512:1024], P[5][:, 0:512], XT[(g0 + 1) % 2][:, 512:1024]), "v")
                        pfree[5] = tk_adds1[1]
                        tk_acc[0] = (tk_adds0[0], tk_adds0[1])
                        tk_acc[1] = (tk_adds1[0], tk_adds1[1])
            for s_ in range(2):
                g = g0 + s_
                w("sync", tk_acc[s_])
                tk_store[g] = dma("sync", y_out[g * 128:(g + 1) * 128, :],
                                  (QA if s_ == 0 else QB)[:], "o")
            if ti == NTI - 1:
                xt_free[0] = (tk_adds0[0], tk_adds0[1])
                xt_free[1] = (tk_adds1[0], tk_adds1[1])
            tk_qab_prev = (tk_store[g0], tk_store[g0 + 1],
                           tk_acc[0], tk_acc[1])
            tk_xnb_free = tk_zmm[EC - 1]
            tk_xq_free = tk_fmm[KD - 1]
            tk_drb_free = tk_dmm
            pfree[0] = tk_epi[KD - 2]
            pfree[1] = tk_epi[KD - 1]

        emit_tail_ymm((tk_epi[KD - 2], tk_epi[KD - 1]))

        for g in range(16):
            w("sync", tk_store[g])

        block = ctx.enter_context(nc.Block())
        S.run(block)
    return nc


def kernel(**inputs):
    x = np.asarray(inputs["L0_attn_x"])[0]
    Win = np.asarray(inputs["L0_attn_Win"])
    conv_w = np.asarray(inputs["L0_attn_conv_w"])
    conv_b = np.asarray(inputs["L0_attn_conv_b"])
    Wx = np.asarray(inputs["L0_attn_Wx"])
    Wdt = np.asarray(inputs["L0_attn_Wdt"])
    bdt = np.asarray(inputs["L0_attn_bdt"])
    A_log = np.asarray(inputs["L0_attn_A_log"])
    D = np.asarray(inputs["L0_attn_D"])
    ln_g = np.asarray(inputs["L0_attn_ln_g"])
    ln_b = np.asarray(inputs["L0_attn_ln_b"])
    Wout = np.asarray(inputs["L0_attn_Wout"])
    res_scale = np.asarray(inputs["L0_attn_res_scale"])
    Wf = np.asarray(inputs["L0_ffn_weight"])
    ffb = np.asarray(inputs["L0_ffn_bias"])

    C = np.ascontiguousarray
    w_xp = C(Win[:, :E])
    w_z = C(Win[:, E:2 * E])
    w_b0 = C(Win[:, 2 * E].reshape(KD, 128).T)
    w_ft = C(Wf.T)
    wdt0 = C(Wdt[:, 0:1])
    alog0 = C(A_log[0].reshape(H, 1))
    cw = C(conv_w.reshape(EC, 128, 4).transpose(1, 0, 2).reshape(128, EC * 4))
    col = lambda v: C(v.reshape(EC, 128).T)
    ffbc = C(ffb.reshape(KD, 128).T)
    rs128 = C(np.broadcast_to(res_scale.reshape(1, 1), (128, 1)))
    bdt0 = C(bdt[0].reshape(1, 1))

    in_maps = []
    for i in range(NCORE):
        t0 = i * T
        xh = x[t0 - 3:t0] if i > 0 else np.zeros((3, DM), np.float32)
        sel = np.zeros((128, NCORE), np.float32)
        sel[:, i] = 1.0
        m8 = np.zeros((128, NCORE), np.float32)
        m8[:, i] = 1.0
        m16 = np.zeros((128, 2 * NCORE), np.float32)
        m16[:, 2 * i:2 * i + 2] = 1.0
        cst_all = np.concatenate([
            cw, col(conv_b), col(D), col(ln_g), col(ln_b), ffbc, rs128,
            sel, m8, m16, np.broadcast_to(bdt0, (128, 1)), alog0], axis=1)
        in_maps.append({
            "x": C(x[t0:t0 + T]), "xh": C(xh),
            "w_xp": w_xp, "w_z": w_z, "w_b0": w_b0, "w_x": Wx, "w_dt0": wdt0,
            "w_out": Wout, "w_ft": w_ft,
            "cst_all": C(cst_all.astype(np.float32)),
        })
    nc = build_nc()
    res = run_bass_kernel_spmd(nc, in_maps, core_ids=list(range(NCORE)))
    out = np.concatenate([r["y"] for r in res.results], axis=0)
    return out.reshape(1, NCORE * T, DM)
